# version 02
# baseline (speedup 1.0000x reference)
import numpy as np
import ml_dtypes
import concourse.bass as bass
import concourse.mybir as mybir
from concourse.bass_utils import run_bass_kernel_spmd

F32 = mybir.dt.float32
BF16 = mybir.dt.bfloat16
U8 = mybir.dt.uint8
AF = mybir.ActivationFunctionType
ALU = mybir.AluOpType

D = 4096
INTER = 8192
PROJ = 18560
FI = 11008
TG = 512
NGRP = 8
NPREF = 6
EPS = 1e-5
MAGIC = 8388608.0 + 4194304.0

HS0, B0, DT0, GATE0, C0 = 0, 8192, 9216, 9344, 17536

BF16NP = ml_dtypes.bfloat16


def tau_perm(n):
    out = np.empty(n, dtype=np.int64)
    ar = 2 * np.arange(128)
    for m in range(n // 256):
        out[m * 256:m * 256 + 128] = m * 256 + ar
        out[m * 256 + 128:m * 256 + 256] = m * 256 + ar + 1
    return out


class _StopEmit(Exception):
    pass


def build(debug=(), skip_phase_a=False, only_ln=False, limit=None):
    ngrp, npref = NGRP, NPREF
    nfull = ngrp - npref
    nc = bass.Bass("TRN2", target_bir_lowering=False)

    xin = nc.dram_tensor("xin", [ngrp * TG, D], F32, kind="ExternalInput").ap()
    maskc = nc.dram_tensor("maskc", [128, ngrp * 4], F32, kind="ExternalInput").ap()
    inprojT = nc.dram_tensor("inprojT", [D, PROJ], F32, kind="ExternalInput").ap()
    outprojT = nc.dram_tensor("outprojT", [INTER, D], F32, kind="ExternalInput").ap()
    convw = nc.dram_tensor("convw", [128, 80 * 4], F32, kind="ExternalInput").ap()
    convb = nc.dram_tensor("convb", [128, 80], F32, kind="ExternalInput").ap()
    dtbias = nc.dram_tensor("dtbias", [128, 1], F32, kind="ExternalInput").ap()
    alog = nc.dram_tensor("alog", [128, 1], F32, kind="ExternalInput").ap()
    dexp = nc.dram_tensor("dexp", [128, 64], F32, kind="ExternalInput").ap()
    normwT = nc.dram_tensor("normwT", [128, 64], F32, kind="ExternalInput").ap()
    gq = nc.dram_tensor("gq", [FI, D // 2], U8, kind="ExternalInput").ap()
    uq = nc.dram_tensor("uq", [FI, D // 2], U8, kind="ExternalInput").ap()
    dq = nc.dram_tensor("dq", [D, FI // 2], U8, kind="ExternalInput").ap()
    gsc = nc.dram_tensor("gsc", [FI, 64], F32, kind="ExternalInput").ap()
    gzr = nc.dram_tensor("gzr", [FI, 64], F32, kind="ExternalInput").ap()
    usc = nc.dram_tensor("usc", [FI, 64], F32, kind="ExternalInput").ap()
    uzr = nc.dram_tensor("uzr", [FI, 64], F32, kind="ExternalInput").ap()
    dsc = nc.dram_tensor("dsc", [D, 172], F32, kind="ExternalInput").ap()
    dzr = nc.dram_tensor("dzr", [D, 172], F32, kind="ExternalInput").ap()

    yout = nc.dram_tensor("yout", [nfull * TG, D], F32, kind="ExternalOutput").ap()

    dbg = {}

    def dbgt(name, shape):
        if name in debug and name not in dbg:
            dbg[name] = nc.dram_tensor("dbg_" + name, shape, F32, kind="ExternalOutput").ap()
        return dbg.get(name)

    g_scr = nc.dram_tensor("g_scr", [86, 128, 4096], BF16, kind="Internal").ap()
    u_scr = nc.dram_tensor("u_scr", [86, 128, 4096], BF16, kind="Internal").ap()
    d_scr = nc.dram_tensor("d_scr", [8, 86, 128, 512], BF16, kind="Internal").ap()
    o_scr = nc.dram_tensor("o_scr", [64, 128, 4096], BF16, kind="Internal").ap()
    gg_scr = nc.dram_tensor("gg_scr", [64, 128, TG], BF16, kind="Internal").ap()
    x1_scr = nc.dram_tensor("x1_scr", [4, 128, D], F32, kind="Internal").ap()

    ident_c = nc.inline_tensor(np.eye(128, dtype=BF16NP), "identc")
    identf_c = nc.inline_tensor(np.eye(128, dtype=np.float32), "identfc")
    ones_row_c = nc.inline_tensor(np.ones((1, 128), dtype=np.float32), "onesrowc")
    ones64_c = nc.inline_tensor(np.ones((128, 64), dtype=np.float32), "ones64c")
    triu_c = nc.inline_tensor(
        np.triu(np.ones((128, 128), dtype=np.float32)).astype(BF16NP), "triuc")
    epj = np.zeros((128, 64), dtype=BF16NP)
    for h in range(128):
        epj[h, h // 2] = 1
    epj01_c = nc.inline_tensor(epj, "epj01c")
    fp_ = np.zeros((128, 128), dtype=BF16NP)
    for h in range(128):
        fp_[h, (h % 2) * 64:(h % 2) * 64 + 64] = 1
    fpair_c = nc.inline_tensor(fp_, "fpairc")

    CVALS = [-1.0, -0.5, 1.0 / 1024.0, 1.0 / 16.0, 16.0, MAGIC, -16.0 * MAGIC,
             -0.46875, EPS, 1.0]
    CIDX = {v: i for i, v in enumerate(CVALS)}
    consts_c = nc.inline_tensor(
        np.tile(np.array(CVALS, dtype=np.float32)[None, :], (128, 1)), "constsc")

    A = nc.alloc_sbuf_tensor
    ident = A("ident", [128, 128], BF16)
    identf = A("identf", [128, 128], F32)
    onesr = A("onesr", [1, 128], F32)
    ones64 = A("ones64", [128, 64], F32)
    triu = A("triu", [128, 128], BF16)
    epj01 = A("epj01", [128, 64], BF16)
    fpair = A("fpair", [128, 128], BF16)
    cpool = A("cpool", [128, len(CVALS)], F32)
    cw = A("cw", [128, 80 * 4], F32)
    cb = A("cb", [128, 80], F32)
    dtb = A("dtb", [128, 1], F32)
    acol = A("acol", [128, 1], F32)
    dxp = A("dxp", [128, 64], F32)
    nwT = A("nwT", [128, 64], F32)
    mcol = A("mcol", [128, ngrp * 4], F32)
    zeros128 = A("zeros128", [128, 128], F32)
    lnst = A("lnst", [128, 8], F32)
    rmsq = A("rmsq", [128, 8], F32)
    s_run = A("s_run", [128, 64 * 128], F32)
    halo = A("halo", [128, 2 * 80 * 3], F32)
    xlnT = A("xlnT", [128, 32 * TG], BF16)

    def cp(v):
        return cpool[:, CIDX[v]:CIDX[v] + 1]

    ARENA = 136 * 1024
    arena = A("arena", [128, ARENA], U8)

    class Rg:
        def __init__(self):
            self.off = 0

        def al(self, nbytes, align=64):
            o = (self.off + align - 1) // align * align
            self.off = o + nbytes
            assert self.off <= ARENA, f"arena overflow {self.off}"
            return o

    def view(off, nelem, dtype, parts=128):
        esz = {F32: 4, BF16: 2, U8: 1}[dtype]
        return arena[0:parts, off:off + nelem * esz].bitcast(dtype)

    ra = Rg()
    qb = view(ra.al(2048 * 4), 2048, F32)
    qh = view(ra.al(2048 * 4), 2048, F32)
    ql = view(ra.al(2048 * 4), 2048, F32)
    qtmp = qb
    qwhi = view(ra.al(2048 * 2), 2048, BF16)
    qwlo = view(ra.al(2048 * 2), 2048, BF16)
    qstg = view(ra.al(4096 * 2), 4096, BF16)
    qsc = view(ra.al(64 * 4), 64, F32)
    qsc16 = view(ra.al(64 * 4), 64, F32)
    qzs = view(ra.al(64 * 4), 64, F32)

    ra2 = Rg()
    qb2 = view(ra2.al(5504 * 4), 5504, F32)
    qh2 = view(ra2.al(5504 * 4), 5504, F32)
    ql2 = view(ra2.al(5504 * 4), 5504, F32)
    qtmp2 = qb2
    qwhi2 = view(ra2.al(2 * 5504 * 2), 2 * 5504, BF16)
    qwlo2 = view(ra2.al(2 * 5504 * 2), 2 * 5504, BF16)
    qstg2 = view(ra2.al(512 * 2), 512, BF16)
    qsc2 = view(ra2.al(172 * 4), 172, F32)
    qsc216 = view(ra2.al(172 * 4), 172, F32)
    qzs2 = view(ra2.al(172 * 4), 172, F32)
    owstg = view(ra2.al(4096 * 2), 4096, BF16)

    rm = Rg()
    wbuf = view(rm.al(2 * 4096 * 2), 2 * 4096, BF16)
    bcm = view(rm.al(32 * 128 * 2), 32 * 128, BF16)
    hsall = view(rm.al(8 * TG * 2), 8 * TG, BF16)
    gsall = view(rm.al(8 * TG * 2), 8 * TG, BF16)
    bsil = view(rm.al(8 * TG * 2), 8 * TG, BF16)
    csil = view(rm.al(8 * TG * 2), 8 * TG, BF16)
    bcpre = view(rm.al((3 + TG) * 4), 3 + TG, F32)
    ggtb = view(rm.al(8 * TG * 2), 8 * TG, BF16)
    dtpre = view(rm.al(TG * 4), TG, F32)
    dtt = view(rm.al(TG * 4), TG, F32)
    adt = view(rm.al(TG * 4), TG, F32)
    acum = view(rm.al(TG * 4), TG, F32)
    acumT = view(rm.al(4 * 128 * 4), 4 * 128, F32)
    d1T = view(rm.al(4 * 128 * 4), 4 * 128, F32)
    dtT = view(rm.al(4 * 128 * 4), 4 * 128, F32)
    dd1T = view(rm.al(4 * 128 * 4), 4 * 128, F32)
    dscr = view(rm.al(TG * 4), TG, F32)
    expacT = view(rm.al(4 * 128 * 4), 4 * 128, F32)
    alast = view(rm.al(4 * 4), 4, F32)
    exal = view(rm.al(4 * 4), 4, F32)
    gexp = view(rm.al(64 * 2), 64, BF16)
    expal = view(rm.al(4 * 64 * 4), 4 * 64, F32)
    hspre = view(rm.al((3 + TG) * 4), 3 + TG, F32)
    convo = view(rm.al(TG * 4), TG, F32)
    convo2 = view(rm.al(TG * 4), TG, F32)
    convs = view(rm.al(TG * 4), TG, F32)
    xch_cm = view(rm.al(128 * 2), 128, BF16)
    xchp = view(rm.al(128 * 2), 128, BF16)
    hsdt = view(rm.al(128 * 2), 128, BF16)
    g1m = view(rm.al(4 * 128 * 2), 4 * 128, BF16)
    mh = view(rm.al(2 * 128 * 2), 2 * 128, BF16)
    relu_s = view(rm.al(2 * 128 * 4), 2 * 128, F32)
    prevT = view(rm.al(128 * 2), 128, BF16)
    yoffc = view(rm.al(128 * 4), 128, F32)
    expacx = view(rm.al(128 * 2), 128, BF16)
    ggf = view(rm.al(1024 * 2), 1024, BF16)
    ggb = view(rm.al(1024 * 2), 1024, BF16)
    acrow = view(rm.al(2048 * 4), 2048, F32, parts=1)
    sub_o = rm.al(max(D * 4 + D * 2, 2 * 4 * 1024 * 2))
    xt = view(sub_o, D, F32)
    xlntm = view(sub_o + D * 4, D, BF16)
    ytm = view(sub_o, 4 * 1024, BF16)
    gatm = view(sub_o + 4 * 1024 * 2, 4 * 1024, BF16)
    mix_end = rm.off

    rf = Rg()
    hsw = view(rf.al(86 * TG * 2), 86 * TG, BF16)
    fwbuf = view(rf.al(2 * 4096 * 2), 2 * 4096, BF16)
    fwbuf2 = view(rf.al(8 * 512 * 2), 8 * 512, BF16)
    fsil = view(rf.al(TG * 2), TG, BF16)
    xsl = view(rf.al(512 * 4), 512, F32)
    outt = view(rf.al(512 * 4), 512, F32)
    obuf = view(rf.al(2 * 512 * 2), 2 * 512, BF16)
    ggl = view(rf.al(2 * 512 * 2), 2 * 512, BF16)
    ffn_end = rf.off

    PS = nc.alloc_psum_tensor
    pA = PS("pA", [128, 512], F32)
    pB = PS("pB", [128, 512], F32)
    pC = PS("pC", [128, 512], F32)
    pD = PS("pD", [128, 512], F32)
    pE = PS("pE", [128, 512], F32)
    pF = PS("pF", [128, 512], F32)
    pT1 = PS("pT1", [128, 1024], BF16)
    pT2 = PS("pT2", [128, 1024], BF16)
    PBANKS = [pA, pB, pC, pD]

    with nc.Block() as block:
        sem = {}
        cnt = {}
        for s in ("ld", "ld2", "cast", "cast2", "ts", "vs", "ss", "st"):
            sem[s] = nc.alloc_semaphore(s)
            cnt[s] = 0

        def inc(instr, s, n=1):
            instr.then_inc(sem[s], n)
            cnt[s] += n
            return instr

        def wt(eng, s, val=None):
            v = cnt[s] if val is None else val
            if v > 0:
                eng.wait_ge(sem[s], v)

        ALLSEMS = ("ld", "ld2", "cast", "cast2", "ts", "vs", "ss", "st")

        def on(eng_attr, fn, deps=None):
            def wrapped(eng):
                for d in ALLSEMS:
                    wt(eng, d)
                fn(eng)
            getattr(block, eng_attr)(wrapped)

        def onx(eng_attr, fn, deps):
            def wrapped(eng):
                for d in deps:
                    wt(eng, d)
                fn(eng)
            getattr(block, eng_attr)(wrapped)

        def tap(name, src_ap, shape):
            t_ = dbgt(name, shape)
            if t_ is None:
                return

            def tp_(gp_, t_=t_, src_ap=src_ap):
                inc(gp_.dma_start(out=t_, in_=src_ap), "cast2", 16)
            on("gpsimd", tp_, deps=("vs", "ss", "ts", "ld", "st"))

        def tap_s(name, src_ap, shape):
            t_ = dbgt(name, shape)
            if t_ is None:
                return

            def tp_(sy_, t_=t_, src_ap=src_ap):
                inc(sy_.dma_start(out=t_, in_=src_ap), "st", 16)
            on("sync", tp_, deps=("vs", "ss", "ts", "ld", "st"))

        def p0_sync(sync):
            for dst, src in ((ident, ident_c), (identf, identf_c), (onesr, ones_row_c),
                             (ones64, ones64_c), (triu, triu_c), (epj01, epj01_c),
                             (fpair, fpair_c), (cpool, consts_c)):
                inc(sync.dma_start(out=dst[:], in_=src.ap()), "ld", 16)
            inc(sync.dma_start(out=cw[:], in_=convw), "ld", 16)
            inc(sync.dma_start(out=cb[:], in_=convb), "ld", 16)
            inc(sync.dma_start(out=dtb[:], in_=dtbias), "ld", 16)
            inc(sync.dma_start(out=acol[:], in_=alog), "ld", 16)
            inc(sync.dma_start(out=dxp[:], in_=dexp), "ld", 16)
            inc(sync.dma_start(out=nwT[:], in_=normwT), "ld", 16)
            inc(sync.dma_start(out=mcol[:], in_=maskc), "ld", 16)
        block.sync(p0_sync)

        def p0_scal(s):
            wt(s, "ld")
            inc(s.activation(out=acol[:], in_=acol[:], func=AF.Exp), "ss", 1)
        block.scalar(p0_scal)

        def p0_vec(v):
            wt(v, "ss")
            v.tensor_scalar(out=acol[:], in0=acol[:], scalar1=-1.0, scalar2=None, op0=ALU.mult)
            v.memset(zeros128[:], 0.0)
            v.memset(s_run[:], 0.0)
            inc(v.memset(halo[:], 0.0), "vs", 1)
        block.vector(p0_vec)

        tap("xin0", xin[0:128, :], [128, D])

        def unpack_ops(s, bsrc, hdst, n):
            inc(s.activation(out=hdst[:, 0:n], in_=bsrc[:, 0:n], func=AF.Identity,
                             scale=cp(1.0 / 16.0), bias=cp(-0.46875)), "ss", 1)
            wt(s, "ss")
            inc(s.activation(out=hdst[:, 0:n], in_=hdst[:, 0:n], func=AF.Identity,
                             bias=cp(MAGIC)), "ss", 1)
            wt(s, "ss")
            return s.activation(out=hdst[:, 0:n], in_=hdst[:, 0:n], func=AF.Identity,
                                scale=cp(16.0), bias=cp(-16.0 * MAGIC))

        def dequant_ops(v, bsrc, hsrc, ldst, scb, sc16b, zsb, tmpb, whib, wlob, n, ng):
            w = n // ng
            sc_e = scb[:, 0:ng].unsqueeze(2).broadcast_to((128, ng, w))
            sc16_e = sc16b[:, 0:ng].unsqueeze(2).broadcast_to((128, ng, w))
            zs_e = zsb[:, 0:ng].unsqueeze(2).broadcast_to((128, ng, w))

            def g3(ap_, o=0):
                return ap_[:, o:o + n].rearrange("p (g r) -> p g r", g=ng)
            inc(v.tensor_sub(out=ldst[:, 0:n], in0=bsrc[:, 0:n], in1=hsrc[:, 0:n]),
                "vs", 1)
            wt(v, "vs")
            v.tensor_mul(out=g3(tmpb), in0=g3(hsrc), in1=sc16_e)
            inc(v.tensor_mul(out=g3(ldst), in0=g3(ldst), in1=sc_e), "vs", 1)
            wt(v, "vs")
            v.tensor_add(out=g3(whib), in0=g3(tmpb), in1=zs_e)
            return v.tensor_add(out=g3(wlob), in0=g3(ldst), in1=zs_e)

        def scale_prep(v, scb, sc16b, zsb, ng):
            v.scalar_tensor_tensor(out=zsb[:, 0:ng], in0=zsb[:, 0:ng], scalar=-1.0,
                                   in1=scb[:, 0:ng], op0=ALU.mult, op1=ALU.mult)
            v.tensor_scalar(out=sc16b[:, 0:ng], in0=scb[:, 0:ng], scalar1=1.0 / 16.0,
                            scalar2=None, op0=ALU.mult)

        for (qsrc, scsrc, zrsrc, scr) in (() if skip_phase_a else ((gq, gsc, gzr, g_scr), (uq, usc, uzr, u_scr))):
            for ib in range(86):
                r0 = ib * 128

                def ga(g, qsrc=qsrc, scsrc=scsrc, zrsrc=zrsrc, r0=r0):
                    inc(g.dma_start(out=qb[:], in_=qsrc[r0:r0 + 128, :]), "cast", 16)
                    inc(g.dma_start(out=qsc[:], in_=scsrc[r0:r0 + 128, :]), "cast", 16)
                    inc(g.dma_start(out=qzs[:], in_=zrsrc[r0:r0 + 128, :]), "cast", 16)
                on("gpsimd", ga, deps=("st", "vs", "ss", "ts"))

                def sa(s):
                    inc(unpack_ops(s, qb, qh, 2048), "ss", 1)
                on("scalar", sa, deps=("cast",))

                def va(v):
                    scale_prep(v, qsc, qsc16, qzs, 64)
                    inc(dequant_ops(v, qb, qh, ql, qsc, qsc16, qzs, qtmp, qwhi, qwlo,
                                    2048, 64), "vs", 1)
                on("vector", va, deps=("cast", "ss"))

                def t_wave(t, wave):
                    for half in range(2):
                        pt = pT1 if half == 0 else pT2
                        for m in range(8):
                            sub = wave * 16 + half * 8 + m
                            src = qwhi if sub % 2 == 0 else qwlo
                            blk = sub // 2
                            mmi = t.transpose(pt[:, 128 * m:128 * m + 128],
                                              src[:, 128 * blk:128 * blk + 128], ident[:])
                    inc(mmi, "ts", 1)

                def v_wave(v, wave):
                    v.tensor_copy(out=qstg[:, 2048 * wave:2048 * wave + 1024], in_=pT1[:])
                    inc(v.tensor_copy(out=qstg[:, 2048 * wave + 1024:2048 * wave + 2048],
                                      in_=pT2[:]), "vs", 1)

                on("tensor", lambda t: t_wave(t, 0), deps=("vs",))
                on("vector", lambda v: v_wave(v, 0), deps=("ts", "st"))
                on("tensor", lambda t: t_wave(t, 1), deps=("vs",))
                on("vector", lambda v: v_wave(v, 1), deps=("ts",))

                def wr(sync, scr=scr, ib=ib):
                    inc(sync.dma_start(out=scr[ib], in_=qstg[:]), "st", 16)
                on("sync", wr, deps=("vs",))

        for dc in ([] if skip_phase_a else range(8)):
            for half2 in range(2):
                for dbi in range(2):
                    db = 2 * half2 + dbi
                    r0 = (4 * dc + db) * 128

                    def ga2(g, r0=r0):
                        inc(g.dma_start(out=qb2[:], in_=dq[r0:r0 + 128, :]), "cast", 16)
                        inc(g.dma_start(out=qsc2[:], in_=dsc[r0:r0 + 128, :]), "cast", 16)
                        inc(g.dma_start(out=qzs2[:], in_=dzr[r0:r0 + 128, :]), "cast", 16)
                    on("gpsimd", ga2, deps=("st", "vs", "ss", "ts"))

                    def sa2(s):
                        inc(unpack_ops(s, qb2, qh2, 5504), "ss", 1)
                    on("scalar", sa2, deps=("cast",))

                    def va2(v, dbi=dbi):
                        scale_prep(v, qsc2, qsc216, qzs2, 172)
                        whiv = qwhi2[:, 5504 * dbi:5504 * dbi + 5504]
                        wlov = qwlo2[:, 5504 * dbi:5504 * dbi + 5504]
                        inc(dequant_ops(v, qb2, qh2, ql2, qsc2, qsc216, qzs2, qtmp2,
                                        whiv, wlov, 5504, 172), "vs", 1)
                    on("vector", va2, deps=("cast", "ss"))

                for tsub in range(86):
                    m, hi = tsub // 2, (tsub % 2 == 0)

                    def t_tr(t, m=m, hi=hi):
                        for dbi in range(2):
                            base = 5504 * dbi
                            src = qwhi2 if hi else qwlo2
                            mmi = t.transpose(pT1[:, 128 * dbi:128 * dbi + 128],
                                              src[:, base + 128 * m:base + 128 * m + 128],
                                              ident[:])
                        inc(mmi, "ts", 1)
                    on("tensor", t_tr, deps=("vs",))

                    def v_tr(v):
                        inc(v.tensor_copy(out=qstg2[:, 0:256], in_=pT1[:, 0:256]), "vs", 1)
                    on("vector", v_tr, deps=("ts", "st"))

                    def wr2(sync, dc=dc, tsub=tsub, half2=half2):
                        inc(sync.dma_start(
                            out=d_scr[dc, tsub][:, 256 * half2:256 * half2 + 256],
                            in_=qstg2[:, 0:256]), "st", 16)
                    on("sync", wr2, deps=("vs",))

        for kk in ([] if skip_phase_a else range(64)):
            def gow(g, kk=kk):
                inc(g.dma_start(out=owstg[:], in_=outprojT[128 * kk:128 * kk + 128, :]),
                    "cast", 16)
            on("gpsimd", gow, deps=("st",))

            def sow(sync, kk=kk):
                inc(sync.dma_start(out=o_scr[kk], in_=owstg[:]), "st", 16)
            on("sync", sow, deps=("cast",))

        tap("gscr0", g_scr[0], [128, 4096])
        tap("uscr0", u_scr[0], [128, 4096])
        tap("dscr00", d_scr[0, 0], [128, 512])
        tap("oscr0", o_scr[0], [128, 4096])

        def ln_stage(g, from_x1):
            sig = from_x1
            for tt in range(4):
                def ldx(sync, tt=tt, g=g, from_x1=from_x1):
                    if from_x1:
                        inc(sync.dma_start(out=xt[:], in_=x1_scr[tt]), "ld", 16)
                    else:
                        r0 = TG * g + 128 * tt
                        inc(sync.dma_start(out=xt[:], in_=xin[r0:r0 + 128, :]), "ld", 16)
                on("sync", ldx, deps=("vs", "ss", "ts"))

                def lns(s):
                    inc(s.activation(out=xlntm[:], in_=xt[:], func=AF.Square,
                                     accum_out=lnst[:, 0:1]), "ss", 1)
                on("scalar", lns, deps=("ld", "vs"))

                def lnv(v):
                    inc(v.reduce_sum(out=lnst[:, 1:2], in_=xt[:],
                                     axis=mybir.AxisListType.X), "vs", 1)
                    wt(v, "vs")
                    v.tensor_scalar(out=lnst[:, 2:3], in0=lnst[:, 1:2], scalar1=1.0 / D,
                                    scalar2=None, op0=ALU.mult)
                    inc(v.tensor_scalar(out=lnst[:, 3:4], in0=lnst[:, 0:1],
                                        scalar1=1.0 / D, scalar2=None, op0=ALU.mult),
                        "vs", 1)
                    wt(v, "vs")
                    inc(v.tensor_mul(out=lnst[:, 4:5], in0=lnst[:, 2:3],
                                     in1=lnst[:, 2:3]), "vs", 1)
                    wt(v, "vs")
                    inc(v.tensor_sub(out=lnst[:, 5:6], in0=lnst[:, 3:4], in1=lnst[:, 4:5]),
                        "vs", 1)
                on("vector", lnv, deps=("ld", "ss"))

                def lns2(s):
                    inc(s.activation(out=lnst[:, 6:7], in_=lnst[:, 5:6], func=AF.Ln,
                                     bias=cp(EPS)), "ss", 1)
                    wt(s, "ss")
                    inc(s.activation(out=lnst[:, 7:8], in_=lnst[:, 6:7], func=AF.Exp,
                                     scale=cp(-0.5)), "ss", 1)
                on("scalar", lns2, deps=("vs",))

                def lnv2(v):
                    inc(v.tensor_scalar(out=xlntm[:], in0=xt[:], scalar1=lnst[:, 2:3],
                                        scalar2=lnst[:, 7:8], op0=ALU.subtract,
                                        op1=ALU.mult), "vs", 1)
                on("vector", lnv2, deps=("ss",))

                for wave in range(2):
                    def t_tw(t, wave=wave, sig=sig):
                        for half in range(2):
                            pt = pT1 if half == 0 else pT2
                            for m in range(8):
                                ks = wave * 16 + half * 8 + m
                                if not sig:
                                    src = xlntm[:, 128 * ks:128 * ks + 128]
                                else:
                                    mm2, par = ks // 2, ks % 2
                                    src = xlntm[:, 256 * mm2 + par:256 * mm2 + par + 255:2]
                                mmi = t.transpose(pt[:, 128 * m:128 * m + 128], src, ident[:])
                        inc(mmi, "ts", 1)
                    on("tensor", t_tw, deps=("vs",))

                    def v_tw(v, tt=tt, wave=wave):
                        for half in range(2):
                            pt = pT1 if half == 0 else pT2
                            ks0 = wave * 16 + half * 8
                            dsts = xlnT[:, TG * ks0:TG * (ks0 + 8)].rearrange(
                                "p (k r) -> p k r", k=8)[:, :, 128 * tt:128 * tt + 128]
                            inc(v.tensor_copy(
                                out=dsts, in_=pt[:].rearrange("p (k r) -> p k r", k=8)),
                                "vs", 1)
                    on("vector", v_tw, deps=("ts",))

        def proj_stage(blocks):
            base_cast = {0: cnt["cast"], 1: cnt["cast2"]}
            base_ts = cnt["ts"]
            base_vs = cnt["vs"]
            base_ss = cnt["ss"]
            ev_mark = []
            vv = ss_ = 0
            for (off, kind, dest) in blocks:
                if kind == "silu":
                    ss_ += 1
                    ev_mark.append(("ss", ss_))
                else:
                    vv += 1
                    ev_mark.append(("vs", vv))

            def gp(gps):
                for i, (off, kind, dest) in enumerate(blocks):
                    if i >= 2:
                        gps.wait_ge(sem["ts"], base_ts + (i - 2) + 1)
                    src = inprojT[:, off:off + 128].rearrange("(ks p) c -> p ks c", p=128)
                    dst = wbuf[:, 4096 * (i % 2):4096 * (i % 2) + 4096].rearrange(
                        "p (ks c) -> p ks c", ks=32)
                    inc(gps.dma_start(out=dst, in_=src),
                        "cast" if i % 2 == 0 else "cast2", 16)
            on("gpsimd", gp, deps=("ts", "vs", "ss"))

            def tp(t):
                for i, (off, kind, dest) in enumerate(blocks):
                    cs_ = "cast" if i % 2 == 0 else "cast2"
                    t.wait_ge(sem[cs_], base_cast[i % 2] + 16 * (i // 2 + 1))
                    bank = PBANKS[i % 2]
                    if i >= 2:
                        s_, v_ = ev_mark[i - 2]
                        t.wait_ge(sem[s_], (base_ss if s_ == "ss" else base_vs) + v_)
                    w = wbuf[:, 4096 * (i % 2):4096 * (i % 2) + 4096]
                    for ks in range(32):
                        mmi = t.matmul(bank[:], w[:, 128 * ks:128 * ks + 128],
                                       xlnT[:, TG * ks:TG * ks + TG],
                                       start=(ks == 0), stop=(ks == 31))
                    inc(mmi, "ts", 1)
            onx("tensor", tp, ())

            def se(s):
                for i, (off, kind, dest) in enumerate(blocks):
                    if kind != "silu":
                        continue
                    s.wait_ge(sem["ts"], base_ts + i + 1)
                    inc(s.activation(out=dest, in_=PBANKS[i % 2][:], func=AF.Silu), "ss", 1)
            onx("scalar", se, ("vs",))

            def vep(v):
                for i, (off, kind, dest) in enumerate(blocks):
                    if kind == "silu":
                        continue
                    v.wait_ge(sem["ts"], base_ts + i + 1)
                    inc(v.tensor_copy(out=dest, in_=PBANKS[i % 2][:]), "vs", 1)
            onx("vector", vep, ("ss",))

        def conv_into(v, tile_idx, pre, par):
            rd = 240 * par + 3 * tile_idx
            sv = 240 * (1 - par) + 3 * tile_idx
            v.tensor_copy(out=pre[:, 0:3], in_=halo[:, rd:rd + 3])
            inc(v.tensor_copy(out=halo[:, sv:sv + 3], in_=pre[:, TG:TG + 3]), "vs", 1)
            wt(v, "vs")
            v.tensor_scalar(out=convo[:], in0=pre[:, 0:TG],
                            scalar1=cw[:, 4 * tile_idx:4 * tile_idx + 1],
                            scalar2=cb[:, tile_idx:tile_idx + 1], op0=ALU.mult, op1=ALU.add)
            inc(v.tensor_scalar(out=convo2[:], in0=pre[:, 1:1 + TG],
                                scalar1=cw[:, 4 * tile_idx + 1:4 * tile_idx + 2],
                                scalar2=None, op0=ALU.mult), "vs", 1)
            wt(v, "vs")
            v.scalar_tensor_tensor(out=convo[:], in0=pre[:, 2:2 + TG],
                                   scalar=cw[:, 4 * tile_idx + 2:4 * tile_idx + 3],
                                   in1=convo[:], op0=ALU.mult, op1=ALU.add)
            inc(v.scalar_tensor_tensor(out=convo2[:], in0=pre[:, 3:3 + TG],
                                       scalar=cw[:, 4 * tile_idx + 3:4 * tile_idx + 4],
                                       in1=convo2[:], op0=ALU.mult, op1=ALU.add), "vs", 1)
            wt(v, "vs")
            return v.tensor_add(out=convs[:], in0=convo[:], in1=convo2[:])

        _ck = [0]

        def ckpt():
            _ck[0] += 1
            if limit is not None and _ck[0] >= limit:
                raise _StopEmit()

        if only_ln:
            ln_stage(0, False)
            def fin0(sync):
                wt(sync, "st")
            on("sync", fin0)
            return nc, dbg
        def _emit_groups():
          for g in range(ngrp):
            full = g >= npref
            ln_stage(g, False)
            ckpt()
            if g == npref:
                tap("srun6", s_run[:], [128, 64 * 128])
                tap("xlnT6", xlnT[:], [128, 32 * TG])

            sec = [(B0 + 128 * gb, 64 + gb, bsil, gb) for gb in range(8)]
            if full:
                sec += [(C0 + 128 * gb, 72 + gb, csil, gb) for gb in range(8)]
            for (off, tile, dstbuf, gb) in sec:
                proj_stage([(off, "f32", bcpre[:, 3:3 + TG])])

                def cvb(v, tile=tile, g=g):
                    inc(conv_into(v, tile, bcpre, g % 2), "vs", 1)
                on("vector", cvb, deps=("ts",))

                def slb(s, dstbuf=dstbuf, gb=gb):
                    inc(s.activation(out=dstbuf[:, TG * gb:TG * gb + TG],
                                     in_=convs[:], func=AF.Silu), "ss", 1)
                on("scalar", slb, deps=("vs",))

            ckpt()
            proj_stage([(DT0, "f32", dtpre[:])])

            def dt_vec(v):
                inc(v.tensor_scalar(out=dtt[:], in0=dtpre[:], scalar1=dtb[:, 0:1],
                                    scalar2=None, op0=ALU.add), "vs", 1)
                wt(v, "vs")
                inc(v.tensor_relu(out=dtpre[:], in_=dtt[:]), "vs", 1)
            on("vector", dt_vec, deps=("vs", "ts"))

            def dt_scal(s):
                inc(s.activation(out=adt[:], in_=dtt[:], func=AF.Abs), "ss", 1)
                wt(s, "ss")
                inc(s.activation(out=adt[:], in_=adt[:], func=AF.Exp, scale=cp(-1.0)),
                    "ss", 1)
                wt(s, "ss")
                inc(s.activation(out=adt[:], in_=adt[:], func=AF.Ln, bias=cp(1.0)), "ss", 1)
            on("scalar", dt_scal, deps=("vs",))

            def dt_vec2(v, g=g):
                inc(v.tensor_add(out=dtt[:], in0=dtpre[:], in1=adt[:]), "vs", 1)
                wt(v, "vs")
                inc(v.tensor_scalar(out=dtt[:], in0=dtt[:], scalar1=100.0, scalar2=None,
                                    op0=ALU.min), "vs", 1)
                wt(v, "vs")
                for c in range(4):
                    slot = 4 * g + c
                    i_ = v.tensor_scalar(out=dtt[:, 128 * c:128 * c + 128],
                                         in0=dtt[:, 128 * c:128 * c + 128],
                                         scalar1=mcol[:, slot:slot + 1], scalar2=None,
                                         op0=ALU.mult)
                inc(i_, "vs", 1)
                wt(v, "vs")
                inc(v.tensor_scalar(out=adt[:], in0=dtt[:], scalar1=acol[:, 0:1],
                                    scalar2=None, op0=ALU.mult), "vs", 1)
                wt(v, "vs")
                for c in range(4):
                    cs = slice(128 * c, 128 * c + 128)
                    i_ = v.tensor_tensor_scan(out=acum[:, cs], data0=adt[:, cs],
                                              data1=zeros128[:], initial=0.0,
                                              op0=ALU.add, op1=ALU.add)
                inc(i_, "vs", 1)
                wt(v, "vs")
                for c in range(4):
                    i_ = v.tensor_copy(out=alast[:, c:c + 1],
                                       in_=acum[:, 128 * c + 127:128 * c + 128])
                inc(i_, "vs", 1)
                wt(v, "vs")
                for c in range(4):
                    cs = slice(128 * c, 128 * c + 128)
                    i_ = v.tensor_scalar(out=dscr[:, cs], in0=acum[:, cs],
                                         scalar1=alast[:, c:c + 1], scalar2=None,
                                         op0=ALU.subtract)
                inc(i_, "vs", 1)
            on("vector", dt_vec2, deps=("ss",))

            def dt_scal2(s):
                s.activation(out=dscr[:], in_=dscr[:], func=AF.Exp, scale=cp(-1.0))
                inc(s.activation(out=exal[:], in_=alast[:], func=AF.Exp), "ss", 1)
            on("scalar", dt_scal2, deps=("vs",))

            for c in range(4):
                def gv(v, c=c):
                    inc(v.tensor_scalar(out=gexp[:], in0=epj01[:], scalar1=exal[:, c:c + 1],
                                        scalar2=None, op0=ALU.mult), "vs", 1)
                on("vector", gv, deps=("ss", "ts"))

                def gt(t, c=c, full=full):
                    cs = slice(128 * c, 128 * c + 128)
                    t.matmul(pF[:, 0:64], fpair[:], gexp[:], start=True, stop=True)
                    t.transpose(pF[:, 128:256], dscr[:, cs], identf[:])
                    mmi = t.transpose(pF[:, 256:384], dtt[:, cs], identf[:])
                    if full:
                        mmi = t.transpose(pF[:, 384:512], acum[:, cs], identf[:])
                    inc(mmi, "ts", 1)
                on("tensor", gt, deps=("vs", "ss"))

                def gv2(v, c=c, full=full):
                    v.tensor_copy(out=expal[:, 64 * c:64 * c + 64], in_=pF[:, 0:64])
                    v.tensor_copy(out=d1T[:, 128 * c:128 * c + 128], in_=pF[:, 128:256])
                    i2 = v.tensor_copy(out=dtT[:, 128 * c:128 * c + 128], in_=pF[:, 256:384])
                    if full:
                        i2 = v.tensor_copy(out=acumT[:, 128 * c:128 * c + 128],
                                           in_=pF[:, 384:512])
                    inc(i2, "vs", 1)
                on("vector", gv2, deps=("ts",))

            def ddv(v):
                inc(v.tensor_mul(out=dd1T[:], in0=dtT[:], in1=d1T[:]), "vs", 1)
            on("vector", ddv, deps=("ts",))

            if full:
                def exa(s):
                    inc(s.activation(out=expacT[:], in_=acumT[:], func=AF.Exp), "ss", 1)
                on("scalar", exa, deps=("vs",))

            if g == npref:
                tap("alast6", alast[:], [128, 4])
                tap("exal6", exal[:], [128, 4])
                tap("adt6", adt[:], [128, TG])
                tap("dtpre6", dtpre[:], [128, TG])
                tap("dscrx6", dscr[:], [128, TG])
                tap("bsil6", bsil[:], [128, 8 * TG])
                tap("csil6", csil[:], [128, 8 * TG])
                tap("dtt6", dtt[:], [128, TG])
                tap("acum6", acum[:], [128, TG])
                tap("d1T6", d1T[:], [128, 4 * 128])
                tap("dtT6", dtT[:], [128, 4 * 128])
                tap("expal6", expal[:], [128, 4 * 64])
                tap("acumT6", acumT[:], [128, 4 * 128])
                tap("expacT6", expacT[:], [128, 4 * 128])

            for wave in range(2):
                def bt(t, wave=wave):
                    for half in range(2):
                        pt = pT1 if half == 0 else pT2
                        for m in range(8):
                            idx = wave * 16 + half * 8 + m
                            c, gb = idx // 8, idx % 8
                            mmi = t.transpose(
                                pt[:, 128 * m:128 * m + 128],
                                bsil[:, TG * gb + 128 * c:TG * gb + 128 * c + 128],
                                ident[:])
                    inc(mmi, "ts", 1)
                on("tensor", bt, deps=("vs", "ss"))

                def bv(v, wave=wave):
                    for half in range(2):
                        pt = pT1 if half == 0 else pT2
                        idx0 = wave * 16 + half * 8
                        dsts = bcm[:, 128 * idx0:128 * (idx0 + 8)].rearrange(
                            "p (k r) -> p k r", k=8)
                        inc(v.tensor_copy(out=dsts,
                                          in_=pt[:].rearrange("p (k r) -> p k r", k=8)),
                            "vs", 1)
                on("vector", bv, deps=("ts",))

            tap(f"sr_a{g}", s_run[:], [128, 64 * 128])
            tap(f"dd_{g}", dd1T[:], [128, 4 * 128])
            tap(f"ex_{g}", expal[:], [128, 4 * 64])
            tap(f"hs_{g}", hsall[:], [128, 8 * TG])

            ckpt()
            for hb in range(8):
                ckpt()
                for j8 in range(8):
                    jj = 8 * hb + j8
                    blocks = []
                    if full:
                        blocks.append((GATE0 + 128 * jj, "silu",
                                       gsall[:, TG * j8:TG * j8 + TG]))
                    blocks.append((HS0 + 128 * jj, "f32", hspre[:, 3:3 + TG]))
                    proj_stage(blocks)

                    def cvh(v, jj=jj, g=g):
                        inc(conv_into(v, jj, hspre, g % 2), "vs", 1)
                    on("vector", cvh, deps=("ts",))

                    def svh(s, j8=j8):
                        inc(s.activation(out=hsall[:, TG * j8:TG * j8 + TG],
                                         in_=convs[:], func=AF.Silu), "ss", 1)
                    on("scalar", svh, deps=("vs",))

                if g == npref and hb == 0:
                    tap("hsall6", hsall[:], [128, 8 * TG])
                    tap("gsall6", gsall[:], [128, 8 * TG])
                if full:
                    for c in range(4):
                        def g1tc(t, hb=hb, c=c):
                            cs = slice(TG * hb + 128 * c, TG * hb + 128 * c + 128)
                            inc(t.matmul(pE[:, 0:128], bsil[:, cs], csil[:, cs],
                                         start=True, stop=True), "ts", 1)
                        on("tensor", g1tc, deps=("vs", "ss"))

                        def g1vc(v, c=c):
                            inc(v.tensor_mul(out=g1m[:, 128 * c:128 * c + 128],
                                             in0=pE[:, 0:128], in1=triu[:]), "vs", 1)
                        on("vector", g1vc, deps=("ts",))
                    if full and g == npref and hb == 0:
                        tap("g1m0", g1m[:], [128, 512])

                for c in range(4):
                    if full:
                        def acg(sync, hb=hb, c=c):
                            cs = slice(128 * c, 128 * c + 128)
                            inc(sync.dma_start(out=acrow[0:1, :],
                                               in_=acum[16 * hb:16 * hb + 16, cs]),
                                "ld", 16)
                        on("sync", acg, deps=("vs", "ts"))

                    for j8 in range(8):
                        jj = 8 * hb + j8

                        def trh(t, j8=j8, full=full, c=c):
                            mmi = t.transpose(
                                pT2[:, 0:128],
                                hsall[:, TG * j8 + 128 * c:TG * j8 + 128 * c + 128],
                                ident[:])
                            if full:
                                mmi = t.transpose(
                                    pT2[:, 128:256],
                                    gsall[:, TG * j8 + 128 * c:TG * j8 + 128 * c + 128],
                                    ident[:])
                            inc(mmi, "ts", 1)
                        on("tensor", trh, deps=("vs", "ss"))

                        def hcm(v, jj=jj, j8=j8, c=c, full=full):
                            for hh in range(2):
                                h = 2 * jj + hh
                                band = slice(64 * hh, 64 * hh + 64)
                                i1 = v.tensor_scalar(
                                    out=xchp[:, band], in0=pT2[:, band],
                                    scalar1=dd1T[:, 128 * c + h:128 * c + h + 1],
                                    scalar2=None, op0=ALU.mult)
                            if full:
                                for hh in range(2):
                                    h = 2 * jj + hh
                                    band = slice(64 * hh, 64 * hh + 64)
                                    v.tensor_scalar(out=xch_cm[:, band], in0=pT2[:, band],
                                                    scalar1=dtT[:, 128 * c + h:128 * c + h + 1],
                                                    scalar2=None, op0=ALU.mult)
                                dg = gatm[:].rearrange("p (k r) -> p k r", k=4)[
                                    :, c:c + 1, 128 * j8:128 * j8 + 128]
                                v.tensor_copy(out=dg, in_=pT2[:, 128:256].unsqueeze(1))
                                i1 = v.tensor_scalar(
                                    out=hsdt[:],
                                    in0=hsall[:, TG * j8 + 128 * c:TG * j8 + 128 * c + 128],
                                    scalar1=dxp[:, jj:jj + 1], scalar2=None, op0=ALU.mult)
                            inc(i1, "vs", 1)
                        on("vector", hcm, deps=("ts",))
                        if g == npref and hb == 0 and c == 0 and j8 == 0:
                            tap("xchp0", xchp[:], [128, 128])
                            tap("xcm0", xch_cm[:], [128, 128])
                            tap("hsdt0", hsdt[:], [128, 128])

                        if full:
                            def sst(t, jj=jj, j8=j8, c=c, hb=hb):
                                t.transpose(pF[:, 0:128],
                                            s_run[:, 128 * jj:128 * jj + 128], identf[:])
                                for hh in range(2):
                                    lh = 2 * j8 + hh
                                    t.matmul(pF[:, 128 + 128 * hh:256 + 128 * hh], onesr[:],
                                             acrow[0:1, 128 * lh:128 * lh + 128],
                                             start=True, stop=True)
                                inc(t.matmul(pE[:, 0:128], xchp[:],
                                             bcm[:, 128 * (8 * c + hb):128 * (8 * c + hb) + 128],
                                             start=True, stop=True), "ts", 1)
                            on("tensor", sst, deps=("vs", "ss", "ld"))
                            if g == npref and hb == 0 and c == 0 and j8 == 0:
                                def dcp1(v):
                                    v.tensor_copy(out=ggf[:, 0:384], in_=pF[:, 0:384])
                                    inc(v.tensor_copy(out=ggf[:, 384:512],
                                                      in_=pE[:, 0:128]), "vs", 1)
                                on("vector", dcp1)
                                tap("pFE0", ggf[:, 0:512], [128, 512])
                                tap("acrow0", acrow[0:1, :], [1, 2048])

                            def ssa(s, jj=jj, c=c):
                                for hh in range(2):
                                    h = 2 * jj + hh
                                    ii = s.activation(
                                        out=relu_s[:, 128 * hh:128 * hh + 128],
                                        in_=pF[:, 128 + 128 * hh:256 + 128 * hh],
                                        func=AF.Relu, scale=cp(-1.0),
                                        bias=acumT[:, 128 * c + h:128 * c + h + 1])
                                inc(ii, "ss", 1)
                                wt(s, "ss")
                                for hh in range(2):
                                    ii = s.activation(out=relu_s[:, 128 * hh:128 * hh + 128],
                                                      in_=relu_s[:, 128 * hh:128 * hh + 128],
                                                      func=AF.Exp, scale=cp(-1.0))
                                inc(ii, "ss", 1)
                            on("scalar", ssa, deps=("ts",))
                            if g == npref and hb == 0 and c == 0 and j8 == 0:
                                tap("relu0", relu_s[:], [128, 256])

                            def ssv(v, jj=jj, c=c):
                                v.tensor_copy(out=prevT[:], in_=pF[:, 0:128])
                                for hh in range(2):
                                    h = 2 * jj + hh
                                    band = slice(64 * hh, 64 * hh + 64)
                                    v.tensor_scalar(
                                        out=expacx[:, band], in0=ones64[:],
                                        scalar1=expacT[:, 128 * c + h:128 * c + h + 1],
                                        scalar2=None, op0=ALU.mult)
                                for hh in range(2):
                                    ii = v.tensor_mul(out=mh[:, 128 * hh:128 * hh + 128],
                                                      in0=relu_s[:, 128 * hh:128 * hh + 128],
                                                      in1=g1m[:, 128 * c:128 * c + 128])
                                inc(ii, "vs", 1)
                            on("vector", ssv, deps=("ss",))
                            if g == npref and hb == 0 and c == 0 and j8 == 0:
                                tap("mh0", mh[:], [128, 256])
                                tap("prevT0", prevT[:], [128, 128])
                                tap("expacx0", expacx[:], [128, 128])

                            def sst2(t, jj=jj, c=c, hb=hb):
                                for hh in range(2):
                                    t.matmul(pC[:, 64 * hh:64 * hh + 64],
                                             mh[:, 128 * hh:128 * hh + 128],
                                             xch_cm[:, 64 * hh:64 * hh + 64],
                                             start=(hh == 0), stop=False)
                                t.matmul(pC[:, 0:128], hsdt[:], ident[:],
                                         start=False, stop=True)
                                inc(t.matmul(pD[:, 0:128],
                                             csil[:, TG * hb + 128 * c:TG * hb + 128 * c + 128],
                                             prevT[:], start=True, stop=True), "ts", 1)
                            on("tensor", sst2, deps=("vs",))
                            if g == npref and hb == 0 and c == 0 and j8 == 0:
                                def dcp2(v):
                                    v.tensor_copy(out=ggf[:, 0:128], in_=pC[:, 0:128])
                                    inc(v.tensor_copy(out=ggf[:, 128:256],
                                                      in_=pD[:, 0:128]), "vs", 1)
                                on("vector", dcp2)
                                tap("pCD0", ggf[:, 0:256], [128, 256])

                            def ssv2(v, jj=jj, c=c, j8=j8):
                                v.tensor_mul(out=yoffc[:], in0=pD[:, 0:128], in1=expacx[:])
                                inc(v.scalar_tensor_tensor(
                                    out=s_run[:, 128 * jj:128 * jj + 128],
                                    in0=s_run[:, 128 * jj:128 * jj + 128],
                                    scalar=expal[:, 64 * c + jj:64 * c + jj + 1],
                                    in1=pE[:, 0:128], op0=ALU.mult, op1=ALU.add), "vs", 1)
                                wt(v, "vs")
                                inc(v.tensor_add(
                                    out=ytm[:, 1024 * c + 128 * j8:1024 * c + 128 * j8 + 128],
                                    in0=pC[:, 0:128], in1=yoffc[:]), "vs", 1)
                            on("vector", ssv2, deps=("ts",))
                            if g == npref and hb == 0 and c == 0 and j8 == 0:
                                tap("yoffc0", yoffc[:], [128, 128])
                        else:
                            def sstp(t, c=c, hb=hb):
                                inc(t.matmul(pE[:, 0:128], xchp[:],
                                             bcm[:, 128 * (8 * c + hb):128 * (8 * c + hb) + 128],
                                             start=True, stop=True), "ts", 1)
                            on("tensor", sstp, deps=("vs",))

                            def ssvp(v, jj=jj, c=c):
                                inc(v.scalar_tensor_tensor(
                                    out=s_run[:, 128 * jj:128 * jj + 128],
                                    in0=s_run[:, 128 * jj:128 * jj + 128],
                                    scalar=expal[:, 64 * c + jj:64 * c + jj + 1],
                                    in1=pE[:, 0:128], op0=ALU.mult, op1=ALU.add), "vs", 1)
                            on("vector", ssvp, deps=("ts",))

                    if full:
                        def ggv(v, c=c):
                            inc(v.tensor_mul(out=ggf[:], in0=ytm[:, 1024 * c:1024 * c + 1024],
                                             in1=gatm[:, 1024 * c:1024 * c + 1024]), "vs", 1)
                        on("vector", ggv, deps=("vs", "ts", "ss"))

                        def ggs(s, c=c):
                            inc(s.activation(out=ytm[:, 1024 * c:1024 * c + 1024],
                                             in_=ggf[:], func=AF.Square,
                                             accum_out=rmsq[:, c:c + 1]), "ss", 1)
                            wt(s, "ss")
                            inc(s.activation(out=rmsq[:, 4 + c:5 + c], in_=rmsq[:, c:c + 1],
                                             func=AF.Ln, scale=cp(1.0 / 1024.0),
                                             bias=cp(EPS)), "ss", 1)
                            wt(s, "ss")
                            inc(s.activation(out=rmsq[:, 4 + c:5 + c],
                                             in_=rmsq[:, 4 + c:5 + c],
                                             func=AF.Exp, scale=cp(-0.5)), "ss", 1)
                        on("scalar", ggs, deps=("vs",))

                        def ggv2(v, c=c):
                            inc(v.tensor_scalar(out=ggb[:], in0=ggf[:],
                                                scalar1=rmsq[:, 4 + c:5 + c], scalar2=None,
                                                op0=ALU.mult), "vs", 1)
                        on("vector", ggv2, deps=("ss",))

                        def ggt(t):
                            for j8 in range(8):
                                mmi = t.transpose(pT1[:, 128 * j8:128 * j8 + 128],
                                                  ggb[:, 128 * j8:128 * j8 + 128], ident[:])
                            inc(mmi, "ts", 1)
                        on("tensor", ggt, deps=("vs",))

                        def ggv3(v, c=c, hb=hb):
                            for j8 in range(8):
                                jj = 8 * hb + j8
                                ii = v.tensor_scalar(
                                    out=ggtb[:, TG * j8 + 128 * c:TG * j8 + 128 * c + 128],
                                    in0=pT1[:, 128 * j8:128 * j8 + 128],
                                    scalar1=nwT[:, jj:jj + 1], scalar2=None, op0=ALU.mult)
                            inc(ii, "vs", 1)
                        on("vector", ggv3, deps=("ts",))

                if g == npref and hb == 0:
                    tap("ytm6", ytm[:], [128, 4 * 1024])
                    tap("gatm6", gatm[:], [128, 4 * 1024])
                    tap("ggtb6", ggtb[:], [128, 8 * TG])
                if full:
                    def ggw(sync, hb=hb):
                        inc(sync.dma_start(
                            out=gg_scr[8 * hb:8 * hb + 8].rearrange("k p c -> p k c"),
                            in_=ggtb[:].rearrange("p (k c) -> p k c", k=8)), "st", 16)
                    on("sync", ggw, deps=("vs",))

            tap(f"sr_b{g}", s_run[:], [128, 64 * 128])
            ckpt()

            if full:
                for db in range(8):
                    base_ld = {0: cnt["ld"], 1: cnt["ld2"]}
                    base_ts = cnt["ts"]

                    def owl(sync, db=db, base_ts=base_ts):
                        for kk in range(64):
                            if kk >= 2:
                                sync.wait_ge(sem["ts"], base_ts + (kk - 2) + 1)
                            ls_ = "ld" if kk % 2 == 0 else "ld2"
                            inc(sync.dma_start(
                                out=obuf[:, 512 * (kk % 2):512 * (kk % 2) + 512],
                                in_=o_scr[kk][:, 512 * db:512 * db + 512]), ls_, 16)
                            inc(sync.dma_start(
                                out=ggl[:, 512 * (kk % 2):512 * (kk % 2) + 512],
                                in_=gg_scr[kk]), ls_, 16)
                    on("sync", owl, deps=("ts", "vs", "st"))

                    def owt(t, base_ld=base_ld):
                        for kk in range(64):
                            ls_ = "ld" if kk % 2 == 0 else "ld2"
                            t.wait_ge(sem[ls_], base_ld[kk % 2] + 32 * (kk // 2 + 1))
                            for tb in range(4):
                                mmi = t.matmul(
                                    PBANKS[tb][:],
                                    ggl[:, 512 * (kk % 2) + 128 * tb:512 * (kk % 2) + 128 * tb + 128],
                                    obuf[:, 512 * (kk % 2):512 * (kk % 2) + 512],
                                    start=(kk == 0), stop=(kk == 63))
                            inc(mmi, "ts", 1)
                    onx("tensor", owt, ("vs",))

                    for tb in range(4):
                        def exl(sync, g=g, db=db, tb=tb):
                            r0 = TG * g + 128 * tb
                            inc(sync.dma_start(out=xsl[:],
                                               in_=xin[r0:r0 + 128, 512 * db:512 * db + 512]),
                                "ld", 16)
                        on("sync", exl, deps=("vs",))

                        def exv(v, tb=tb):
                            inc(v.scalar_tensor_tensor(out=outt[:], in0=PBANKS[tb][:],
                                                       scalar=1.0, in1=xsl[:],
                                                       op0=ALU.mult, op1=ALU.add), "vs", 1)
                        on("vector", exv, deps=("ts", "ld", "st"))

                        def exw(sync, db=db, tb=tb):
                            inc(sync.dma_start(out=x1_scr[tb][:, 512 * db:512 * db + 512],
                                               in_=outt[:]), "st", 16)
                        on("sync", exw, deps=("vs",))

                if g == npref:
                    for tt in range(4):
                        tap(f"x1_{tt}", x1_scr[tt], [128, D])

                ln_stage(g, True)
                if g == npref:
                    tap("xln2T6", xlnT[:], [128, 32 * TG])

                for ib in range(86):
                    def fl(sync, ib=ib):
                        inc(sync.dma_start(out=fwbuf[:, 0:4096], in_=g_scr[ib]), "ld", 16)
                        inc(sync.dma_start(out=fwbuf[:, 4096:8192], in_=u_scr[ib]), "ld", 16)
                    on("sync", fl, deps=("ts", "vs", "ss"))

                    def ft(t):
                        for ks in range(32):
                            t.matmul(pA[:], fwbuf[:, 128 * ks:128 * ks + 128],
                                     xlnT[:, TG * ks:TG * ks + TG],
                                     start=(ks == 0), stop=(ks == 31))
                        for ks in range(32):
                            mmi = t.matmul(pB[:], fwbuf[:, 4096 + 128 * ks:4096 + 128 * ks + 128],
                                           xlnT[:, TG * ks:TG * ks + TG],
                                           start=(ks == 0), stop=(ks == 31))
                        inc(mmi, "ts", 1)
                    on("tensor", ft, deps=("ld",))

                    def fs(s):
                        inc(s.activation(out=fsil[:], in_=pA[:], func=AF.Silu), "ss", 1)
                    on("scalar", fs, deps=("ts",))

                    def fv(v, ib=ib):
                        inc(v.tensor_mul(out=hsw[:, TG * ib:TG * ib + TG], in0=pB[:],
                                         in1=fsil[:]), "vs", 1)
                    on("vector", fv, deps=("ss",))

                if g == npref:
                    tap("hsw6", hsw[:, 0:4 * TG], [128, 4 * TG])

                for dc in range(8):
                    for tb8 in range(11):
                        nt = 8 if tb8 < 10 else 6
                        t0_ = 8 * tb8

                        def fdl(sync, dc=dc, t0_=t0_, nt=nt):
                            inc(sync.dma_start(
                                out=fwbuf2[:, 0:512 * nt].rearrange("p (t c) -> p t c", t=nt),
                                in_=d_scr[dc, t0_:t0_ + nt].rearrange("t p c -> p t c")),
                                "ld", 16)
                        on("sync", fdl, deps=("ts",))

                        def fdt(t, t0_=t0_, nt=nt):
                            for ti in range(nt):
                                tsub = t0_ + ti
                                for tb in range(4):
                                    mmi = t.matmul(
                                        PBANKS[tb][:],
                                        hsw[:, TG * tsub + 128 * tb:TG * tsub + 128 * tb + 128],
                                        fwbuf2[:, 512 * ti:512 * ti + 512],
                                        start=(tsub == 0), stop=(tsub == 85))
                            inc(mmi, "ts", 1)
                        on("tensor", fdt, deps=("ld", "ss", "vs"))

                    for tb in range(4):
                        def fxl(sync, dc=dc, tb=tb):
                            inc(sync.dma_start(out=xsl[:],
                                               in_=x1_scr[tb][:, 512 * dc:512 * dc + 512]),
                                "ld", 16)
                        on("sync", fxl, deps=("vs",))

                        def fxv(v, tb=tb):
                            inc(v.scalar_tensor_tensor(out=outt[:], in0=PBANKS[tb][:],
                                                       scalar=1.0, in1=xsl[:],
                                                       op0=ALU.mult, op1=ALU.add), "vs", 1)
                        on("vector", fxv, deps=("ts", "ld", "st"))

                        def fxw(sync, g=g, dc=dc, tb=tb):
                            r0 = TG * (g - npref) + 128 * tb
                            inc(sync.dma_start(out=yout[r0:r0 + 128, 512 * dc:512 * dc + 512],
                                               in_=outt[:]), "st", 16)
                        on("sync", fxw, deps=("vs",))

        try:
            _emit_groups()
        except _StopEmit:
            pass
        def fin(sync):
            wt(sync, "st")
            wt(sync, "cast2")
        on("sync", fin)

    return nc, dbg



def host_prep(inputs):
    W = np.asarray(inputs["L0_attn_in_proj_w"])
    inprojT = np.concatenate([
        W[8192:16384].T, W[16384:17408].T, W[18432:18560].T,
        W[0:8192].T, W[17408:18432].T], axis=1)
    inprojT = np.ascontiguousarray(inprojT, dtype=np.float32)
    outprojT = np.ascontiguousarray(np.asarray(inputs["L0_attn_out_proj_w"]).T,
                                    dtype=np.float32)
    convw = np.ascontiguousarray(
        np.asarray(inputs["L0_attn_conv_w"])[:, 0, :].reshape(80, 128, 4)
        .transpose(1, 0, 2).reshape(128, 80 * 4), dtype=np.float32)
    convb = np.ascontiguousarray(
        np.asarray(inputs["L0_attn_conv_b"]).reshape(80, 128).T, dtype=np.float32)
    dtbias = np.ascontiguousarray(np.asarray(inputs["L0_attn_dt_bias"]).reshape(128, 1),
                                  dtype=np.float32)
    alogv = np.ascontiguousarray(np.asarray(inputs["L0_attn_A_log"]).reshape(128, 1),
                                 dtype=np.float32)
    Dv = np.asarray(inputs["L0_attn_D"]).astype(np.float32)
    dexp = np.empty((128, 64), dtype=np.float32)
    for j in range(64):
        dexp[0:64, j] = Dv[2 * j]
        dexp[64:128, j] = Dv[2 * j + 1]
    normw = np.asarray(inputs["L0_attn_norm_w"]).astype(np.float32)
    normwT = np.ascontiguousarray(normw.reshape(64, 128).T, dtype=np.float32)

    tp = tau_perm(FI)

    def u8(x):
        a = np.asarray(x)
        return np.ascontiguousarray(a.view(np.uint8).reshape(a.shape[0], a.shape[1], 4)[:, :, 0])

    gqv = np.ascontiguousarray(u8(inputs["L0_ffn_gate_Wq"])[tp])
    uqv = np.ascontiguousarray(u8(inputs["L0_ffn_up_Wq"])[tp])
    dqv = u8(inputs["L0_ffn_down_Wq"])
    gscv = np.ascontiguousarray(np.asarray(inputs["L0_ffn_gate_scale"], dtype=np.float32)[tp])
    gzrv = np.ascontiguousarray(np.asarray(inputs["L0_ffn_gate_zero"], dtype=np.float32)[tp])
    uscv = np.ascontiguousarray(np.asarray(inputs["L0_ffn_up_scale"], dtype=np.float32)[tp])
    uzrv = np.ascontiguousarray(np.asarray(inputs["L0_ffn_up_zero"], dtype=np.float32)[tp])
    dscv = np.ascontiguousarray(np.asarray(inputs["L0_ffn_down_scale"], dtype=np.float32))
    dzrv = np.ascontiguousarray(np.asarray(inputs["L0_ffn_down_zero"], dtype=np.float32))

    shared = dict(inprojT=inprojT, outprojT=outprojT, convw=convw, convb=convb,
                  dtbias=dtbias, alog=alogv, dexp=dexp, normwT=normwT,
                  gq=gqv, uq=uqv, dq=dqv, gsc=gscv, gzr=gzrv, usc=uscv, uzr=uzrv,
                  dsc=dscv, dzr=dzrv)

    x = np.asarray(inputs["L0_attn_x"], dtype=np.float32)
    own_len = (NGRP - NPREF) * TG
    in_maps = []
    for core in range(8):
        b, q = core // 4, core % 4
        xb = x[b]
        npref_real = min((own_len // 128) * q, NPREF * 4)
        njunk = NPREF * 4 - npref_real
        parts = []
        if njunk > 0:
            parts.append(np.zeros((njunk * 128, D), dtype=np.float32))
        if npref_real > 0:
            parts.append(xb[0:npref_real * 128])
        parts.append(xb[own_len * q:own_len * (q + 1)])
        xinv = np.ascontiguousarray(np.concatenate(parts, axis=0))
        mask = np.zeros(NGRP * 4, dtype=np.float32)
        mask[njunk:] = 1.0
        maskcv = np.ascontiguousarray(np.tile(mask[None, :], (128, 1)).astype(np.float32))
        m = dict(shared)
        m["xin"] = xinv
        m["maskc"] = maskcv
        in_maps.append(m)
    return in_maps


def kernel(**inputs):
    in_maps = host_prep(inputs)
    nc, _ = build()
    res = run_bass_kernel_spmd(nc, in_maps, core_ids=list(range(8)))
    out = np.empty((2, 4096, 4096), dtype=np.float32)
    for core in range(8):
        b, q = core // 4, core % 4
        out[b, 1024 * q:1024 * (q + 1)] = res.results[core]["yout"]
    return out


# version 03
# speedup vs baseline: 1.0904x; 1.0904x over previous
import numpy as np
import ml_dtypes
import concourse.bass as bass
import concourse.mybir as mybir
from concourse.bass_utils import run_bass_kernel_spmd

F32 = mybir.dt.float32
BF16 = mybir.dt.bfloat16
U8 = mybir.dt.uint8
AF = mybir.ActivationFunctionType
ALU = mybir.AluOpType

D = 4096
INTER = 8192
PROJ = 18560
FI = 11008
TG = 512
NGRP = 8
NPREF = 6
EPS = 1e-5
MAGIC = 8388608.0 + 4194304.0

HS0, B0, DT0, GATE0, C0 = 0, 8192, 9216, 9344, 17536

BF16NP = ml_dtypes.bfloat16


def tau_perm(n):
    out = np.empty(n, dtype=np.int64)
    ar = 2 * np.arange(128)
    for m in range(n // 256):
        out[m * 256:m * 256 + 128] = m * 256 + ar
        out[m * 256 + 128:m * 256 + 256] = m * 256 + ar + 1
    return out


class _StopEmit(Exception):
    pass


def build(debug=(), skip_phase_a=False, only_ln=False, limit=None):
    ngrp, npref = NGRP, NPREF
    nfull = ngrp - npref
    nc = bass.Bass("TRN2", target_bir_lowering=False)

    xin = nc.dram_tensor("xin", [ngrp * TG, D], F32, kind="ExternalInput").ap()
    maskc = nc.dram_tensor("maskc", [128, ngrp * 4], F32, kind="ExternalInput").ap()
    inprojT = nc.dram_tensor("inprojT", [D, PROJ], F32, kind="ExternalInput").ap()
    outprojT = nc.dram_tensor("outprojT", [INTER, D], F32, kind="ExternalInput").ap()
    convw = nc.dram_tensor("convw", [128, 80 * 4], F32, kind="ExternalInput").ap()
    convb = nc.dram_tensor("convb", [128, 80], F32, kind="ExternalInput").ap()
    dtbias = nc.dram_tensor("dtbias", [128, 1], F32, kind="ExternalInput").ap()
    alog = nc.dram_tensor("alog", [128, 1], F32, kind="ExternalInput").ap()
    dexp = nc.dram_tensor("dexp", [128, 64], F32, kind="ExternalInput").ap()
    normwT = nc.dram_tensor("normwT", [128, 64], F32, kind="ExternalInput").ap()
    gq = nc.dram_tensor("gq", [FI, D // 2], U8, kind="ExternalInput").ap()
    uq = nc.dram_tensor("uq", [FI, D // 2], U8, kind="ExternalInput").ap()
    dq = nc.dram_tensor("dq", [D, FI // 2], U8, kind="ExternalInput").ap()
    gsc = nc.dram_tensor("gsc", [FI, 64], F32, kind="ExternalInput").ap()
    gzr = nc.dram_tensor("gzr", [FI, 64], F32, kind="ExternalInput").ap()
    usc = nc.dram_tensor("usc", [FI, 64], F32, kind="ExternalInput").ap()
    uzr = nc.dram_tensor("uzr", [FI, 64], F32, kind="ExternalInput").ap()
    dsc = nc.dram_tensor("dsc", [D, 172], F32, kind="ExternalInput").ap()
    dzr = nc.dram_tensor("dzr", [D, 172], F32, kind="ExternalInput").ap()

    yout = nc.dram_tensor("yout", [nfull * TG, D], F32, kind="ExternalOutput").ap()

    dbg = {}

    def dbgt(name, shape):
        if name in debug and name not in dbg:
            dbg[name] = nc.dram_tensor("dbg_" + name, shape, F32, kind="ExternalOutput").ap()
        return dbg.get(name)

    g_scr = nc.dram_tensor("g_scr", [86, 128, 4096], BF16, kind="Internal").ap()
    u_scr = nc.dram_tensor("u_scr", [86, 128, 4096], BF16, kind="Internal").ap()
    d_scr = nc.dram_tensor("d_scr", [8, 86, 128, 512], BF16, kind="Internal").ap()
    o_scr = nc.dram_tensor("o_scr", [64, 128, 4096], BF16, kind="Internal").ap()
    gg_scr = nc.dram_tensor("gg_scr", [64, 128, TG], BF16, kind="Internal").ap()
    x1_scr = nc.dram_tensor("x1_scr", [4, 128, D], F32, kind="Internal").ap()

    ident_c = nc.inline_tensor(np.eye(128, dtype=BF16NP), "identc")
    identf_c = nc.inline_tensor(np.eye(128, dtype=np.float32), "identfc")
    ones_row_c = nc.inline_tensor(np.ones((1, 128), dtype=np.float32), "onesrowc")
    ones64_c = nc.inline_tensor(np.ones((128, 64), dtype=np.float32), "ones64c")
    triu_c = nc.inline_tensor(
        np.triu(np.ones((128, 128), dtype=np.float32)).astype(BF16NP), "triuc")
    epj = np.zeros((128, 64), dtype=BF16NP)
    for h in range(128):
        epj[h, h // 2] = 1
    epj01_c = nc.inline_tensor(epj, "epj01c")
    fp_ = np.zeros((128, 128), dtype=BF16NP)
    for h in range(128):
        fp_[h, (h % 2) * 64:(h % 2) * 64 + 64] = 1
    fpair_c = nc.inline_tensor(fp_, "fpairc")

    CVALS = [-1.0, -0.5, 1.0 / 1024.0, 1.0 / 16.0, 16.0, MAGIC, -16.0 * MAGIC,
             -0.46875, EPS, 1.0]
    CIDX = {v: i for i, v in enumerate(CVALS)}
    consts_c = nc.inline_tensor(
        np.tile(np.array(CVALS, dtype=np.float32)[None, :], (128, 1)), "constsc")

    A = nc.alloc_sbuf_tensor
    ident = A("ident", [128, 128], BF16)
    identf = A("identf", [128, 128], F32)
    onesr = A("onesr", [1, 128], F32)
    ones64 = A("ones64", [128, 64], F32)
    triu = A("triu", [128, 128], BF16)
    epj01 = A("epj01", [128, 64], BF16)
    fpair = A("fpair", [128, 128], BF16)
    cpool = A("cpool", [128, len(CVALS)], F32)
    cw = A("cw", [128, 80 * 4], F32)
    cb = A("cb", [128, 80], F32)
    dtb = A("dtb", [128, 1], F32)
    acol = A("acol", [128, 1], F32)
    dxp = A("dxp", [128, 64], F32)
    nwT = A("nwT", [128, 64], F32)
    mcol = A("mcol", [128, ngrp * 4], F32)
    zeros128 = A("zeros128", [128, 128], F32)
    lnst = A("lnst", [128, 8], F32)
    rmsq = A("rmsq", [128, 8], F32)
    s_run = A("s_run", [128, 64 * 128], F32)
    halo = A("halo", [128, 2 * 80 * 3], F32)
    xlnT = A("xlnT", [128, 32 * TG], BF16)

    def cp(v):
        return cpool[:, CIDX[v]:CIDX[v] + 1]

    ARENA = 136 * 1024
    arena = A("arena", [128, ARENA], U8)

    class Rg:
        def __init__(self):
            self.off = 0

        def al(self, nbytes, align=64):
            o = (self.off + align - 1) // align * align
            self.off = o + nbytes
            assert self.off <= ARENA, f"arena overflow {self.off}"
            return o

    def view(off, nelem, dtype, parts=128):
        esz = {F32: 4, BF16: 2, U8: 1}[dtype]
        return arena[0:parts, off:off + nelem * esz].bitcast(dtype)

    ra = Rg()
    qb = view(ra.al(2048 * 4), 2048, F32)
    qh = view(ra.al(2048 * 4), 2048, F32)
    ql = view(ra.al(2048 * 4), 2048, F32)
    qtmp = qb
    qwhi = view(ra.al(2048 * 2), 2048, BF16)
    qwlo = view(ra.al(2048 * 2), 2048, BF16)
    qstg = view(ra.al(4096 * 2), 4096, BF16)
    qsc = view(ra.al(64 * 4), 64, F32)
    qsc16 = view(ra.al(64 * 4), 64, F32)
    qzs = view(ra.al(64 * 4), 64, F32)

    ra2 = Rg()
    qb2 = view(ra2.al(5504 * 4), 5504, F32)
    qh2 = view(ra2.al(5504 * 4), 5504, F32)
    ql2 = view(ra2.al(5504 * 4), 5504, F32)
    qtmp2 = qb2
    qwhi2 = view(ra2.al(2 * 5504 * 2), 2 * 5504, BF16)
    qwlo2 = view(ra2.al(2 * 5504 * 2), 2 * 5504, BF16)
    qstg2 = view(ra2.al(512 * 2), 512, BF16)
    qsc2 = view(ra2.al(172 * 4), 172, F32)
    qsc216 = view(ra2.al(172 * 4), 172, F32)
    qzs2 = view(ra2.al(172 * 4), 172, F32)
    owstg = view(ra2.al(4096 * 2), 4096, BF16)

    rm = Rg()
    wbuf = view(rm.al(2 * 4096 * 2), 2 * 4096, BF16)
    bcm = view(rm.al(32 * 128 * 2), 32 * 128, BF16)
    hsall = view(rm.al(8 * TG * 2), 8 * TG, BF16)
    gsall = view(rm.al(8 * TG * 2), 8 * TG, BF16)
    bsil = view(rm.al(8 * TG * 2), 8 * TG, BF16)
    csil = view(rm.al(8 * TG * 2), 8 * TG, BF16)
    bcpre = view(rm.al((3 + TG) * 4), 3 + TG, F32)
    ggtb = view(rm.al(8 * TG * 2), 8 * TG, BF16)
    dtpre = view(rm.al(TG * 4), TG, F32)
    dtt = view(rm.al(TG * 4), TG, F32)
    adt = view(rm.al(TG * 4), TG, F32)
    acum = view(rm.al(TG * 4), TG, F32)
    acumT = view(rm.al(4 * 128 * 4), 4 * 128, F32)
    d1T = view(rm.al(4 * 128 * 4), 4 * 128, F32)
    dtT = view(rm.al(4 * 128 * 4), 4 * 128, F32)
    dd1T = view(rm.al(4 * 128 * 4), 4 * 128, F32)
    dscr = view(rm.al(TG * 4), TG, F32)
    expacT = view(rm.al(4 * 128 * 4), 4 * 128, F32)
    alast = view(rm.al(4 * 4), 4, F32)
    exal = view(rm.al(4 * 4), 4, F32)
    gexp = view(rm.al(64 * 2), 64, BF16)
    expal = view(rm.al(4 * 64 * 4), 4 * 64, F32)
    hspre = view(rm.al((3 + TG) * 4), 3 + TG, F32)
    convo = view(rm.al(TG * 4), TG, F32)
    convo2 = view(rm.al(TG * 4), TG, F32)
    convs = view(rm.al(TG * 4), TG, F32)
    xch_cm = view(rm.al(128 * 2), 128, BF16)
    xchp = view(rm.al(128 * 2), 128, BF16)
    hsdt = view(rm.al(128 * 2), 128, BF16)
    g1m = view(rm.al(4 * 128 * 2), 4 * 128, BF16)
    mh = view(rm.al(2 * 128 * 2), 2 * 128, BF16)
    relu_s = view(rm.al(2 * 128 * 4), 2 * 128, F32)
    prevT = view(rm.al(128 * 2), 128, BF16)
    yoffc = view(rm.al(128 * 4), 128, F32)
    expacx = view(rm.al(128 * 2), 128, BF16)
    ggf = view(rm.al(1024 * 2), 1024, BF16)
    ggb = view(rm.al(1024 * 2), 1024, BF16)
    acrow = view(rm.al(2048 * 4), 2048, F32, parts=1)
    sub_o = rm.al(max(D * 4 + D * 2, 2 * 4 * 1024 * 2))
    xt = view(sub_o, D, F32)
    xlntm = view(sub_o + D * 4, D, BF16)
    ytm = view(sub_o, 4 * 1024, BF16)
    gatm = view(sub_o + 4 * 1024 * 2, 4 * 1024, BF16)
    mix_end = rm.off

    rf = Rg()
    hsw = view(rf.al(86 * TG * 2), 86 * TG, BF16)
    fwbuf = view(rf.al(2 * 4096 * 2), 2 * 4096, BF16)
    fwbuf2 = view(rf.al(8 * 512 * 2), 8 * 512, BF16)
    fsil = view(rf.al(TG * 2), TG, BF16)
    xsl = view(rf.al(512 * 4), 512, F32)
    outt = view(rf.al(512 * 4), 512, F32)
    obuf = view(rf.al(2 * 512 * 2), 2 * 512, BF16)
    ggl = view(rf.al(2 * 512 * 2), 2 * 512, BF16)
    ffn_end = rf.off

    PS = nc.alloc_psum_tensor
    pA = PS("pA", [128, 512], F32)
    pB = PS("pB", [128, 512], F32)
    pC = PS("pC", [128, 512], F32)
    pD = PS("pD", [128, 512], F32)
    pE = PS("pE", [128, 512], F32)
    pF = PS("pF", [128, 512], F32)
    pT1 = PS("pT1", [128, 1024], BF16)
    pT2 = PS("pT2", [128, 1024], BF16)
    PBANKS = [pA, pB, pC, pD]

    with nc.Block() as block:
        sem = {}
        cnt = {}
        for s in ("ld", "ld2", "cast", "cast2", "ts", "vs", "ss", "st"):
            sem[s] = nc.alloc_semaphore(s)
            cnt[s] = 0

        def inc(instr, s, n=1):
            instr.then_inc(sem[s], n)
            cnt[s] += n
            return instr

        def wt(eng, s, val=None):
            v = cnt[s] if val is None else val
            if v > 0:
                eng.wait_ge(sem[s], v)

        ALLSEMS = ("ld", "ld2", "cast", "cast2", "ts", "vs", "ss", "st")

        def on(eng_attr, fn, deps=None):
            if deps is not None and debug:
                deps = tuple(deps) + ("cast2", "st")

            def wrapped(eng):
                for d in (ALLSEMS if deps is None else deps):
                    wt(eng, d)
                fn(eng)
            getattr(block, eng_attr)(wrapped)

        def onx(eng_attr, fn, deps):
            if debug:
                deps = tuple(deps) + ("cast2", "st")

            def wrapped(eng):
                for d in deps:
                    wt(eng, d)
                fn(eng)
            getattr(block, eng_attr)(wrapped)

        def tap(name, src_ap, shape):
            t_ = dbgt(name, shape)
            if t_ is None:
                return

            def tp_(gp_, t_=t_, src_ap=src_ap):
                inc(gp_.dma_start(out=t_, in_=src_ap), "cast2", 16)
            on("gpsimd", tp_, deps=("vs", "ss", "ts", "ld", "st"))

        def tap_s(name, src_ap, shape):
            t_ = dbgt(name, shape)
            if t_ is None:
                return

            def tp_(sy_, t_=t_, src_ap=src_ap):
                inc(sy_.dma_start(out=t_, in_=src_ap), "st", 16)
            on("sync", tp_, deps=("vs", "ss", "ts", "ld", "st"))

        def p0_sync(sync):
            for dst, src in ((ident, ident_c), (identf, identf_c), (onesr, ones_row_c),
                             (ones64, ones64_c), (triu, triu_c), (epj01, epj01_c),
                             (fpair, fpair_c), (cpool, consts_c)):
                inc(sync.dma_start(out=dst[:], in_=src.ap()), "ld", 16)
            inc(sync.dma_start(out=cw[:], in_=convw), "ld", 16)
            inc(sync.dma_start(out=cb[:], in_=convb), "ld", 16)
            inc(sync.dma_start(out=dtb[:], in_=dtbias), "ld", 16)
            inc(sync.dma_start(out=acol[:], in_=alog), "ld", 16)
            inc(sync.dma_start(out=dxp[:], in_=dexp), "ld", 16)
            inc(sync.dma_start(out=nwT[:], in_=normwT), "ld", 16)
            inc(sync.dma_start(out=mcol[:], in_=maskc), "ld", 16)
        block.sync(p0_sync)

        def p0_scal(s):
            wt(s, "ld")
            inc(s.activation(out=acol[:], in_=acol[:], func=AF.Exp), "ss", 1)
        block.scalar(p0_scal)

        def p0_vec(v):
            wt(v, "ss")
            v.tensor_scalar(out=acol[:], in0=acol[:], scalar1=-1.0, scalar2=None, op0=ALU.mult)
            v.memset(zeros128[:], 0.0)
            v.memset(s_run[:], 0.0)
            inc(v.memset(halo[:], 0.0), "vs", 1)
        block.vector(p0_vec)

        tap("xin0", xin[0:128, :], [128, D])

        def unpack_ops(s, bsrc, hdst, n):
            inc(s.activation(out=hdst[:, 0:n], in_=bsrc[:, 0:n], func=AF.Identity,
                             scale=cp(1.0 / 16.0), bias=cp(-0.46875)), "ss", 1)
            wt(s, "ss")
            inc(s.activation(out=hdst[:, 0:n], in_=hdst[:, 0:n], func=AF.Identity,
                             bias=cp(MAGIC)), "ss", 1)
            wt(s, "ss")
            return s.activation(out=hdst[:, 0:n], in_=hdst[:, 0:n], func=AF.Identity,
                                scale=cp(16.0), bias=cp(-16.0 * MAGIC))

        def dequant_ops(v, bsrc, hsrc, ldst, scb, sc16b, zsb, tmpb, whib, wlob, n, ng):
            w = n // ng
            sc_e = scb[:, 0:ng].unsqueeze(2).broadcast_to((128, ng, w))
            sc16_e = sc16b[:, 0:ng].unsqueeze(2).broadcast_to((128, ng, w))
            zs_e = zsb[:, 0:ng].unsqueeze(2).broadcast_to((128, ng, w))

            def g3(ap_, o=0):
                return ap_[:, o:o + n].rearrange("p (g r) -> p g r", g=ng)
            inc(v.tensor_sub(out=ldst[:, 0:n], in0=bsrc[:, 0:n], in1=hsrc[:, 0:n]),
                "vs", 1)
            wt(v, "vs")
            v.tensor_mul(out=g3(tmpb), in0=g3(hsrc), in1=sc16_e)
            inc(v.tensor_mul(out=g3(ldst), in0=g3(ldst), in1=sc_e), "vs", 1)
            wt(v, "vs")
            v.tensor_add(out=g3(whib), in0=g3(tmpb), in1=zs_e)
            return v.tensor_add(out=g3(wlob), in0=g3(ldst), in1=zs_e)

        def scale_prep(v, scb, sc16b, zsb, ng):
            v.scalar_tensor_tensor(out=zsb[:, 0:ng], in0=zsb[:, 0:ng], scalar=-1.0,
                                   in1=scb[:, 0:ng], op0=ALU.mult, op1=ALU.mult)
            v.tensor_scalar(out=sc16b[:, 0:ng], in0=scb[:, 0:ng], scalar1=1.0 / 16.0,
                            scalar2=None, op0=ALU.mult)

        for (qsrc, scsrc, zrsrc, scr) in (() if skip_phase_a else ((gq, gsc, gzr, g_scr), (uq, usc, uzr, u_scr))):
            for ib in range(86):
                r0 = ib * 128

                def ga(g, qsrc=qsrc, scsrc=scsrc, zrsrc=zrsrc, r0=r0):
                    inc(g.dma_start(out=qb[:], in_=qsrc[r0:r0 + 128, :]), "cast", 16)
                    inc(g.dma_start(out=qsc[:], in_=scsrc[r0:r0 + 128, :]), "cast", 16)
                    inc(g.dma_start(out=qzs[:], in_=zrsrc[r0:r0 + 128, :]), "cast", 16)
                on("gpsimd", ga, deps=("ss", "vs"))

                def sa(s):
                    inc(unpack_ops(s, qb, qh, 2048), "ss", 1)
                on("scalar", sa, deps=("cast", "vs"))

                def va(v):
                    scale_prep(v, qsc, qsc16, qzs, 64)
                    inc(dequant_ops(v, qb, qh, ql, qsc, qsc16, qzs, qtmp, qwhi, qwlo,
                                    2048, 64), "vs", 1)
                on("vector", va, deps=("cast", "ss", "ts"))

                def t_wave(t, wave):
                    for half in range(2):
                        pt = pT1 if half == 0 else pT2
                        for m in range(8):
                            sub = wave * 16 + half * 8 + m
                            src = qwhi if sub % 2 == 0 else qwlo
                            blk = sub // 2
                            mmi = t.transpose(pt[:, 128 * m:128 * m + 128],
                                              src[:, 128 * blk:128 * blk + 128], ident[:])
                    inc(mmi, "ts", 1)

                def v_wave(v, wave):
                    v.tensor_copy(out=qstg[:, 2048 * wave:2048 * wave + 1024], in_=pT1[:])
                    inc(v.tensor_copy(out=qstg[:, 2048 * wave + 1024:2048 * wave + 2048],
                                      in_=pT2[:]), "vs", 1)

                on("tensor", lambda t: t_wave(t, 0), deps=("vs",))
                on("vector", lambda v: v_wave(v, 0), deps=("ts", "st"))
                on("tensor", lambda t: t_wave(t, 1), deps=("vs",))
                on("vector", lambda v: v_wave(v, 1), deps=("ts",))

                def wr(sync, scr=scr, ib=ib):
                    inc(sync.dma_start(out=scr[ib], in_=qstg[:]), "st", 16)
                on("sync", wr, deps=("vs",))

        for dc in ([] if skip_phase_a else range(8)):
            for half2 in range(2):
                for dbi in range(2):
                    db = 2 * half2 + dbi
                    r0 = (4 * dc + db) * 128

                    def ga2(g, r0=r0):
                        inc(g.dma_start(out=qb2[:], in_=dq[r0:r0 + 128, :]), "cast", 16)
                        inc(g.dma_start(out=qsc2[:], in_=dsc[r0:r0 + 128, :]), "cast", 16)
                        inc(g.dma_start(out=qzs2[:], in_=dzr[r0:r0 + 128, :]), "cast", 16)
                    on("gpsimd", ga2, deps=("ss", "vs", "st"))

                    def sa2(s):
                        inc(unpack_ops(s, qb2, qh2, 5504), "ss", 1)
                    on("scalar", sa2, deps=("cast", "vs"))

                    def va2(v, dbi=dbi):
                        scale_prep(v, qsc2, qsc216, qzs2, 172)
                        whiv = qwhi2[:, 5504 * dbi:5504 * dbi + 5504]
                        wlov = qwlo2[:, 5504 * dbi:5504 * dbi + 5504]
                        inc(dequant_ops(v, qb2, qh2, ql2, qsc2, qsc216, qzs2, qtmp2,
                                        whiv, wlov, 5504, 172), "vs", 1)
                    on("vector", va2, deps=("cast", "ss", "ts"))

                for tsub in range(86):
                    m, hi = tsub // 2, (tsub % 2 == 0)

                    def t_tr(t, m=m, hi=hi):
                        for dbi in range(2):
                            base = 5504 * dbi
                            src = qwhi2 if hi else qwlo2
                            mmi = t.transpose(pT1[:, 128 * dbi:128 * dbi + 128],
                                              src[:, base + 128 * m:base + 128 * m + 128],
                                              ident[:])
                        inc(mmi, "ts", 1)
                    on("tensor", t_tr, deps=("vs",))

                    def v_tr(v):
                        inc(v.tensor_copy(out=qstg2[:, 0:256], in_=pT1[:, 0:256]), "vs", 1)
                    on("vector", v_tr, deps=("ts", "st"))

                    def wr2(sync, dc=dc, tsub=tsub, half2=half2):
                        inc(sync.dma_start(
                            out=d_scr[dc, tsub][:, 256 * half2:256 * half2 + 256],
                            in_=qstg2[:, 0:256]), "st", 16)
                    on("sync", wr2, deps=("vs",))

        for kk in ([] if skip_phase_a else range(64)):
            def gow(g, kk=kk):
                inc(g.dma_start(out=owstg[:], in_=outprojT[128 * kk:128 * kk + 128, :]),
                    "cast", 16)
            on("gpsimd", gow, deps=("st",))

            def sow(sync, kk=kk):
                inc(sync.dma_start(out=o_scr[kk], in_=owstg[:]), "st", 16)
            on("sync", sow, deps=("cast",))

        tap("gscr0", g_scr[0], [128, 4096])
        tap("uscr0", u_scr[0], [128, 4096])
        tap("dscr00", d_scr[0, 0], [128, 512])
        tap("oscr0", o_scr[0], [128, 4096])

        def ln_stage(g, from_x1):
            sig = from_x1
            for tt in range(4):
                def ldx(sync, tt=tt, g=g, from_x1=from_x1):
                    if from_x1:
                        inc(sync.dma_start(out=xt[:], in_=x1_scr[tt]), "ld", 16)
                    else:
                        r0 = TG * g + 128 * tt
                        inc(sync.dma_start(out=xt[:], in_=xin[r0:r0 + 128, :]), "ld", 16)
                on("sync", ldx, deps=("vs", "ss", "st"))

                def lns(s):
                    inc(s.activation(out=xlntm[:], in_=xt[:], func=AF.Square,
                                     accum_out=lnst[:, 0:1]), "ss", 1)
                on("scalar", lns, deps=("ld", "ts", "vs"))

                def lnv(v):
                    inc(v.reduce_sum(out=lnst[:, 1:2], in_=xt[:],
                                     axis=mybir.AxisListType.X), "vs", 1)
                    wt(v, "vs")
                    v.tensor_scalar(out=lnst[:, 2:3], in0=lnst[:, 1:2], scalar1=1.0 / D,
                                    scalar2=None, op0=ALU.mult)
                    inc(v.tensor_scalar(out=lnst[:, 3:4], in0=lnst[:, 0:1],
                                        scalar1=1.0 / D, scalar2=None, op0=ALU.mult),
                        "vs", 1)
                    wt(v, "vs")
                    inc(v.tensor_mul(out=lnst[:, 4:5], in0=lnst[:, 2:3],
                                     in1=lnst[:, 2:3]), "vs", 1)
                    wt(v, "vs")
                    inc(v.tensor_sub(out=lnst[:, 5:6], in0=lnst[:, 3:4], in1=lnst[:, 4:5]),
                        "vs", 1)
                on("vector", lnv, deps=("ld", "ss"))

                def lns2(s):
                    inc(s.activation(out=lnst[:, 6:7], in_=lnst[:, 5:6], func=AF.Ln,
                                     bias=cp(EPS)), "ss", 1)
                    wt(s, "ss")
                    inc(s.activation(out=lnst[:, 7:8], in_=lnst[:, 6:7], func=AF.Exp,
                                     scale=cp(-0.5)), "ss", 1)
                on("scalar", lns2, deps=("vs",))

                def lnv2(v):
                    inc(v.tensor_scalar(out=xlntm[:], in0=xt[:], scalar1=lnst[:, 2:3],
                                        scalar2=lnst[:, 7:8], op0=ALU.subtract,
                                        op1=ALU.mult), "vs", 1)
                on("vector", lnv2, deps=("ss",))

                for wave in range(2):
                    def t_tw(t, wave=wave, sig=sig):
                        for half in range(2):
                            pt = pT1 if half == 0 else pT2
                            for m in range(8):
                                ks = wave * 16 + half * 8 + m
                                if not sig:
                                    src = xlntm[:, 128 * ks:128 * ks + 128]
                                else:
                                    mm2, par = ks // 2, ks % 2
                                    src = xlntm[:, 256 * mm2 + par:256 * mm2 + par + 255:2]
                                mmi = t.transpose(pt[:, 128 * m:128 * m + 128], src, ident[:])
                        inc(mmi, "ts", 1)
                    on("tensor", t_tw, deps=("vs",))

                    def v_tw(v, tt=tt, wave=wave):
                        for half in range(2):
                            pt = pT1 if half == 0 else pT2
                            ks0 = wave * 16 + half * 8
                            dsts = xlnT[:, TG * ks0:TG * (ks0 + 8)].rearrange(
                                "p (k r) -> p k r", k=8)[:, :, 128 * tt:128 * tt + 128]
                            inc(v.tensor_copy(
                                out=dsts, in_=pt[:].rearrange("p (k r) -> p k r", k=8)),
                                "vs", 1)
                    on("vector", v_tw, deps=("ts",))

        def proj_stage(blocks):
            base_cast = {0: cnt["cast"], 1: cnt["cast2"]}
            base_ts = cnt["ts"]
            base_vs = cnt["vs"]
            base_ss = cnt["ss"]
            ev_mark = []
            vv = ss_ = 0
            for (off, kind, dest) in blocks:
                if kind == "silu":
                    ss_ += 1
                    ev_mark.append(("ss", ss_))
                else:
                    vv += 1
                    ev_mark.append(("vs", vv))

            def gp(gps):
                for i, (off, kind, dest) in enumerate(blocks):
                    if i >= 2:
                        gps.wait_ge(sem["ts"], base_ts + (i - 2) + 1)
                    src = inprojT[:, off:off + 128].rearrange("(ks p) c -> p ks c", p=128)
                    dst = wbuf[:, 4096 * (i % 2):4096 * (i % 2) + 4096].rearrange(
                        "p (ks c) -> p ks c", ks=32)
                    inc(gps.dma_start(out=dst, in_=src),
                        "cast" if i % 2 == 0 else "cast2", 16)
            on("gpsimd", gp, deps=("ts", "vs", "ss"))

            def tp(t):
                for i, (off, kind, dest) in enumerate(blocks):
                    cs_ = "cast" if i % 2 == 0 else "cast2"
                    t.wait_ge(sem[cs_], base_cast[i % 2] + 16 * (i // 2 + 1))
                    bank = PBANKS[i % 2]
                    if i >= 2:
                        s_, v_ = ev_mark[i - 2]
                        t.wait_ge(sem[s_], (base_ss if s_ == "ss" else base_vs) + v_)
                    w = wbuf[:, 4096 * (i % 2):4096 * (i % 2) + 4096]
                    for ks in range(32):
                        mmi = t.matmul(bank[:], w[:, 128 * ks:128 * ks + 128],
                                       xlnT[:, TG * ks:TG * ks + TG],
                                       start=(ks == 0), stop=(ks == 31))
                    inc(mmi, "ts", 1)
            onx("tensor", tp, ())

            def se(s):
                for i, (off, kind, dest) in enumerate(blocks):
                    if kind != "silu":
                        continue
                    s.wait_ge(sem["ts"], base_ts + i + 1)
                    inc(s.activation(out=dest, in_=PBANKS[i % 2][:], func=AF.Silu), "ss", 1)
            onx("scalar", se, ("ts",))

            def vep(v):
                for i, (off, kind, dest) in enumerate(blocks):
                    if kind == "silu":
                        continue
                    v.wait_ge(sem["ts"], base_ts + i + 1)
                    inc(v.tensor_copy(out=dest, in_=PBANKS[i % 2][:]), "vs", 1)
            onx("vector", vep, ("ss",))

        def conv_into(v, tile_idx, pre, par):
            rd = 240 * par + 3 * tile_idx
            sv = 240 * (1 - par) + 3 * tile_idx
            v.tensor_copy(out=pre[:, 0:3], in_=halo[:, rd:rd + 3])
            inc(v.tensor_copy(out=halo[:, sv:sv + 3], in_=pre[:, TG:TG + 3]), "vs", 1)
            wt(v, "vs")
            v.tensor_scalar(out=convo[:], in0=pre[:, 0:TG],
                            scalar1=cw[:, 4 * tile_idx:4 * tile_idx + 1],
                            scalar2=cb[:, tile_idx:tile_idx + 1], op0=ALU.mult, op1=ALU.add)
            inc(v.tensor_scalar(out=convo2[:], in0=pre[:, 1:1 + TG],
                                scalar1=cw[:, 4 * tile_idx + 1:4 * tile_idx + 2],
                                scalar2=None, op0=ALU.mult), "vs", 1)
            wt(v, "vs")
            v.scalar_tensor_tensor(out=convo[:], in0=pre[:, 2:2 + TG],
                                   scalar=cw[:, 4 * tile_idx + 2:4 * tile_idx + 3],
                                   in1=convo[:], op0=ALU.mult, op1=ALU.add)
            inc(v.scalar_tensor_tensor(out=convo2[:], in0=pre[:, 3:3 + TG],
                                       scalar=cw[:, 4 * tile_idx + 3:4 * tile_idx + 4],
                                       in1=convo2[:], op0=ALU.mult, op1=ALU.add), "vs", 1)
            wt(v, "vs")
            return v.tensor_add(out=convs[:], in0=convo[:], in1=convo2[:])

        _ck = [0]

        def ckpt():
            _ck[0] += 1
            if limit is not None and _ck[0] >= limit:
                raise _StopEmit()

        if only_ln:
            ln_stage(0, False)
            def fin0(sync):
                wt(sync, "st")
            on("sync", fin0)
            return nc, dbg
        def _emit_groups():
          for g in range(ngrp):
            full = g >= npref
            ln_stage(g, False)
            ckpt()
            if g == npref:
                tap("srun6", s_run[:], [128, 64 * 128])
                tap("xlnT6", xlnT[:], [128, 32 * TG])

            sec = [(B0 + 128 * gb, 64 + gb, bsil, gb) for gb in range(8)]
            if full:
                sec += [(C0 + 128 * gb, 72 + gb, csil, gb) for gb in range(8)]
            for (off, tile, dstbuf, gb) in sec:
                proj_stage([(off, "f32", bcpre[:, 3:3 + TG])])

                def cvb(v, tile=tile, g=g):
                    inc(conv_into(v, tile, bcpre, g % 2), "vs", 1)
                on("vector", cvb, deps=("vs", "ss"))

                def slb(s, dstbuf=dstbuf, gb=gb):
                    inc(s.activation(out=dstbuf[:, TG * gb:TG * gb + TG],
                                     in_=convs[:], func=AF.Silu), "ss", 1)
                on("scalar", slb, deps=("vs",))

            ckpt()
            proj_stage([(DT0, "f32", dtpre[:])])

            def dt_vec(v):
                inc(v.tensor_scalar(out=dtt[:], in0=dtpre[:], scalar1=dtb[:, 0:1],
                                    scalar2=None, op0=ALU.add), "vs", 1)
                wt(v, "vs")
                inc(v.tensor_relu(out=dtpre[:], in_=dtt[:]), "vs", 1)
            on("vector", dt_vec, deps=("vs", "ts"))

            def dt_scal(s):
                inc(s.activation(out=adt[:], in_=dtt[:], func=AF.Abs), "ss", 1)
                wt(s, "ss")
                inc(s.activation(out=adt[:], in_=adt[:], func=AF.Exp, scale=cp(-1.0)),
                    "ss", 1)
                wt(s, "ss")
                inc(s.activation(out=adt[:], in_=adt[:], func=AF.Ln, bias=cp(1.0)), "ss", 1)
            on("scalar", dt_scal, deps=("vs",))

            def dt_vec2(v, g=g):
                inc(v.tensor_add(out=dtt[:], in0=dtpre[:], in1=adt[:]), "vs", 1)
                wt(v, "vs")
                inc(v.tensor_scalar(out=dtt[:], in0=dtt[:], scalar1=100.0, scalar2=None,
                                    op0=ALU.min), "vs", 1)
                wt(v, "vs")
                for c in range(4):
                    slot = 4 * g + c
                    i_ = v.tensor_scalar(out=dtt[:, 128 * c:128 * c + 128],
                                         in0=dtt[:, 128 * c:128 * c + 128],
                                         scalar1=mcol[:, slot:slot + 1], scalar2=None,
                                         op0=ALU.mult)
                inc(i_, "vs", 1)
                wt(v, "vs")
                inc(v.tensor_scalar(out=adt[:], in0=dtt[:], scalar1=acol[:, 0:1],
                                    scalar2=None, op0=ALU.mult), "vs", 1)
                wt(v, "vs")
                for c in range(4):
                    cs = slice(128 * c, 128 * c + 128)
                    i_ = v.tensor_tensor_scan(out=acum[:, cs], data0=adt[:, cs],
                                              data1=zeros128[:], initial=0.0,
                                              op0=ALU.add, op1=ALU.add)
                inc(i_, "vs", 1)
                wt(v, "vs")
                for c in range(4):
                    i_ = v.tensor_copy(out=alast[:, c:c + 1],
                                       in_=acum[:, 128 * c + 127:128 * c + 128])
                inc(i_, "vs", 1)
                wt(v, "vs")
                for c in range(4):
                    cs = slice(128 * c, 128 * c + 128)
                    i_ = v.tensor_scalar(out=dscr[:, cs], in0=acum[:, cs],
                                         scalar1=alast[:, c:c + 1], scalar2=None,
                                         op0=ALU.subtract)
                inc(i_, "vs", 1)
            on("vector", dt_vec2, deps=("ss", "vs", "ld", "ts"))

            def dt_scal2(s):
                s.activation(out=dscr[:], in_=dscr[:], func=AF.Exp, scale=cp(-1.0))
                inc(s.activation(out=exal[:], in_=alast[:], func=AF.Exp), "ss", 1)
            on("scalar", dt_scal2, deps=("vs", "ts"))

            for c in range(4):
                def gv(v, c=c):
                    inc(v.tensor_scalar(out=gexp[:], in0=epj01[:], scalar1=exal[:, c:c + 1],
                                        scalar2=None, op0=ALU.mult), "vs", 1)
                on("vector", gv, deps=("ss", "ts"))

                def gt(t, c=c, full=full):
                    cs = slice(128 * c, 128 * c + 128)
                    t.matmul(pF[:, 0:64], fpair[:], gexp[:], start=True, stop=True)
                    t.transpose(pF[:, 128:256], dscr[:, cs], identf[:])
                    mmi = t.transpose(pF[:, 256:384], dtt[:, cs], identf[:])
                    if full:
                        mmi = t.transpose(pF[:, 384:512], acum[:, cs], identf[:])
                    inc(mmi, "ts", 1)
                on("tensor", gt, deps=("vs", "ss"))

                def gv2(v, c=c, full=full):
                    v.tensor_copy(out=expal[:, 64 * c:64 * c + 64], in_=pF[:, 0:64])
                    v.tensor_copy(out=d1T[:, 128 * c:128 * c + 128], in_=pF[:, 128:256])
                    i2 = v.tensor_copy(out=dtT[:, 128 * c:128 * c + 128], in_=pF[:, 256:384])
                    if full:
                        i2 = v.tensor_copy(out=acumT[:, 128 * c:128 * c + 128],
                                           in_=pF[:, 384:512])
                    inc(i2, "vs", 1)
                on("vector", gv2, deps=("ts",))

            def ddv(v):
                inc(v.tensor_mul(out=dd1T[:], in0=dtT[:], in1=d1T[:]), "vs", 1)
            on("vector", ddv, deps=("vs",))

            if full:
                def exa(s):
                    inc(s.activation(out=expacT[:], in_=acumT[:], func=AF.Exp), "ss", 1)
                on("scalar", exa, deps=("vs",))

            if g == npref:
                tap("alast6", alast[:], [128, 4])
                tap("exal6", exal[:], [128, 4])
                tap("adt6", adt[:], [128, TG])
                tap("dtpre6", dtpre[:], [128, TG])
                tap("dscrx6", dscr[:], [128, TG])
                tap("bsil6", bsil[:], [128, 8 * TG])
                tap("csil6", csil[:], [128, 8 * TG])
                tap("dtt6", dtt[:], [128, TG])
                tap("acum6", acum[:], [128, TG])
                tap("d1T6", d1T[:], [128, 4 * 128])
                tap("dtT6", dtT[:], [128, 4 * 128])
                tap("expal6", expal[:], [128, 4 * 64])
                tap("acumT6", acumT[:], [128, 4 * 128])
                tap("expacT6", expacT[:], [128, 4 * 128])

            for wave in range(2):
                def bt(t, wave=wave):
                    for half in range(2):
                        pt = pT1 if half == 0 else pT2
                        for m in range(8):
                            idx = wave * 16 + half * 8 + m
                            c, gb = idx // 8, idx % 8
                            mmi = t.transpose(
                                pt[:, 128 * m:128 * m + 128],
                                bsil[:, TG * gb + 128 * c:TG * gb + 128 * c + 128],
                                ident[:])
                    inc(mmi, "ts", 1)
                on("tensor", bt, deps=("vs", "ss"))

                def bv(v, wave=wave):
                    for half in range(2):
                        pt = pT1 if half == 0 else pT2
                        idx0 = wave * 16 + half * 8
                        dsts = bcm[:, 128 * idx0:128 * (idx0 + 8)].rearrange(
                            "p (k r) -> p k r", k=8)
                        inc(v.tensor_copy(out=dsts,
                                          in_=pt[:].rearrange("p (k r) -> p k r", k=8)),
                            "vs", 1)
                on("vector", bv, deps=("ts",))

            tap(f"sr_a{g}", s_run[:], [128, 64 * 128])
            tap(f"dd_{g}", dd1T[:], [128, 4 * 128])
            tap(f"ex_{g}", expal[:], [128, 4 * 64])
            tap(f"hs_{g}", hsall[:], [128, 8 * TG])

            ckpt()
            for hb in range(8):
                ckpt()
                for j8 in range(8):
                    jj = 8 * hb + j8
                    blocks = []
                    if full:
                        blocks.append((GATE0 + 128 * jj, "silu",
                                       gsall[:, TG * j8:TG * j8 + TG]))
                    blocks.append((HS0 + 128 * jj, "f32", hspre[:, 3:3 + TG]))
                    proj_stage(blocks)

                    def cvh(v, jj=jj, g=g):
                        inc(conv_into(v, jj, hspre, g % 2), "vs", 1)
                    on("vector", cvh, deps=("vs", "ss"))

                    def svh(s, j8=j8):
                        inc(s.activation(out=hsall[:, TG * j8:TG * j8 + TG],
                                         in_=convs[:], func=AF.Silu), "ss", 1)
                    on("scalar", svh, deps=("vs",))

                if g == npref and hb == 0:
                    tap("hsall6", hsall[:], [128, 8 * TG])
                    tap("gsall6", gsall[:], [128, 8 * TG])
                if full:
                    for c in range(4):
                        def g1tc(t, hb=hb, c=c):
                            cs = slice(TG * hb + 128 * c, TG * hb + 128 * c + 128)
                            inc(t.matmul(pE[:, 0:128], bsil[:, cs], csil[:, cs],
                                         start=True, stop=True), "ts", 1)
                        on("tensor", g1tc, deps=("vs", "ss"))

                        def g1vc(v, c=c):
                            inc(v.tensor_mul(out=g1m[:, 128 * c:128 * c + 128],
                                             in0=pE[:, 0:128], in1=triu[:]), "vs", 1)
                        on("vector", g1vc, deps=("ts",))
                    if full and g == npref and hb == 0:
                        tap("g1m0", g1m[:], [128, 512])

                for c in range(4):
                    if full:
                        def acg(sync, hb=hb, c=c):
                            cs = slice(128 * c, 128 * c + 128)
                            inc(sync.dma_start(out=acrow[0:1, :],
                                               in_=acum[16 * hb:16 * hb + 16, cs]),
                                "ld", 16)
                        on("sync", acg, deps=("vs", "ts"))

                    for j8 in range(8):
                        jj = 8 * hb + j8

                        def trh(t, j8=j8, full=full, c=c):
                            mmi = t.transpose(
                                pT2[:, 0:128],
                                hsall[:, TG * j8 + 128 * c:TG * j8 + 128 * c + 128],
                                ident[:])
                            if full:
                                mmi = t.transpose(
                                    pT2[:, 128:256],
                                    gsall[:, TG * j8 + 128 * c:TG * j8 + 128 * c + 128],
                                    ident[:])
                            inc(mmi, "ts", 1)
                        on("tensor", trh, deps=("vs", "ss"))

                        def hcm(v, jj=jj, j8=j8, c=c, full=full):
                            for hh in range(2):
                                h = 2 * jj + hh
                                band = slice(64 * hh, 64 * hh + 64)
                                i1 = v.tensor_scalar(
                                    out=xchp[:, band], in0=pT2[:, band],
                                    scalar1=dd1T[:, 128 * c + h:128 * c + h + 1],
                                    scalar2=None, op0=ALU.mult)
                            if full:
                                for hh in range(2):
                                    h = 2 * jj + hh
                                    band = slice(64 * hh, 64 * hh + 64)
                                    v.tensor_scalar(out=xch_cm[:, band], in0=pT2[:, band],
                                                    scalar1=dtT[:, 128 * c + h:128 * c + h + 1],
                                                    scalar2=None, op0=ALU.mult)
                                dg = gatm[:].rearrange("p (k r) -> p k r", k=4)[
                                    :, c:c + 1, 128 * j8:128 * j8 + 128]
                                v.tensor_copy(out=dg, in_=pT2[:, 128:256].unsqueeze(1))
                                i1 = v.tensor_scalar(
                                    out=hsdt[:],
                                    in0=hsall[:, TG * j8 + 128 * c:TG * j8 + 128 * c + 128],
                                    scalar1=dxp[:, jj:jj + 1], scalar2=None, op0=ALU.mult)
                            inc(i1, "vs", 1)
                        on("vector", hcm, deps=("ts",))
                        if g == npref and hb == 0 and c == 0 and j8 == 0:
                            tap("xchp0", xchp[:], [128, 128])
                            tap("xcm0", xch_cm[:], [128, 128])
                            tap("hsdt0", hsdt[:], [128, 128])

                        if full:
                            def sst(t, jj=jj, j8=j8, c=c, hb=hb):
                                t.transpose(pF[:, 0:128],
                                            s_run[:, 128 * jj:128 * jj + 128], identf[:])
                                for hh in range(2):
                                    lh = 2 * j8 + hh
                                    t.matmul(pF[:, 128 + 128 * hh:256 + 128 * hh], onesr[:],
                                             acrow[0:1, 128 * lh:128 * lh + 128],
                                             start=True, stop=True)
                                inc(t.matmul(pE[:, 0:128], xchp[:],
                                             bcm[:, 128 * (8 * c + hb):128 * (8 * c + hb) + 128],
                                             start=True, stop=True), "ts", 1)
                            on("tensor", sst, deps=("vs", "ss", "ld"))
                            if g == npref and hb == 0 and c == 0 and j8 == 0:
                                def dcp1(v):
                                    v.tensor_copy(out=ggf[:, 0:384], in_=pF[:, 0:384])
                                    inc(v.tensor_copy(out=ggf[:, 384:512],
                                                      in_=pE[:, 0:128]), "vs", 1)
                                on("vector", dcp1)
                                tap("pFE0", ggf[:, 0:512], [128, 512])
                                tap("acrow0", acrow[0:1, :], [1, 2048])

                            def ssa(s, jj=jj, c=c):
                                for hh in range(2):
                                    h = 2 * jj + hh
                                    ii = s.activation(
                                        out=relu_s[:, 128 * hh:128 * hh + 128],
                                        in_=pF[:, 128 + 128 * hh:256 + 128 * hh],
                                        func=AF.Relu, scale=cp(-1.0),
                                        bias=acumT[:, 128 * c + h:128 * c + h + 1])
                                inc(ii, "ss", 1)
                                wt(s, "ss")
                                for hh in range(2):
                                    ii = s.activation(out=relu_s[:, 128 * hh:128 * hh + 128],
                                                      in_=relu_s[:, 128 * hh:128 * hh + 128],
                                                      func=AF.Exp, scale=cp(-1.0))
                                inc(ii, "ss", 1)
                            on("scalar", ssa, deps=("ts",))
                            if g == npref and hb == 0 and c == 0 and j8 == 0:
                                tap("relu0", relu_s[:], [128, 256])

                            def ssv(v, jj=jj, c=c):
                                v.tensor_copy(out=prevT[:], in_=pF[:, 0:128])
                                for hh in range(2):
                                    h = 2 * jj + hh
                                    band = slice(64 * hh, 64 * hh + 64)
                                    v.tensor_scalar(
                                        out=expacx[:, band], in0=ones64[:],
                                        scalar1=expacT[:, 128 * c + h:128 * c + h + 1],
                                        scalar2=None, op0=ALU.mult)
                                for hh in range(2):
                                    ii = v.tensor_mul(out=mh[:, 128 * hh:128 * hh + 128],
                                                      in0=relu_s[:, 128 * hh:128 * hh + 128],
                                                      in1=g1m[:, 128 * c:128 * c + 128])
                                inc(ii, "vs", 1)
                            on("vector", ssv, deps=("ss",))
                            if g == npref and hb == 0 and c == 0 and j8 == 0:
                                tap("mh0", mh[:], [128, 256])
                                tap("prevT0", prevT[:], [128, 128])
                                tap("expacx0", expacx[:], [128, 128])

                            def sst2(t, jj=jj, c=c, hb=hb):
                                for hh in range(2):
                                    t.matmul(pC[:, 64 * hh:64 * hh + 64],
                                             mh[:, 128 * hh:128 * hh + 128],
                                             xch_cm[:, 64 * hh:64 * hh + 64],
                                             start=(hh == 0), stop=False)
                                t.matmul(pC[:, 0:128], hsdt[:], ident[:],
                                         start=False, stop=True)
                                inc(t.matmul(pD[:, 0:128],
                                             csil[:, TG * hb + 128 * c:TG * hb + 128 * c + 128],
                                             prevT[:], start=True, stop=True), "ts", 1)
                            on("tensor", sst2, deps=("vs",))
                            if g == npref and hb == 0 and c == 0 and j8 == 0:
                                def dcp2(v):
                                    v.tensor_copy(out=ggf[:, 0:128], in_=pC[:, 0:128])
                                    inc(v.tensor_copy(out=ggf[:, 128:256],
                                                      in_=pD[:, 0:128]), "vs", 1)
                                on("vector", dcp2)
                                tap("pCD0", ggf[:, 0:256], [128, 256])

                            def ssv2(v, jj=jj, c=c, j8=j8):
                                v.tensor_mul(out=yoffc[:], in0=pD[:, 0:128], in1=expacx[:])
                                inc(v.scalar_tensor_tensor(
                                    out=s_run[:, 128 * jj:128 * jj + 128],
                                    in0=s_run[:, 128 * jj:128 * jj + 128],
                                    scalar=expal[:, 64 * c + jj:64 * c + jj + 1],
                                    in1=pE[:, 0:128], op0=ALU.mult, op1=ALU.add), "vs", 1)
                                wt(v, "vs")
                                inc(v.tensor_add(
                                    out=ytm[:, 1024 * c + 128 * j8:1024 * c + 128 * j8 + 128],
                                    in0=pC[:, 0:128], in1=yoffc[:]), "vs", 1)
                            on("vector", ssv2, deps=("ts",))
                            if g == npref and hb == 0 and c == 0 and j8 == 0:
                                tap("yoffc0", yoffc[:], [128, 128])
                        else:
                            def sstp(t, c=c, hb=hb):
                                inc(t.matmul(pE[:, 0:128], xchp[:],
                                             bcm[:, 128 * (8 * c + hb):128 * (8 * c + hb) + 128],
                                             start=True, stop=True), "ts", 1)
                            on("tensor", sstp, deps=("vs",))

                            def ssvp(v, jj=jj, c=c):
                                inc(v.scalar_tensor_tensor(
                                    out=s_run[:, 128 * jj:128 * jj + 128],
                                    in0=s_run[:, 128 * jj:128 * jj + 128],
                                    scalar=expal[:, 64 * c + jj:64 * c + jj + 1],
                                    in1=pE[:, 0:128], op0=ALU.mult, op1=ALU.add), "vs", 1)
                            on("vector", ssvp, deps=("ts",))

                    if full:
                        def ggv(v, c=c):
                            inc(v.tensor_mul(out=ggf[:], in0=ytm[:, 1024 * c:1024 * c + 1024],
                                             in1=gatm[:, 1024 * c:1024 * c + 1024]), "vs", 1)
                        on("vector", ggv, deps=("vs", "ss"))

                        def ggs(s, c=c):
                            inc(s.activation(out=ytm[:, 1024 * c:1024 * c + 1024],
                                             in_=ggf[:], func=AF.Square,
                                             accum_out=rmsq[:, c:c + 1]), "ss", 1)
                            wt(s, "ss")
                            inc(s.activation(out=rmsq[:, 4 + c:5 + c], in_=rmsq[:, c:c + 1],
                                             func=AF.Ln, scale=cp(1.0 / 1024.0),
                                             bias=cp(EPS)), "ss", 1)
                            wt(s, "ss")
                            inc(s.activation(out=rmsq[:, 4 + c:5 + c],
                                             in_=rmsq[:, 4 + c:5 + c],
                                             func=AF.Exp, scale=cp(-0.5)), "ss", 1)
                        on("scalar", ggs, deps=("vs",))

                        def ggv2(v, c=c):
                            inc(v.tensor_scalar(out=ggb[:], in0=ggf[:],
                                                scalar1=rmsq[:, 4 + c:5 + c], scalar2=None,
                                                op0=ALU.mult), "vs", 1)
                        on("vector", ggv2, deps=("ss",))

                        def ggt(t):
                            for j8 in range(8):
                                mmi = t.transpose(pT1[:, 128 * j8:128 * j8 + 128],
                                                  ggb[:, 128 * j8:128 * j8 + 128], ident[:])
                            inc(mmi, "ts", 1)
                        on("tensor", ggt, deps=("vs",))

                        def ggv3(v, c=c, hb=hb):
                            for j8 in range(8):
                                jj = 8 * hb + j8
                                ii = v.tensor_scalar(
                                    out=ggtb[:, TG * j8 + 128 * c:TG * j8 + 128 * c + 128],
                                    in0=pT1[:, 128 * j8:128 * j8 + 128],
                                    scalar1=nwT[:, jj:jj + 1], scalar2=None, op0=ALU.mult)
                            inc(ii, "vs", 1)
                        on("vector", ggv3, deps=("ts", "st"))

                if g == npref and hb == 0:
                    tap("ytm6", ytm[:], [128, 4 * 1024])
                    tap("gatm6", gatm[:], [128, 4 * 1024])
                    tap("ggtb6", ggtb[:], [128, 8 * TG])
                if full:
                    def ggw(sync, hb=hb):
                        inc(sync.dma_start(
                            out=gg_scr[8 * hb:8 * hb + 8].rearrange("k p c -> p k c"),
                            in_=ggtb[:].rearrange("p (k c) -> p k c", k=8)), "st", 16)
                    on("sync", ggw, deps=("vs",))

            tap(f"sr_b{g}", s_run[:], [128, 64 * 128])
            ckpt()

            if full:
                for db in range(8):
                    base_ld = {0: cnt["ld"], 1: cnt["ld2"]}
                    base_ts = cnt["ts"]

                    def owl(sync, db=db, base_ts=base_ts):
                        for kk in range(64):
                            if kk >= 2:
                                sync.wait_ge(sem["ts"], base_ts + (kk - 2) + 1)
                            ls_ = "ld" if kk % 2 == 0 else "ld2"
                            inc(sync.dma_start(
                                out=obuf[:, 512 * (kk % 2):512 * (kk % 2) + 512],
                                in_=o_scr[kk][:, 512 * db:512 * db + 512]), ls_, 16)
                            inc(sync.dma_start(
                                out=ggl[:, 512 * (kk % 2):512 * (kk % 2) + 512],
                                in_=gg_scr[kk]), ls_, 16)
                    on("sync", owl, deps=("ts", "st"))

                    def owt(t, base_ld=base_ld):
                        for kk in range(64):
                            ls_ = "ld" if kk % 2 == 0 else "ld2"
                            t.wait_ge(sem[ls_], base_ld[kk % 2] + 32 * (kk // 2 + 1))
                            for tb in range(4):
                                mmi = t.matmul(
                                    PBANKS[tb][:],
                                    ggl[:, 512 * (kk % 2) + 128 * tb:512 * (kk % 2) + 128 * tb + 128],
                                    obuf[:, 512 * (kk % 2):512 * (kk % 2) + 512],
                                    start=(kk == 0), stop=(kk == 63))
                            inc(mmi, "ts", 1)
                    onx("tensor", owt, ("vs",))

                    for tb in range(4):
                        def exl(sync, g=g, db=db, tb=tb):
                            r0 = TG * g + 128 * tb
                            inc(sync.dma_start(out=xsl[:],
                                               in_=xin[r0:r0 + 128, 512 * db:512 * db + 512]),
                                "ld", 16)
                        on("sync", exl, deps=("ts",))

                        def exv(v, tb=tb):
                            inc(v.scalar_tensor_tensor(out=outt[:], in0=PBANKS[tb][:],
                                                       scalar=1.0, in1=xsl[:],
                                                       op0=ALU.mult, op1=ALU.add), "vs", 1)
                        on("vector", exv, deps=("ts", "ld", "st"))

                        def exw(sync, db=db, tb=tb):
                            inc(sync.dma_start(out=x1_scr[tb][:, 512 * db:512 * db + 512],
                                               in_=outt[:]), "st", 16)
                        on("sync", exw, deps=("vs",))

                if g == npref:
                    for tt in range(4):
                        tap(f"x1_{tt}", x1_scr[tt], [128, D])

                ln_stage(g, True)
                if g == npref:
                    tap("xln2T6", xlnT[:], [128, 32 * TG])

                for ib in range(86):
                    def fl(sync, ib=ib):
                        inc(sync.dma_start(out=fwbuf[:, 0:4096], in_=g_scr[ib]), "ld", 16)
                        inc(sync.dma_start(out=fwbuf[:, 4096:8192], in_=u_scr[ib]), "ld", 16)
                    on("sync", fl, deps=("ts", "vs"))

                    def ft(t):
                        for ks in range(32):
                            t.matmul(pA[:], fwbuf[:, 128 * ks:128 * ks + 128],
                                     xlnT[:, TG * ks:TG * ks + TG],
                                     start=(ks == 0), stop=(ks == 31))
                        for ks in range(32):
                            mmi = t.matmul(pB[:], fwbuf[:, 4096 + 128 * ks:4096 + 128 * ks + 128],
                                           xlnT[:, TG * ks:TG * ks + TG],
                                           start=(ks == 0), stop=(ks == 31))
                        inc(mmi, "ts", 1)
                    on("tensor", ft, deps=("ld",))

                    def fs(s):
                        inc(s.activation(out=fsil[:], in_=pA[:], func=AF.Silu), "ss", 1)
                    on("scalar", fs, deps=("ts",))

                    def fv(v, ib=ib):
                        inc(v.tensor_mul(out=hsw[:, TG * ib:TG * ib + TG], in0=pB[:],
                                         in1=fsil[:]), "vs", 1)
                    on("vector", fv, deps=("ss",))

                if g == npref:
                    tap("hsw6", hsw[:, 0:4 * TG], [128, 4 * TG])

                for dc in range(8):
                    for tb8 in range(11):
                        nt = 8 if tb8 < 10 else 6
                        t0_ = 8 * tb8

                        def fdl(sync, dc=dc, t0_=t0_, nt=nt):
                            inc(sync.dma_start(
                                out=fwbuf2[:, 0:512 * nt].rearrange("p (t c) -> p t c", t=nt),
                                in_=d_scr[dc, t0_:t0_ + nt].rearrange("t p c -> p t c")),
                                "ld", 16)
                        on("sync", fdl, deps=("ts",))

                        def fdt(t, t0_=t0_, nt=nt):
                            for ti in range(nt):
                                tsub = t0_ + ti
                                for tb in range(4):
                                    mmi = t.matmul(
                                        PBANKS[tb][:],
                                        hsw[:, TG * tsub + 128 * tb:TG * tsub + 128 * tb + 128],
                                        fwbuf2[:, 512 * ti:512 * ti + 512],
                                        start=(tsub == 0), stop=(tsub == 85))
                            inc(mmi, "ts", 1)
                        on("tensor", fdt, deps=("ld", "ss", "vs"))

                    for tb in range(4):
                        def fxl(sync, dc=dc, tb=tb):
                            inc(sync.dma_start(out=xsl[:],
                                               in_=x1_scr[tb][:, 512 * dc:512 * dc + 512]),
                                "ld", 16)
                        on("sync", fxl, deps=("ts",))

                        def fxv(v, tb=tb):
                            inc(v.scalar_tensor_tensor(out=outt[:], in0=PBANKS[tb][:],
                                                       scalar=1.0, in1=xsl[:],
                                                       op0=ALU.mult, op1=ALU.add), "vs", 1)
                        on("vector", fxv, deps=("ts", "ld", "st"))

                        def fxw(sync, g=g, dc=dc, tb=tb):
                            r0 = TG * (g - npref) + 128 * tb
                            inc(sync.dma_start(out=yout[r0:r0 + 128, 512 * dc:512 * dc + 512],
                                               in_=outt[:]), "st", 16)
                        on("sync", fxw, deps=("vs",))

        try:
            _emit_groups()
        except _StopEmit:
            pass
        def fin(sync):
            wt(sync, "st")
            wt(sync, "cast2")
        on("sync", fin)

    return nc, dbg



def host_prep(inputs):
    W = np.asarray(inputs["L0_attn_in_proj_w"])
    inprojT = np.concatenate([
        W[8192:16384].T, W[16384:17408].T, W[18432:18560].T,
        W[0:8192].T, W[17408:18432].T], axis=1)
    inprojT = np.ascontiguousarray(inprojT, dtype=np.float32)
    outprojT = np.ascontiguousarray(np.asarray(inputs["L0_attn_out_proj_w"]).T,
                                    dtype=np.float32)
    convw = np.ascontiguousarray(
        np.asarray(inputs["L0_attn_conv_w"])[:, 0, :].reshape(80, 128, 4)
        .transpose(1, 0, 2).reshape(128, 80 * 4), dtype=np.float32)
    convb = np.ascontiguousarray(
        np.asarray(inputs["L0_attn_conv_b"]).reshape(80, 128).T, dtype=np.float32)
    dtbias = np.ascontiguousarray(np.asarray(inputs["L0_attn_dt_bias"]).reshape(128, 1),
                                  dtype=np.float32)
    alogv = np.ascontiguousarray(np.asarray(inputs["L0_attn_A_log"]).reshape(128, 1),
                                 dtype=np.float32)
    Dv = np.asarray(inputs["L0_attn_D"]).astype(np.float32)
    dexp = np.empty((128, 64), dtype=np.float32)
    for j in range(64):
        dexp[0:64, j] = Dv[2 * j]
        dexp[64:128, j] = Dv[2 * j + 1]
    normw = np.asarray(inputs["L0_attn_norm_w"]).astype(np.float32)
    normwT = np.ascontiguousarray(normw.reshape(64, 128).T, dtype=np.float32)

    tp = tau_perm(FI)

    def u8(x):
        a = np.asarray(x)
        return np.ascontiguousarray(a.view(np.uint8).reshape(a.shape[0], a.shape[1], 4)[:, :, 0])

    gqv = np.ascontiguousarray(u8(inputs["L0_ffn_gate_Wq"])[tp])
    uqv = np.ascontiguousarray(u8(inputs["L0_ffn_up_Wq"])[tp])
    dqv = u8(inputs["L0_ffn_down_Wq"])
    gscv = np.ascontiguousarray(np.asarray(inputs["L0_ffn_gate_scale"], dtype=np.float32)[tp])
    gzrv = np.ascontiguousarray(np.asarray(inputs["L0_ffn_gate_zero"], dtype=np.float32)[tp])
    uscv = np.ascontiguousarray(np.asarray(inputs["L0_ffn_up_scale"], dtype=np.float32)[tp])
    uzrv = np.ascontiguousarray(np.asarray(inputs["L0_ffn_up_zero"], dtype=np.float32)[tp])
    dscv = np.ascontiguousarray(np.asarray(inputs["L0_ffn_down_scale"], dtype=np.float32))
    dzrv = np.ascontiguousarray(np.asarray(inputs["L0_ffn_down_zero"], dtype=np.float32))

    shared = dict(inprojT=inprojT, outprojT=outprojT, convw=convw, convb=convb,
                  dtbias=dtbias, alog=alogv, dexp=dexp, normwT=normwT,
                  gq=gqv, uq=uqv, dq=dqv, gsc=gscv, gzr=gzrv, usc=uscv, uzr=uzrv,
                  dsc=dscv, dzr=dzrv)

    x = np.asarray(inputs["L0_attn_x"], dtype=np.float32)
    own_len = (NGRP - NPREF) * TG
    in_maps = []
    for core in range(8):
        b, q = core // 4, core % 4
        xb = x[b]
        npref_real = min((own_len // 128) * q, NPREF * 4)
        njunk = NPREF * 4 - npref_real
        parts = []
        if njunk > 0:
            parts.append(np.zeros((njunk * 128, D), dtype=np.float32))
        if npref_real > 0:
            parts.append(xb[0:npref_real * 128])
        parts.append(xb[own_len * q:own_len * (q + 1)])
        xinv = np.ascontiguousarray(np.concatenate(parts, axis=0))
        mask = np.zeros(NGRP * 4, dtype=np.float32)
        mask[njunk:] = 1.0
        maskcv = np.ascontiguousarray(np.tile(mask[None, :], (128, 1)).astype(np.float32))
        m = dict(shared)
        m["xin"] = xinv
        m["maskc"] = maskcv
        in_maps.append(m)
    return in_maps


def kernel(**inputs):
    in_maps = host_prep(inputs)
    nc, _ = build()
    res = run_bass_kernel_spmd(nc, in_maps, core_ids=list(range(8)))
    out = np.empty((2, 4096, 4096), dtype=np.float32)
    for core in range(8):
        b, q = core // 4, core % 4
        out[b, 1024 * q:1024 * (q + 1)] = res.results[core]["yout"]
    return out


# version 04
# speedup vs baseline: 1.1548x; 1.0590x over previous
import numpy as np
import ml_dtypes
import concourse.bass as bass
import concourse.mybir as mybir
from concourse.bass_utils import run_bass_kernel_spmd

F32 = mybir.dt.float32
BF16 = mybir.dt.bfloat16
U8 = mybir.dt.uint8
AF = mybir.ActivationFunctionType
ALU = mybir.AluOpType

D = 4096
INTER = 8192
PROJ = 18560
FI = 11008
TG = 512
NGRP = 8
NPREF = 6
EPS = 1e-5
MAGIC = 8388608.0 + 4194304.0

HS0, B0, DT0, GATE0, C0 = 0, 8192, 9216, 9344, 17536

BF16NP = ml_dtypes.bfloat16


def tau_perm(n):
    out = np.empty(n, dtype=np.int64)
    ar = 2 * np.arange(128)
    for m in range(n // 256):
        out[m * 256:m * 256 + 128] = m * 256 + ar
        out[m * 256 + 128:m * 256 + 256] = m * 256 + ar + 1
    return out


class _StopEmit(Exception):
    pass


def build(debug=(), skip_phase_a=False, only_ln=False, limit=None):
    ngrp, npref = NGRP, NPREF
    nfull = ngrp - npref
    nc = bass.Bass("TRN2", target_bir_lowering=False)

    xin = nc.dram_tensor("xin", [ngrp * TG, D], F32, kind="ExternalInput").ap()
    maskc = nc.dram_tensor("maskc", [128, ngrp * 4], F32, kind="ExternalInput").ap()
    inprojT = nc.dram_tensor("inprojT", [D, PROJ], F32, kind="ExternalInput").ap()
    outprojT = nc.dram_tensor("outprojT", [INTER, D], F32, kind="ExternalInput").ap()
    convw = nc.dram_tensor("convw", [128, 80 * 4], F32, kind="ExternalInput").ap()
    convb = nc.dram_tensor("convb", [128, 80], F32, kind="ExternalInput").ap()
    dtbias = nc.dram_tensor("dtbias", [128, 1], F32, kind="ExternalInput").ap()
    alog = nc.dram_tensor("alog", [128, 1], F32, kind="ExternalInput").ap()
    dexp = nc.dram_tensor("dexp", [128, 64], F32, kind="ExternalInput").ap()
    normwT = nc.dram_tensor("normwT", [128, 64], F32, kind="ExternalInput").ap()
    gq = nc.dram_tensor("gq", [FI, D // 2], U8, kind="ExternalInput").ap()
    uq = nc.dram_tensor("uq", [FI, D // 2], U8, kind="ExternalInput").ap()
    dq = nc.dram_tensor("dq", [D, FI // 2], U8, kind="ExternalInput").ap()
    gsc = nc.dram_tensor("gsc", [FI, 64], F32, kind="ExternalInput").ap()
    gzr = nc.dram_tensor("gzr", [FI, 64], F32, kind="ExternalInput").ap()
    usc = nc.dram_tensor("usc", [FI, 64], F32, kind="ExternalInput").ap()
    uzr = nc.dram_tensor("uzr", [FI, 64], F32, kind="ExternalInput").ap()
    dsc = nc.dram_tensor("dsc", [D, 172], F32, kind="ExternalInput").ap()
    dzr = nc.dram_tensor("dzr", [D, 172], F32, kind="ExternalInput").ap()

    yout = nc.dram_tensor("yout", [nfull * TG, D], F32, kind="ExternalOutput").ap()

    dbg = {}

    def dbgt(name, shape):
        if name in debug and name not in dbg:
            dbg[name] = nc.dram_tensor("dbg_" + name, shape, F32, kind="ExternalOutput").ap()
        return dbg.get(name)

    g_scr = nc.dram_tensor("g_scr", [86, 128, 4096], BF16, kind="Internal").ap()
    u_scr = nc.dram_tensor("u_scr", [86, 128, 4096], BF16, kind="Internal").ap()
    d_scr = nc.dram_tensor("d_scr", [8, 86, 128, 512], BF16, kind="Internal").ap()
    o_scr = nc.dram_tensor("o_scr", [64, 128, 4096], BF16, kind="Internal").ap()
    gg_scr = nc.dram_tensor("gg_scr", [64, 128, TG], BF16, kind="Internal").ap()
    x1_scr = nc.dram_tensor("x1_scr", [4, 128, D], F32, kind="Internal").ap()
    ip2 = nc.dram_tensor("ip_scr", [PROJ // 128, 128, 4096], BF16, kind="Internal").ap()

    ident_c = nc.inline_tensor(np.eye(128, dtype=BF16NP), "identc")
    identf_c = nc.inline_tensor(np.eye(128, dtype=np.float32), "identfc")
    ones_row_c = nc.inline_tensor(np.ones((1, 128), dtype=np.float32), "onesrowc")
    ones64_c = nc.inline_tensor(np.ones((128, 64), dtype=np.float32), "ones64c")
    triu_c = nc.inline_tensor(
        np.triu(np.ones((128, 128), dtype=np.float32)).astype(BF16NP), "triuc")
    epj = np.zeros((128, 64), dtype=BF16NP)
    for h in range(128):
        epj[h, h // 2] = 1
    epj01_c = nc.inline_tensor(epj, "epj01c")
    fp_ = np.zeros((128, 128), dtype=BF16NP)
    for h in range(128):
        fp_[h, (h % 2) * 64:(h % 2) * 64 + 64] = 1
    fpair_c = nc.inline_tensor(fp_, "fpairc")

    CVALS = [-1.0, -0.5, 1.0 / 1024.0, 1.0 / 16.0, 16.0, MAGIC, -16.0 * MAGIC,
             -0.46875, EPS, 1.0]
    CIDX = {v: i for i, v in enumerate(CVALS)}
    consts_c = nc.inline_tensor(
        np.tile(np.array(CVALS, dtype=np.float32)[None, :], (128, 1)), "constsc")

    A = nc.alloc_sbuf_tensor
    ident = A("ident", [128, 128], BF16)
    identf = A("identf", [128, 128], F32)
    onesr = A("onesr", [1, 128], F32)
    ones64 = A("ones64", [128, 64], F32)
    triu = A("triu", [128, 128], BF16)
    epj01 = A("epj01", [128, 64], BF16)
    fpair = A("fpair", [128, 128], BF16)
    cpool = A("cpool", [128, len(CVALS)], F32)
    cw = A("cw", [128, 80 * 4], F32)
    cb = A("cb", [128, 80], F32)
    dtb = A("dtb", [128, 1], F32)
    acol = A("acol", [128, 1], F32)
    dxp = A("dxp", [128, 64], F32)
    nwT = A("nwT", [128, 64], F32)
    mcol = A("mcol", [128, ngrp * 4], F32)
    zeros128 = A("zeros128", [128, 128], F32)
    lnst = A("lnst", [128, 8], F32)
    rmsq = A("rmsq", [128, 8], F32)
    s_run = A("s_run", [128, 64 * 128], F32)
    halo = A("halo", [128, 2 * 80 * 3], F32)
    xlnT = A("xlnT", [128, 32 * TG], BF16)

    def cp(v):
        return cpool[:, CIDX[v]:CIDX[v] + 1]

    ARENA = 136 * 1024
    arena = A("arena", [128, ARENA], U8)

    class Rg:
        def __init__(self):
            self.off = 0

        def al(self, nbytes, align=64):
            o = (self.off + align - 1) // align * align
            self.off = o + nbytes
            assert self.off <= ARENA, f"arena overflow {self.off}"
            return o

    def view(off, nelem, dtype, parts=128):
        esz = {F32: 4, BF16: 2, U8: 1}[dtype]
        return arena[0:parts, off:off + nelem * esz].bitcast(dtype)

    ra = Rg()
    qb = view(ra.al(2048 * 4), 2048, F32)
    qh = view(ra.al(2048 * 4), 2048, F32)
    ql = view(ra.al(2048 * 4), 2048, F32)
    qtmp = qb
    qwhi = view(ra.al(2048 * 2), 2048, BF16)
    qwlo = view(ra.al(2048 * 2), 2048, BF16)
    qstg = view(ra.al(4096 * 2), 4096, BF16)
    qsc = view(ra.al(64 * 4), 64, F32)
    qsc16 = view(ra.al(64 * 4), 64, F32)
    qzs = view(ra.al(64 * 4), 64, F32)

    ra2 = Rg()
    qb2 = view(ra2.al(5504 * 4), 5504, F32)
    qh2 = view(ra2.al(5504 * 4), 5504, F32)
    ql2 = view(ra2.al(5504 * 4), 5504, F32)
    qtmp2 = qb2
    qwhi2 = view(ra2.al(2 * 5504 * 2), 2 * 5504, BF16)
    qwlo2 = view(ra2.al(2 * 5504 * 2), 2 * 5504, BF16)
    qstg2 = view(ra2.al(512 * 2), 512, BF16)
    qsc2 = view(ra2.al(172 * 4), 172, F32)
    qsc216 = view(ra2.al(172 * 4), 172, F32)
    qzs2 = view(ra2.al(172 * 4), 172, F32)
    owstg = view(ra2.al(4096 * 2), 4096, BF16)

    rm = Rg()
    wbuf = view(rm.al(2 * 4096 * 2), 2 * 4096, BF16)
    bcm = view(rm.al(32 * 128 * 2), 32 * 128, BF16)
    hsall = view(rm.al(8 * TG * 2), 8 * TG, BF16)
    gsall = view(rm.al(8 * TG * 2), 8 * TG, BF16)
    bsil = view(rm.al(8 * TG * 2), 8 * TG, BF16)
    csil = view(rm.al(8 * TG * 2), 8 * TG, BF16)
    bcpre = view(rm.al((3 + TG) * 4), 3 + TG, F32)
    ggtb = view(rm.al(8 * TG * 2), 8 * TG, BF16)
    dtpre = view(rm.al(TG * 4), TG, F32)
    dtt = view(rm.al(TG * 4), TG, F32)
    adt = view(rm.al(TG * 4), TG, F32)
    acum = view(rm.al(TG * 4), TG, F32)
    acumT = view(rm.al(4 * 128 * 4), 4 * 128, F32)
    d1T = view(rm.al(4 * 128 * 4), 4 * 128, F32)
    dtT = view(rm.al(4 * 128 * 4), 4 * 128, F32)
    dd1T = view(rm.al(4 * 128 * 4), 4 * 128, F32)
    dscr = view(rm.al(TG * 4), TG, F32)
    expacT = view(rm.al(4 * 128 * 4), 4 * 128, F32)
    alast = view(rm.al(4 * 4), 4, F32)
    exal = view(rm.al(4 * 4), 4, F32)
    gexp = view(rm.al(64 * 2), 64, BF16)
    expal = view(rm.al(4 * 64 * 4), 4 * 64, F32)
    hspre = view(rm.al((3 + TG) * 4), 3 + TG, F32)
    convo = view(rm.al(TG * 4), TG, F32)
    convo2 = view(rm.al(TG * 4), TG, F32)
    convs = view(rm.al(TG * 4), TG, F32)
    xch_cm = view(rm.al(128 * 2), 128, BF16)
    xchp = view(rm.al(128 * 2), 128, BF16)
    hsdt = view(rm.al(128 * 2), 128, BF16)
    g1m = view(rm.al(4 * 128 * 2), 4 * 128, BF16)
    mh = view(rm.al(2 * 128 * 2), 2 * 128, BF16)
    relu_s = view(rm.al(2 * 128 * 4), 2 * 128, F32)
    prevT = view(rm.al(128 * 2), 128, BF16)
    yoffc = view(rm.al(128 * 4), 128, F32)
    expacx = view(rm.al(128 * 2), 128, BF16)
    ggf = view(rm.al(1024 * 2), 1024, BF16)
    ggb = view(rm.al(1024 * 2), 1024, BF16)
    acrow = view(rm.al(2048 * 4), 2048, F32, parts=1)
    sub_o = rm.al(max(D * 4 + D * 2, 2 * 4 * 1024 * 2))
    xt = view(sub_o, D, F32)
    xlntm = view(sub_o + D * 4, D, BF16)
    ytm = view(sub_o, 4 * 1024, BF16)
    gatm = view(sub_o + 4 * 1024 * 2, 4 * 1024, BF16)
    mix_end = rm.off

    rf = Rg()
    hsw = view(rf.al(86 * TG * 2), 86 * TG, BF16)
    fwbuf = view(rf.al(2 * 4096 * 2), 2 * 4096, BF16)
    fwbuf2 = view(rf.al(8 * 512 * 2), 8 * 512, BF16)
    fsil = view(rf.al(TG * 2), TG, BF16)
    xsl = view(rf.al(512 * 4), 512, F32)
    outt = view(rf.al(512 * 4), 512, F32)
    obuf = view(rf.al(2 * 512 * 2), 2 * 512, BF16)
    ggl = view(rf.al(2 * 512 * 2), 2 * 512, BF16)
    ffn_end = rf.off

    PS = nc.alloc_psum_tensor
    pA = PS("pA", [128, 512], F32)
    pB = PS("pB", [128, 512], F32)
    pC = PS("pC", [128, 512], F32)
    pD = PS("pD", [128, 512], F32)
    pE = PS("pE", [128, 512], F32)
    pF = PS("pF", [128, 512], F32)
    pT1 = PS("pT1", [128, 1024], BF16)
    pT2 = PS("pT2", [128, 1024], BF16)
    PBANKS = [pA, pB, pC, pD]

    with nc.Block() as block:
        sem = {}
        cnt = {}
        for s in ("ld", "ld2", "cast", "cast2", "ts", "vs", "ss", "st",
                  "ldp", "ldp2", "ipc0", "ipc1", "ipc2", "ipc3"):
            sem[s] = nc.alloc_semaphore(s)
            cnt[s] = 0

        def inc(instr, s, n=1):
            instr.then_inc(sem[s], n)
            cnt[s] += n
            return instr

        def wt(eng, s, val=None):
            v = cnt[s] if val is None else val
            if v > 0:
                eng.wait_ge(sem[s], v)

        ALLSEMS = ("ld", "ld2", "cast", "cast2", "ts", "vs", "ss", "st",
                   "ldp", "ldp2", "ipc0", "ipc1", "ipc2", "ipc3")

        def on(eng_attr, fn, deps=None):
            if deps is not None and debug:
                deps = tuple(deps) + ("cast2", "st")

            def wrapped(eng):
                for d in (ALLSEMS if deps is None else deps):
                    wt(eng, d)
                fn(eng)
            getattr(block, eng_attr)(wrapped)

        def onx(eng_attr, fn, deps):
            if debug:
                deps = tuple(deps) + ("cast2", "st")

            def wrapped(eng):
                for d in deps:
                    wt(eng, d)
                fn(eng)
            getattr(block, eng_attr)(wrapped)

        def tap(name, src_ap, shape):
            t_ = dbgt(name, shape)
            if t_ is None:
                return

            def tp_(gp_, t_=t_, src_ap=src_ap):
                inc(gp_.dma_start(out=t_, in_=src_ap), "cast2", 16)
            on("gpsimd", tp_, deps=("vs", "ss", "ts", "ld", "st"))

        def tap_s(name, src_ap, shape):
            t_ = dbgt(name, shape)
            if t_ is None:
                return

            def tp_(sy_, t_=t_, src_ap=src_ap):
                inc(sy_.dma_start(out=t_, in_=src_ap), "st", 16)
            on("sync", tp_, deps=("vs", "ss", "ts", "ld", "st"))

        def p0_sync(sync):
            for dst, src in ((ident, ident_c), (identf, identf_c), (onesr, ones_row_c),
                             (ones64, ones64_c), (triu, triu_c), (epj01, epj01_c),
                             (fpair, fpair_c), (cpool, consts_c)):
                inc(sync.dma_start(out=dst[:], in_=src.ap()), "ld", 16)
            inc(sync.dma_start(out=cw[:], in_=convw), "ld", 16)
            inc(sync.dma_start(out=cb[:], in_=convb), "ld", 16)
            inc(sync.dma_start(out=dtb[:], in_=dtbias), "ld", 16)
            inc(sync.dma_start(out=acol[:], in_=alog), "ld", 16)
            inc(sync.dma_start(out=dxp[:], in_=dexp), "ld", 16)
            inc(sync.dma_start(out=nwT[:], in_=normwT), "ld", 16)
            inc(sync.dma_start(out=mcol[:], in_=maskc), "ld", 16)
        block.sync(p0_sync)

        def p0_scal(s):
            wt(s, "ld")
            inc(s.activation(out=acol[:], in_=acol[:], func=AF.Exp), "ss", 1)
        block.scalar(p0_scal)

        def p0_vec(v):
            wt(v, "ss")
            v.tensor_scalar(out=acol[:], in0=acol[:], scalar1=-1.0, scalar2=None, op0=ALU.mult)
            v.memset(zeros128[:], 0.0)
            v.memset(s_run[:], 0.0)
            inc(v.memset(halo[:], 0.0), "vs", 1)
        block.vector(p0_vec)

        tap("xin0", xin[0:128, :], [128, D])

        CONV_ORDER = (list(range(64, 73)) + list(range(64)) +
                      list(range(137, 145)) + list(range(73, 137)))
        conv_pos = {t: k for k, t in enumerate(CONV_ORDER)}

        def conv_waits(eng, t):
            k = conv_pos[t]
            for j in range(4):
                need = (k - j) // 4 + 1 if k >= j else 0
                if need:
                    eng.wait_ge(sem[f"ipc{j}"], 16 * need)

        def ipconv(gp_):
            for k, t in enumerate(CONV_ORDER):
                if k >= 4:
                    gp_.wait_ge(sem[f"ipc{k % 4}"], 16 * (k // 4))
                src_ = inprojT[:, 128 * t:128 * t + 128].rearrange(
                    "(ks p) c -> p ks c", p=128)
                dst_ = ip2[t].rearrange("p (ks c) -> p ks c", ks=32)
                inc(gp_.dma_start(out=dst_, in_=src_), f"ipc{k % 4}", 16)
        on("gpsimd", ipconv, deps=())

        def unpack_ops(s, bsrc, hdst, n):
            inc(s.activation(out=hdst[:, 0:n], in_=bsrc[:, 0:n], func=AF.Identity,
                             scale=cp(1.0 / 16.0), bias=cp(-0.46875)), "ss", 1)
            wt(s, "ss")
            inc(s.activation(out=hdst[:, 0:n], in_=hdst[:, 0:n], func=AF.Identity,
                             bias=cp(MAGIC)), "ss", 1)
            wt(s, "ss")
            return s.activation(out=hdst[:, 0:n], in_=hdst[:, 0:n], func=AF.Identity,
                                scale=cp(16.0), bias=cp(-16.0 * MAGIC))

        def dequant_ops(v, bsrc, hsrc, ldst, scb, sc16b, zsb, tmpb, whib, wlob, n, ng):
            w = n // ng
            sc_e = scb[:, 0:ng].unsqueeze(2).broadcast_to((128, ng, w))
            sc16_e = sc16b[:, 0:ng].unsqueeze(2).broadcast_to((128, ng, w))
            zs_e = zsb[:, 0:ng].unsqueeze(2).broadcast_to((128, ng, w))

            def g3(ap_, o=0):
                return ap_[:, o:o + n].rearrange("p (g r) -> p g r", g=ng)
            inc(v.tensor_sub(out=ldst[:, 0:n], in0=bsrc[:, 0:n], in1=hsrc[:, 0:n]),
                "vs", 1)
            wt(v, "vs")
            v.tensor_mul(out=g3(tmpb), in0=g3(hsrc), in1=sc16_e)
            inc(v.tensor_mul(out=g3(ldst), in0=g3(ldst), in1=sc_e), "vs", 1)
            wt(v, "vs")
            v.tensor_add(out=g3(whib), in0=g3(tmpb), in1=zs_e)
            return v.tensor_add(out=g3(wlob), in0=g3(ldst), in1=zs_e)

        def scale_prep(v, scb, sc16b, zsb, ng):
            v.scalar_tensor_tensor(out=zsb[:, 0:ng], in0=zsb[:, 0:ng], scalar=-1.0,
                                   in1=scb[:, 0:ng], op0=ALU.mult, op1=ALU.mult)
            v.tensor_scalar(out=sc16b[:, 0:ng], in0=scb[:, 0:ng], scalar1=1.0 / 16.0,
                            scalar2=None, op0=ALU.mult)

        for (qsrc, scsrc, zrsrc, scr) in (() if skip_phase_a else ((gq, gsc, gzr, g_scr), (uq, usc, uzr, u_scr))):
          with nc.named_scope("phA_gu"):
            for ib in range(86):
                r0 = ib * 128

                def ga(g, qsrc=qsrc, scsrc=scsrc, zrsrc=zrsrc, r0=r0):
                    inc(g.dma_start(out=qb[:], in_=qsrc[r0:r0 + 128, :]), "cast", 16)
                    inc(g.dma_start(out=qsc[:], in_=scsrc[r0:r0 + 128, :]), "cast", 16)
                    inc(g.dma_start(out=qzs[:], in_=zrsrc[r0:r0 + 128, :]), "cast", 16)
                on("gpsimd", ga, deps=("ss", "vs"))

                def sa(s):
                    inc(unpack_ops(s, qb, qh, 2048), "ss", 1)
                on("scalar", sa, deps=("cast", "vs"))

                def va(v):
                    scale_prep(v, qsc, qsc16, qzs, 64)
                    inc(dequant_ops(v, qb, qh, ql, qsc, qsc16, qzs, qtmp, qwhi, qwlo,
                                    2048, 64), "vs", 1)
                on("vector", va, deps=("cast", "ss", "ts"))

                def t_wave(t, wave):
                    for half in range(2):
                        pt = pT1 if half == 0 else pT2
                        for m in range(8):
                            sub = wave * 16 + half * 8 + m
                            src = qwhi if sub % 2 == 0 else qwlo
                            blk = sub // 2
                            mmi = t.transpose(pt[:, 128 * m:128 * m + 128],
                                              src[:, 128 * blk:128 * blk + 128], ident[:])
                    inc(mmi, "ts", 1)

                def v_wave(v, wave):
                    v.tensor_copy(out=qstg[:, 2048 * wave:2048 * wave + 1024], in_=pT1[:])
                    inc(v.tensor_copy(out=qstg[:, 2048 * wave + 1024:2048 * wave + 2048],
                                      in_=pT2[:]), "vs", 1)

                on("tensor", lambda t: t_wave(t, 0), deps=("vs",))
                on("vector", lambda v: v_wave(v, 0), deps=("ts", "st"))
                on("tensor", lambda t: t_wave(t, 1), deps=("vs",))
                on("vector", lambda v: v_wave(v, 1), deps=("ts",))

                def wr(sync, scr=scr, ib=ib):
                    inc(sync.dma_start(out=scr[ib], in_=qstg[:]), "st", 16)
                on("sync", wr, deps=("vs",))

        for dc in ([] if skip_phase_a else range(8)):
          with nc.named_scope("phA_dn"):
            for half2 in range(2):
                for dbi in range(2):
                    db = 2 * half2 + dbi
                    r0 = (4 * dc + db) * 128

                    def ga2(g, r0=r0):
                        inc(g.dma_start(out=qb2[:], in_=dq[r0:r0 + 128, :]), "cast", 16)
                        inc(g.dma_start(out=qsc2[:], in_=dsc[r0:r0 + 128, :]), "cast", 16)
                        inc(g.dma_start(out=qzs2[:], in_=dzr[r0:r0 + 128, :]), "cast", 16)
                    on("gpsimd", ga2, deps=("ss", "vs", "st"))

                    def sa2(s):
                        inc(unpack_ops(s, qb2, qh2, 5504), "ss", 1)
                    on("scalar", sa2, deps=("cast", "vs"))

                    def va2(v, dbi=dbi):
                        scale_prep(v, qsc2, qsc216, qzs2, 172)
                        whiv = qwhi2[:, 5504 * dbi:5504 * dbi + 5504]
                        wlov = qwlo2[:, 5504 * dbi:5504 * dbi + 5504]
                        inc(dequant_ops(v, qb2, qh2, ql2, qsc2, qsc216, qzs2, qtmp2,
                                        whiv, wlov, 5504, 172), "vs", 1)
                    on("vector", va2, deps=("cast", "ss", "ts"))

                for tsub in range(86):
                    m, hi = tsub // 2, (tsub % 2 == 0)

                    def t_tr(t, m=m, hi=hi):
                        for dbi in range(2):
                            base = 5504 * dbi
                            src = qwhi2 if hi else qwlo2
                            mmi = t.transpose(pT1[:, 128 * dbi:128 * dbi + 128],
                                              src[:, base + 128 * m:base + 128 * m + 128],
                                              ident[:])
                        inc(mmi, "ts", 1)
                    on("tensor", t_tr, deps=("vs",))

                    def v_tr(v):
                        inc(v.tensor_copy(out=qstg2[:, 0:256], in_=pT1[:, 0:256]), "vs", 1)
                    on("vector", v_tr, deps=("ts", "st"))

                    def wr2(sync, dc=dc, tsub=tsub, half2=half2):
                        inc(sync.dma_start(
                            out=d_scr[dc, tsub][:, 256 * half2:256 * half2 + 256],
                            in_=qstg2[:, 0:256]), "st", 16)
                    on("sync", wr2, deps=("vs",))

        for kk in ([] if skip_phase_a else range(64)):
          with nc.named_scope("phA_ow"):
            def gow(g, kk=kk):
                inc(g.dma_start(out=owstg[:], in_=outprojT[128 * kk:128 * kk + 128, :]),
                    "cast", 16)
            on("gpsimd", gow, deps=("st",))

            def sow(sync, kk=kk):
                inc(sync.dma_start(out=o_scr[kk], in_=owstg[:]), "st", 16)
            on("sync", sow, deps=("cast",))

        tap("gscr0", g_scr[0], [128, 4096])
        tap("uscr0", u_scr[0], [128, 4096])
        tap("dscr00", d_scr[0, 0], [128, 512])
        tap("oscr0", o_scr[0], [128, 4096])

        def ln_stage(g, from_x1):
            sig = from_x1
            for tt in range(4):
                def ldx(sync, tt=tt, g=g, from_x1=from_x1):
                    if from_x1:
                        inc(sync.dma_start(out=xt[:], in_=x1_scr[tt]), "ld", 16)
                    else:
                        r0 = TG * g + 128 * tt
                        inc(sync.dma_start(out=xt[:], in_=xin[r0:r0 + 128, :]), "ld", 16)
                on("sync", ldx, deps=("vs", "ss", "st"))

                def lns(s):
                    inc(s.activation(out=xlntm[:], in_=xt[:], func=AF.Square,
                                     accum_out=lnst[:, 0:1]), "ss", 1)
                on("scalar", lns, deps=("ld", "ts", "vs"))

                def lnv(v):
                    inc(v.reduce_sum(out=lnst[:, 1:2], in_=xt[:],
                                     axis=mybir.AxisListType.X), "vs", 1)
                    wt(v, "vs")
                    v.tensor_scalar(out=lnst[:, 2:3], in0=lnst[:, 1:2], scalar1=1.0 / D,
                                    scalar2=None, op0=ALU.mult)
                    inc(v.tensor_scalar(out=lnst[:, 3:4], in0=lnst[:, 0:1],
                                        scalar1=1.0 / D, scalar2=None, op0=ALU.mult),
                        "vs", 1)
                    wt(v, "vs")
                    inc(v.tensor_mul(out=lnst[:, 4:5], in0=lnst[:, 2:3],
                                     in1=lnst[:, 2:3]), "vs", 1)
                    wt(v, "vs")
                    inc(v.tensor_sub(out=lnst[:, 5:6], in0=lnst[:, 3:4], in1=lnst[:, 4:5]),
                        "vs", 1)
                on("vector", lnv, deps=("ld", "ss"))

                def lns2(s):
                    inc(s.activation(out=lnst[:, 6:7], in_=lnst[:, 5:6], func=AF.Ln,
                                     bias=cp(EPS)), "ss", 1)
                    wt(s, "ss")
                    inc(s.activation(out=lnst[:, 7:8], in_=lnst[:, 6:7], func=AF.Exp,
                                     scale=cp(-0.5)), "ss", 1)
                on("scalar", lns2, deps=("vs",))

                def lnv2(v):
                    inc(v.tensor_scalar(out=xlntm[:], in0=xt[:], scalar1=lnst[:, 2:3],
                                        scalar2=lnst[:, 7:8], op0=ALU.subtract,
                                        op1=ALU.mult), "vs", 1)
                on("vector", lnv2, deps=("ss",))

                for wave in range(2):
                    def t_tw(t, wave=wave, sig=sig):
                        for half in range(2):
                            pt = pT1 if half == 0 else pT2
                            for m in range(8):
                                ks = wave * 16 + half * 8 + m
                                if not sig:
                                    src = xlntm[:, 128 * ks:128 * ks + 128]
                                else:
                                    mm2, par = ks // 2, ks % 2
                                    src = xlntm[:, 256 * mm2 + par:256 * mm2 + par + 255:2]
                                mmi = t.transpose(pt[:, 128 * m:128 * m + 128], src, ident[:])
                        inc(mmi, "ts", 1)
                    on("tensor", t_tw, deps=("vs",))

                    def v_tw(v, tt=tt, wave=wave):
                        for half in range(2):
                            pt = pT1 if half == 0 else pT2
                            ks0 = wave * 16 + half * 8
                            dsts = xlnT[:, TG * ks0:TG * (ks0 + 8)].rearrange(
                                "p (k r) -> p k r", k=8)[:, :, 128 * tt:128 * tt + 128]
                            inc(v.tensor_copy(
                                out=dsts, in_=pt[:].rearrange("p (k r) -> p k r", k=8)),
                                "vs", 1)
                    on("vector", v_tw, deps=("ts",))

        def proj_stage(blocks):
            base_cast = {0: cnt["ldp"], 1: cnt["ldp2"]}
            base_ts = cnt["ts"]
            base_vs = cnt["vs"]
            base_ss = cnt["ss"]
            ev_mark = []
            vv = ss_ = 0
            for (off, kind, dest) in blocks:
                if kind == "silu":
                    ss_ += 1
                    ev_mark.append(("ss", ss_))
                else:
                    vv += 1
                    ev_mark.append(("vs", vv))

            def gp(gps):
                for i, (off, kind, dest) in enumerate(blocks):
                    if i >= 2:
                        gps.wait_ge(sem["ts"], base_ts + (i - 2) + 1)
                    conv_waits(gps, off // 128)
                    inc(gps.dma_start(
                        out=wbuf[:, 4096 * (i % 2):4096 * (i % 2) + 4096],
                        in_=ip2[off // 128]),
                        "ldp" if i % 2 == 0 else "ldp2", 16)
            on("sync", gp, deps=("ts", "vs", "ss"))

            def tp(t):
                for i, (off, kind, dest) in enumerate(blocks):
                    cs_ = "ldp" if i % 2 == 0 else "ldp2"
                    t.wait_ge(sem[cs_], base_cast[i % 2] + 16 * (i // 2 + 1))
                    bank = PBANKS[i % 2]
                    if i >= 2:
                        s_, v_ = ev_mark[i - 2]
                        t.wait_ge(sem[s_], (base_ss if s_ == "ss" else base_vs) + v_)
                    w = wbuf[:, 4096 * (i % 2):4096 * (i % 2) + 4096]
                    for ks in range(32):
                        mmi = t.matmul(bank[:], w[:, 128 * ks:128 * ks + 128],
                                       xlnT[:, TG * ks:TG * ks + TG],
                                       start=(ks == 0), stop=(ks == 31))
                    inc(mmi, "ts", 1)
            onx("tensor", tp, ())

            def se(s):
                for i, (off, kind, dest) in enumerate(blocks):
                    if kind != "silu":
                        continue
                    s.wait_ge(sem["ts"], base_ts + i + 1)
                    inc(s.activation(out=dest, in_=PBANKS[i % 2][:], func=AF.Silu), "ss", 1)
            onx("scalar", se, ("ts",))

            def vep(v):
                for i, (off, kind, dest) in enumerate(blocks):
                    if kind == "silu":
                        continue
                    v.wait_ge(sem["ts"], base_ts + i + 1)
                    inc(v.tensor_copy(out=dest, in_=PBANKS[i % 2][:]), "vs", 1)
            onx("vector", vep, ("ss",))

        def conv_into(v, tile_idx, pre, par):
            rd = 240 * par + 3 * tile_idx
            sv = 240 * (1 - par) + 3 * tile_idx
            v.tensor_copy(out=pre[:, 0:3], in_=halo[:, rd:rd + 3])
            inc(v.tensor_copy(out=halo[:, sv:sv + 3], in_=pre[:, TG:TG + 3]), "vs", 1)
            wt(v, "vs")
            v.tensor_scalar(out=convo[:], in0=pre[:, 0:TG],
                            scalar1=cw[:, 4 * tile_idx:4 * tile_idx + 1],
                            scalar2=cb[:, tile_idx:tile_idx + 1], op0=ALU.mult, op1=ALU.add)
            inc(v.tensor_scalar(out=convo2[:], in0=pre[:, 1:1 + TG],
                                scalar1=cw[:, 4 * tile_idx + 1:4 * tile_idx + 2],
                                scalar2=None, op0=ALU.mult), "vs", 1)
            wt(v, "vs")
            v.scalar_tensor_tensor(out=convo[:], in0=pre[:, 2:2 + TG],
                                   scalar=cw[:, 4 * tile_idx + 2:4 * tile_idx + 3],
                                   in1=convo[:], op0=ALU.mult, op1=ALU.add)
            inc(v.scalar_tensor_tensor(out=convo2[:], in0=pre[:, 3:3 + TG],
                                       scalar=cw[:, 4 * tile_idx + 3:4 * tile_idx + 4],
                                       in1=convo2[:], op0=ALU.mult, op1=ALU.add), "vs", 1)
            wt(v, "vs")
            return v.tensor_add(out=convs[:], in0=convo[:], in1=convo2[:])

        _ck = [0]

        def ckpt():
            _ck[0] += 1
            if limit is not None and _ck[0] >= limit:
                raise _StopEmit()

        if only_ln:
            ln_stage(0, False)
            def fin0(sync):
                wt(sync, "st")
            on("sync", fin0)
            return nc, dbg
        def _emit_groups():
          for g in range(ngrp):
            full = g >= npref
            with nc.named_scope(f"g{g}_ln"):
                ln_stage(g, False)
            ckpt()
            if g == npref:
                tap("srun6", s_run[:], [128, 64 * 128])
                tap("xlnT6", xlnT[:], [128, 32 * TG])

            sec = [(B0 + 128 * gb, 64 + gb, bsil, gb) for gb in range(8)]
            if full:
                sec += [(C0 + 128 * gb, 72 + gb, csil, gb) for gb in range(8)]
            scope_bc = nc.enter_named_scope(f"g{g}_bc", False)
            for (off, tile, dstbuf, gb) in sec:
                proj_stage([(off, "f32", bcpre[:, 3:3 + TG])])

                def cvb(v, tile=tile, g=g):
                    inc(conv_into(v, tile, bcpre, g % 2), "vs", 1)
                on("vector", cvb, deps=("vs", "ss"))

                def slb(s, dstbuf=dstbuf, gb=gb):
                    inc(s.activation(out=dstbuf[:, TG * gb:TG * gb + TG],
                                     in_=convs[:], func=AF.Silu), "ss", 1)
                on("scalar", slb, deps=("vs",))

            nc.leave_named_scope(f"g{g}_bc", scope_bc[0], False)
            ckpt()
            scope_dt = nc.enter_named_scope(f"g{g}_dt", False)
            proj_stage([(DT0, "f32", dtpre[:])])

            def dt_vec(v):
                inc(v.tensor_scalar(out=dtt[:], in0=dtpre[:], scalar1=dtb[:, 0:1],
                                    scalar2=None, op0=ALU.add), "vs", 1)
                wt(v, "vs")
                inc(v.tensor_relu(out=dtpre[:], in_=dtt[:]), "vs", 1)
            on("vector", dt_vec, deps=("vs", "ts"))

            def dt_scal(s):
                inc(s.activation(out=adt[:], in_=dtt[:], func=AF.Abs), "ss", 1)
                wt(s, "ss")
                inc(s.activation(out=adt[:], in_=adt[:], func=AF.Exp, scale=cp(-1.0)),
                    "ss", 1)
                wt(s, "ss")
                inc(s.activation(out=adt[:], in_=adt[:], func=AF.Ln, bias=cp(1.0)), "ss", 1)
            on("scalar", dt_scal, deps=("vs",))

            def dt_vec2(v, g=g):
                inc(v.tensor_add(out=dtt[:], in0=dtpre[:], in1=adt[:]), "vs", 1)
                wt(v, "vs")
                inc(v.tensor_scalar(out=dtt[:], in0=dtt[:], scalar1=100.0, scalar2=None,
                                    op0=ALU.min), "vs", 1)
                wt(v, "vs")
                for c in range(4):
                    slot = 4 * g + c
                    i_ = v.tensor_scalar(out=dtt[:, 128 * c:128 * c + 128],
                                         in0=dtt[:, 128 * c:128 * c + 128],
                                         scalar1=mcol[:, slot:slot + 1], scalar2=None,
                                         op0=ALU.mult)
                inc(i_, "vs", 1)
                wt(v, "vs")
                inc(v.tensor_scalar(out=adt[:], in0=dtt[:], scalar1=acol[:, 0:1],
                                    scalar2=None, op0=ALU.mult), "vs", 1)
                wt(v, "vs")
                for c in range(4):
                    cs = slice(128 * c, 128 * c + 128)
                    i_ = v.tensor_tensor_scan(out=acum[:, cs], data0=adt[:, cs],
                                              data1=zeros128[:], initial=0.0,
                                              op0=ALU.add, op1=ALU.add)
                inc(i_, "vs", 1)
                wt(v, "vs")
                for c in range(4):
                    i_ = v.tensor_copy(out=alast[:, c:c + 1],
                                       in_=acum[:, 128 * c + 127:128 * c + 128])
                inc(i_, "vs", 1)
                wt(v, "vs")
                for c in range(4):
                    cs = slice(128 * c, 128 * c + 128)
                    i_ = v.tensor_scalar(out=dscr[:, cs], in0=acum[:, cs],
                                         scalar1=alast[:, c:c + 1], scalar2=None,
                                         op0=ALU.subtract)
                inc(i_, "vs", 1)
            on("vector", dt_vec2, deps=("ss", "vs", "ld", "ts"))

            def dt_scal2(s):
                s.activation(out=dscr[:], in_=dscr[:], func=AF.Exp, scale=cp(-1.0))
                inc(s.activation(out=exal[:], in_=alast[:], func=AF.Exp), "ss", 1)
            on("scalar", dt_scal2, deps=("vs", "ts"))

            for c in range(4):
                def gv(v, c=c):
                    inc(v.tensor_scalar(out=gexp[:], in0=epj01[:], scalar1=exal[:, c:c + 1],
                                        scalar2=None, op0=ALU.mult), "vs", 1)
                on("vector", gv, deps=("ss", "ts"))

                def gt(t, c=c, full=full):
                    cs = slice(128 * c, 128 * c + 128)
                    t.matmul(pF[:, 0:64], fpair[:], gexp[:], start=True, stop=True)
                    t.transpose(pF[:, 128:256], dscr[:, cs], identf[:])
                    mmi = t.transpose(pF[:, 256:384], dtt[:, cs], identf[:])
                    if full:
                        mmi = t.transpose(pF[:, 384:512], acum[:, cs], identf[:])
                    inc(mmi, "ts", 1)
                on("tensor", gt, deps=("vs", "ss"))

                def gv2(v, c=c, full=full):
                    v.tensor_copy(out=expal[:, 64 * c:64 * c + 64], in_=pF[:, 0:64])
                    v.tensor_copy(out=d1T[:, 128 * c:128 * c + 128], in_=pF[:, 128:256])
                    i2 = v.tensor_copy(out=dtT[:, 128 * c:128 * c + 128], in_=pF[:, 256:384])
                    if full:
                        i2 = v.tensor_copy(out=acumT[:, 128 * c:128 * c + 128],
                                           in_=pF[:, 384:512])
                    inc(i2, "vs", 1)
                on("vector", gv2, deps=("ts",))

            def ddv(v):
                inc(v.tensor_mul(out=dd1T[:], in0=dtT[:], in1=d1T[:]), "vs", 1)
            on("vector", ddv, deps=("vs",))

            if full:
                def exa(s):
                    inc(s.activation(out=expacT[:], in_=acumT[:], func=AF.Exp), "ss", 1)
                on("scalar", exa, deps=("vs",))

            if g == npref:
                tap("alast6", alast[:], [128, 4])
                tap("exal6", exal[:], [128, 4])
                tap("adt6", adt[:], [128, TG])
                tap("dtpre6", dtpre[:], [128, TG])
                tap("dscrx6", dscr[:], [128, TG])
                tap("bsil6", bsil[:], [128, 8 * TG])
                tap("csil6", csil[:], [128, 8 * TG])
                tap("dtt6", dtt[:], [128, TG])
                tap("acum6", acum[:], [128, TG])
                tap("d1T6", d1T[:], [128, 4 * 128])
                tap("dtT6", dtT[:], [128, 4 * 128])
                tap("expal6", expal[:], [128, 4 * 64])
                tap("acumT6", acumT[:], [128, 4 * 128])
                tap("expacT6", expacT[:], [128, 4 * 128])

            for wave in range(2):
                def bt(t, wave=wave):
                    for half in range(2):
                        pt = pT1 if half == 0 else pT2
                        for m in range(8):
                            idx = wave * 16 + half * 8 + m
                            c, gb = idx // 8, idx % 8
                            mmi = t.transpose(
                                pt[:, 128 * m:128 * m + 128],
                                bsil[:, TG * gb + 128 * c:TG * gb + 128 * c + 128],
                                ident[:])
                    inc(mmi, "ts", 1)
                on("tensor", bt, deps=("vs", "ss"))

                def bv(v, wave=wave):
                    for half in range(2):
                        pt = pT1 if half == 0 else pT2
                        idx0 = wave * 16 + half * 8
                        dsts = bcm[:, 128 * idx0:128 * (idx0 + 8)].rearrange(
                            "p (k r) -> p k r", k=8)
                        inc(v.tensor_copy(out=dsts,
                                          in_=pt[:].rearrange("p (k r) -> p k r", k=8)),
                            "vs", 1)
                on("vector", bv, deps=("ts",))

            tap(f"sr_a{g}", s_run[:], [128, 64 * 128])
            tap(f"dd_{g}", dd1T[:], [128, 4 * 128])
            tap(f"ex_{g}", expal[:], [128, 4 * 64])
            tap(f"hs_{g}", hsall[:], [128, 8 * TG])

            nc.leave_named_scope(f"g{g}_dt", scope_dt[0], False)
            ckpt()
            scope_hb = nc.enter_named_scope(f"g{g}_hb", False)
            for hb in range(8):
                ckpt()
                for j8 in range(8):
                    jj = 8 * hb + j8
                    blocks = []
                    if full:
                        blocks.append((GATE0 + 128 * jj, "silu",
                                       gsall[:, TG * j8:TG * j8 + TG]))
                    blocks.append((HS0 + 128 * jj, "f32", hspre[:, 3:3 + TG]))
                    proj_stage(blocks)

                    def cvh(v, jj=jj, g=g):
                        inc(conv_into(v, jj, hspre, g % 2), "vs", 1)
                    on("vector", cvh, deps=("vs", "ss"))

                    def svh(s, j8=j8):
                        inc(s.activation(out=hsall[:, TG * j8:TG * j8 + TG],
                                         in_=convs[:], func=AF.Silu), "ss", 1)
                    on("scalar", svh, deps=("vs",))

                if g == npref and hb == 0:
                    tap("hsall6", hsall[:], [128, 8 * TG])
                    tap("gsall6", gsall[:], [128, 8 * TG])
                if full:
                    for c in range(4):
                        def g1tc(t, hb=hb, c=c):
                            cs = slice(TG * hb + 128 * c, TG * hb + 128 * c + 128)
                            inc(t.matmul(pE[:, 0:128], bsil[:, cs], csil[:, cs],
                                         start=True, stop=True), "ts", 1)
                        on("tensor", g1tc, deps=("vs", "ss"))

                        def g1vc(v, c=c):
                            inc(v.tensor_mul(out=g1m[:, 128 * c:128 * c + 128],
                                             in0=pE[:, 0:128], in1=triu[:]), "vs", 1)
                        on("vector", g1vc, deps=("ts",))
                    if full and g == npref and hb == 0:
                        tap("g1m0", g1m[:], [128, 512])

                for c in range(4):
                    if full:
                        def acg(sync, hb=hb, c=c):
                            cs = slice(128 * c, 128 * c + 128)
                            inc(sync.dma_start(out=acrow[0:1, :],
                                               in_=acum[16 * hb:16 * hb + 16, cs]),
                                "ld", 16)
                        on("sync", acg, deps=("vs", "ts"))

                    for j8 in range(8):
                        jj = 8 * hb + j8

                        def trh(t, j8=j8, full=full, c=c):
                            mmi = t.transpose(
                                pT2[:, 0:128],
                                hsall[:, TG * j8 + 128 * c:TG * j8 + 128 * c + 128],
                                ident[:])
                            if full:
                                mmi = t.transpose(
                                    pT2[:, 128:256],
                                    gsall[:, TG * j8 + 128 * c:TG * j8 + 128 * c + 128],
                                    ident[:])
                            inc(mmi, "ts", 1)
                        on("tensor", trh, deps=("vs", "ss"))

                        def hcm(v, jj=jj, j8=j8, c=c, full=full):
                            for hh in range(2):
                                h = 2 * jj + hh
                                band = slice(64 * hh, 64 * hh + 64)
                                i1 = v.tensor_scalar(
                                    out=xchp[:, band], in0=pT2[:, band],
                                    scalar1=dd1T[:, 128 * c + h:128 * c + h + 1],
                                    scalar2=None, op0=ALU.mult)
                            if full:
                                for hh in range(2):
                                    h = 2 * jj + hh
                                    band = slice(64 * hh, 64 * hh + 64)
                                    v.tensor_scalar(out=xch_cm[:, band], in0=pT2[:, band],
                                                    scalar1=dtT[:, 128 * c + h:128 * c + h + 1],
                                                    scalar2=None, op0=ALU.mult)
                                dg = gatm[:].rearrange("p (k r) -> p k r", k=4)[
                                    :, c:c + 1, 128 * j8:128 * j8 + 128]
                                v.tensor_copy(out=dg, in_=pT2[:, 128:256].unsqueeze(1))
                                i1 = v.tensor_scalar(
                                    out=hsdt[:],
                                    in0=hsall[:, TG * j8 + 128 * c:TG * j8 + 128 * c + 128],
                                    scalar1=dxp[:, jj:jj + 1], scalar2=None, op0=ALU.mult)
                            inc(i1, "vs", 1)
                        on("vector", hcm, deps=("ts",))
                        if g == npref and hb == 0 and c == 0 and j8 == 0:
                            tap("xchp0", xchp[:], [128, 128])
                            tap("xcm0", xch_cm[:], [128, 128])
                            tap("hsdt0", hsdt[:], [128, 128])

                        if full:
                            def sst(t, jj=jj, j8=j8, c=c, hb=hb):
                                t.transpose(pF[:, 0:128],
                                            s_run[:, 128 * jj:128 * jj + 128], identf[:])
                                for hh in range(2):
                                    lh = 2 * j8 + hh
                                    t.matmul(pF[:, 128 + 128 * hh:256 + 128 * hh], onesr[:],
                                             acrow[0:1, 128 * lh:128 * lh + 128],
                                             start=True, stop=True)
                                inc(t.matmul(pE[:, 0:128], xchp[:],
                                             bcm[:, 128 * (8 * c + hb):128 * (8 * c + hb) + 128],
                                             start=True, stop=True), "ts", 1)
                            on("tensor", sst, deps=("vs", "ss", "ld"))
                            if g == npref and hb == 0 and c == 0 and j8 == 0:
                                def dcp1(v):
                                    v.tensor_copy(out=ggf[:, 0:384], in_=pF[:, 0:384])
                                    inc(v.tensor_copy(out=ggf[:, 384:512],
                                                      in_=pE[:, 0:128]), "vs", 1)
                                on("vector", dcp1)
                                tap("pFE0", ggf[:, 0:512], [128, 512])
                                tap("acrow0", acrow[0:1, :], [1, 2048])

                            def ssa(s, jj=jj, c=c):
                                for hh in range(2):
                                    h = 2 * jj + hh
                                    ii = s.activation(
                                        out=relu_s[:, 128 * hh:128 * hh + 128],
                                        in_=pF[:, 128 + 128 * hh:256 + 128 * hh],
                                        func=AF.Relu, scale=cp(-1.0),
                                        bias=acumT[:, 128 * c + h:128 * c + h + 1])
                                inc(ii, "ss", 1)
                                wt(s, "ss")
                                for hh in range(2):
                                    ii = s.activation(out=relu_s[:, 128 * hh:128 * hh + 128],
                                                      in_=relu_s[:, 128 * hh:128 * hh + 128],
                                                      func=AF.Exp, scale=cp(-1.0))
                                inc(ii, "ss", 1)
                            on("scalar", ssa, deps=("ts",))
                            if g == npref and hb == 0 and c == 0 and j8 == 0:
                                tap("relu0", relu_s[:], [128, 256])

                            def ssv(v, jj=jj, c=c):
                                v.tensor_copy(out=prevT[:], in_=pF[:, 0:128])
                                for hh in range(2):
                                    h = 2 * jj + hh
                                    band = slice(64 * hh, 64 * hh + 64)
                                    v.tensor_scalar(
                                        out=expacx[:, band], in0=ones64[:],
                                        scalar1=expacT[:, 128 * c + h:128 * c + h + 1],
                                        scalar2=None, op0=ALU.mult)
                                for hh in range(2):
                                    ii = v.tensor_mul(out=mh[:, 128 * hh:128 * hh + 128],
                                                      in0=relu_s[:, 128 * hh:128 * hh + 128],
                                                      in1=g1m[:, 128 * c:128 * c + 128])
                                inc(ii, "vs", 1)
                            on("vector", ssv, deps=("ss",))
                            if g == npref and hb == 0 and c == 0 and j8 == 0:
                                tap("mh0", mh[:], [128, 256])
                                tap("prevT0", prevT[:], [128, 128])
                                tap("expacx0", expacx[:], [128, 128])

                            def sst2(t, jj=jj, c=c, hb=hb):
                                for hh in range(2):
                                    t.matmul(pC[:, 64 * hh:64 * hh + 64],
                                             mh[:, 128 * hh:128 * hh + 128],
                                             xch_cm[:, 64 * hh:64 * hh + 64],
                                             start=(hh == 0), stop=False)
                                t.matmul(pC[:, 0:128], hsdt[:], ident[:],
                                         start=False, stop=True)
                                inc(t.matmul(pD[:, 0:128],
                                             csil[:, TG * hb + 128 * c:TG * hb + 128 * c + 128],
                                             prevT[:], start=True, stop=True), "ts", 1)
                            on("tensor", sst2, deps=("vs",))
                            if g == npref and hb == 0 and c == 0 and j8 == 0:
                                def dcp2(v):
                                    v.tensor_copy(out=ggf[:, 0:128], in_=pC[:, 0:128])
                                    inc(v.tensor_copy(out=ggf[:, 128:256],
                                                      in_=pD[:, 0:128]), "vs", 1)
                                on("vector", dcp2)
                                tap("pCD0", ggf[:, 0:256], [128, 256])

                            def ssv2(v, jj=jj, c=c, j8=j8):
                                v.tensor_mul(out=yoffc[:], in0=pD[:, 0:128], in1=expacx[:])
                                inc(v.scalar_tensor_tensor(
                                    out=s_run[:, 128 * jj:128 * jj + 128],
                                    in0=s_run[:, 128 * jj:128 * jj + 128],
                                    scalar=expal[:, 64 * c + jj:64 * c + jj + 1],
                                    in1=pE[:, 0:128], op0=ALU.mult, op1=ALU.add), "vs", 1)
                                wt(v, "vs")
                                inc(v.tensor_add(
                                    out=ytm[:, 1024 * c + 128 * j8:1024 * c + 128 * j8 + 128],
                                    in0=pC[:, 0:128], in1=yoffc[:]), "vs", 1)
                            on("vector", ssv2, deps=("ts",))
                            if g == npref and hb == 0 and c == 0 and j8 == 0:
                                tap("yoffc0", yoffc[:], [128, 128])
                        else:
                            def sstp(t, c=c, hb=hb):
                                inc(t.matmul(pE[:, 0:128], xchp[:],
                                             bcm[:, 128 * (8 * c + hb):128 * (8 * c + hb) + 128],
                                             start=True, stop=True), "ts", 1)
                            on("tensor", sstp, deps=("vs",))

                            def ssvp(v, jj=jj, c=c):
                                inc(v.scalar_tensor_tensor(
                                    out=s_run[:, 128 * jj:128 * jj + 128],
                                    in0=s_run[:, 128 * jj:128 * jj + 128],
                                    scalar=expal[:, 64 * c + jj:64 * c + jj + 1],
                                    in1=pE[:, 0:128], op0=ALU.mult, op1=ALU.add), "vs", 1)
                            on("vector", ssvp, deps=("ts",))

                    if full:
                        def ggv(v, c=c):
                            inc(v.tensor_mul(out=ggf[:], in0=ytm[:, 1024 * c:1024 * c + 1024],
                                             in1=gatm[:, 1024 * c:1024 * c + 1024]), "vs", 1)
                        on("vector", ggv, deps=("vs", "ss"))

                        def ggs(s, c=c):
                            inc(s.activation(out=ytm[:, 1024 * c:1024 * c + 1024],
                                             in_=ggf[:], func=AF.Square,
                                             accum_out=rmsq[:, c:c + 1]), "ss", 1)
                            wt(s, "ss")
                            inc(s.activation(out=rmsq[:, 4 + c:5 + c], in_=rmsq[:, c:c + 1],
                                             func=AF.Ln, scale=cp(1.0 / 1024.0),
                                             bias=cp(EPS)), "ss", 1)
                            wt(s, "ss")
                            inc(s.activation(out=rmsq[:, 4 + c:5 + c],
                                             in_=rmsq[:, 4 + c:5 + c],
                                             func=AF.Exp, scale=cp(-0.5)), "ss", 1)
                        on("scalar", ggs, deps=("vs",))

                        def ggv2(v, c=c):
                            inc(v.tensor_scalar(out=ggb[:], in0=ggf[:],
                                                scalar1=rmsq[:, 4 + c:5 + c], scalar2=None,
                                                op0=ALU.mult), "vs", 1)
                        on("vector", ggv2, deps=("ss",))

                        def ggt(t):
                            for j8 in range(8):
                                mmi = t.transpose(pT1[:, 128 * j8:128 * j8 + 128],
                                                  ggb[:, 128 * j8:128 * j8 + 128], ident[:])
                            inc(mmi, "ts", 1)
                        on("tensor", ggt, deps=("vs",))

                        def ggv3(v, c=c, hb=hb):
                            for j8 in range(8):
                                jj = 8 * hb + j8
                                ii = v.tensor_scalar(
                                    out=ggtb[:, TG * j8 + 128 * c:TG * j8 + 128 * c + 128],
                                    in0=pT1[:, 128 * j8:128 * j8 + 128],
                                    scalar1=nwT[:, jj:jj + 1], scalar2=None, op0=ALU.mult)
                            inc(ii, "vs", 1)
                        on("vector", ggv3, deps=("ts", "st"))

                if g == npref and hb == 0:
                    tap("ytm6", ytm[:], [128, 4 * 1024])
                    tap("gatm6", gatm[:], [128, 4 * 1024])
                    tap("ggtb6", ggtb[:], [128, 8 * TG])
                if full:
                    def ggw(sync, hb=hb):
                        inc(sync.dma_start(
                            out=gg_scr[8 * hb:8 * hb + 8].rearrange("k p c -> p k c"),
                            in_=ggtb[:].rearrange("p (k c) -> p k c", k=8)), "st", 16)
                    on("sync", ggw, deps=("vs",))

            nc.leave_named_scope(f"g{g}_hb", scope_hb[0], False)
            tap(f"sr_b{g}", s_run[:], [128, 64 * 128])
            ckpt()

            if full:
                scope_op = nc.enter_named_scope(f"g{g}_op", False)
                for db in range(8):
                    base_ld = {0: cnt["ld"], 1: cnt["ld2"]}
                    base_ts = cnt["ts"]

                    def owl(sync, db=db, base_ts=base_ts):
                        for kk in range(64):
                            if kk >= 2:
                                sync.wait_ge(sem["ts"], base_ts + (kk - 2) + 1)
                            ls_ = "ld" if kk % 2 == 0 else "ld2"
                            inc(sync.dma_start(
                                out=obuf[:, 512 * (kk % 2):512 * (kk % 2) + 512],
                                in_=o_scr[kk][:, 512 * db:512 * db + 512]), ls_, 16)
                            inc(sync.dma_start(
                                out=ggl[:, 512 * (kk % 2):512 * (kk % 2) + 512],
                                in_=gg_scr[kk]), ls_, 16)
                    on("sync", owl, deps=("ts", "st"))

                    def owt(t, base_ld=base_ld):
                        for kk in range(64):
                            ls_ = "ld" if kk % 2 == 0 else "ld2"
                            t.wait_ge(sem[ls_], base_ld[kk % 2] + 32 * (kk // 2 + 1))
                            for tb in range(4):
                                mmi = t.matmul(
                                    PBANKS[tb][:],
                                    ggl[:, 512 * (kk % 2) + 128 * tb:512 * (kk % 2) + 128 * tb + 128],
                                    obuf[:, 512 * (kk % 2):512 * (kk % 2) + 512],
                                    start=(kk == 0), stop=(kk == 63))
                            inc(mmi, "ts", 1)
                    onx("tensor", owt, ("vs",))

                    for tb in range(4):
                        def exl(sync, g=g, db=db, tb=tb):
                            r0 = TG * g + 128 * tb
                            inc(sync.dma_start(out=xsl[:],
                                               in_=xin[r0:r0 + 128, 512 * db:512 * db + 512]),
                                "ld", 16)
                        on("sync", exl, deps=("ts",))

                        def exv(v, tb=tb):
                            inc(v.scalar_tensor_tensor(out=outt[:], in0=PBANKS[tb][:],
                                                       scalar=1.0, in1=xsl[:],
                                                       op0=ALU.mult, op1=ALU.add), "vs", 1)
                        on("vector", exv, deps=("ts", "ld", "st"))

                        def exw(sync, db=db, tb=tb):
                            inc(sync.dma_start(out=x1_scr[tb][:, 512 * db:512 * db + 512],
                                               in_=outt[:]), "st", 16)
                        on("sync", exw, deps=("vs",))

                if g == npref:
                    for tt in range(4):
                        tap(f"x1_{tt}", x1_scr[tt], [128, D])

                nc.leave_named_scope(f"g{g}_op", scope_op[0], False)
                with nc.named_scope(f"g{g}_ln2"):
                    ln_stage(g, True)
                if g == npref:
                    tap("xln2T6", xlnT[:], [128, 32 * TG])

                scope_fg = nc.enter_named_scope(f"g{g}_fgu", False)
                for ib in range(86):
                    def fl(sync, ib=ib):
                        inc(sync.dma_start(out=fwbuf[:, 0:4096], in_=g_scr[ib]), "ld", 16)
                        inc(sync.dma_start(out=fwbuf[:, 4096:8192], in_=u_scr[ib]), "ld", 16)
                    on("sync", fl, deps=("ts", "vs"))

                    def ft(t):
                        for ks in range(32):
                            t.matmul(pA[:], fwbuf[:, 128 * ks:128 * ks + 128],
                                     xlnT[:, TG * ks:TG * ks + TG],
                                     start=(ks == 0), stop=(ks == 31))
                        for ks in range(32):
                            mmi = t.matmul(pB[:], fwbuf[:, 4096 + 128 * ks:4096 + 128 * ks + 128],
                                           xlnT[:, TG * ks:TG * ks + TG],
                                           start=(ks == 0), stop=(ks == 31))
                        inc(mmi, "ts", 1)
                    on("tensor", ft, deps=("ld",))

                    def fs(s):
                        inc(s.activation(out=fsil[:], in_=pA[:], func=AF.Silu), "ss", 1)
                    on("scalar", fs, deps=("ts",))

                    def fv(v, ib=ib):
                        inc(v.tensor_mul(out=hsw[:, TG * ib:TG * ib + TG], in0=pB[:],
                                         in1=fsil[:]), "vs", 1)
                    on("vector", fv, deps=("ss",))

                if g == npref:
                    tap("hsw6", hsw[:, 0:4 * TG], [128, 4 * TG])

                nc.leave_named_scope(f"g{g}_fgu", scope_fg[0], False)
                scope_fd = nc.enter_named_scope(f"g{g}_fdn", False)
                for dc in range(8):
                    for tb8 in range(11):
                        nt = 8 if tb8 < 10 else 6
                        t0_ = 8 * tb8

                        def fdl(sync, dc=dc, t0_=t0_, nt=nt):
                            inc(sync.dma_start(
                                out=fwbuf2[:, 0:512 * nt].rearrange("p (t c) -> p t c", t=nt),
                                in_=d_scr[dc, t0_:t0_ + nt].rearrange("t p c -> p t c")),
                                "ld", 16)
                        on("sync", fdl, deps=("ts",))

                        def fdt(t, t0_=t0_, nt=nt):
                            for ti in range(nt):
                                tsub = t0_ + ti
                                for tb in range(4):
                                    mmi = t.matmul(
                                        PBANKS[tb][:],
                                        hsw[:, TG * tsub + 128 * tb:TG * tsub + 128 * tb + 128],
                                        fwbuf2[:, 512 * ti:512 * ti + 512],
                                        start=(tsub == 0), stop=(tsub == 85))
                            inc(mmi, "ts", 1)
                        on("tensor", fdt, deps=("ld", "ss", "vs"))

                    for tb in range(4):
                        def fxl(sync, dc=dc, tb=tb):
                            inc(sync.dma_start(out=xsl[:],
                                               in_=x1_scr[tb][:, 512 * dc:512 * dc + 512]),
                                "ld", 16)
                        on("sync", fxl, deps=("ts",))

                        def fxv(v, tb=tb):
                            inc(v.scalar_tensor_tensor(out=outt[:], in0=PBANKS[tb][:],
                                                       scalar=1.0, in1=xsl[:],
                                                       op0=ALU.mult, op1=ALU.add), "vs", 1)
                        on("vector", fxv, deps=("ts", "ld", "st"))

                        def fxw(sync, g=g, dc=dc, tb=tb):
                            r0 = TG * (g - npref) + 128 * tb
                            inc(sync.dma_start(out=yout[r0:r0 + 128, 512 * dc:512 * dc + 512],
                                               in_=outt[:]), "st", 16)
                        on("sync", fxw, deps=("vs",))

            if full:
                nc.leave_named_scope(f"g{g}_fdn", scope_fd[0], False)
        try:
            _emit_groups()
        except _StopEmit:
            pass
        def fin(sync):
            wt(sync, "st")
            wt(sync, "cast2")
        on("sync", fin)

    return nc, dbg



def host_prep(inputs):
    W = np.asarray(inputs["L0_attn_in_proj_w"])
    inprojT = np.concatenate([
        W[8192:16384].T, W[16384:17408].T, W[18432:18560].T,
        W[0:8192].T, W[17408:18432].T], axis=1)
    inprojT = np.ascontiguousarray(inprojT, dtype=np.float32)
    outprojT = np.ascontiguousarray(np.asarray(inputs["L0_attn_out_proj_w"]).T,
                                    dtype=np.float32)
    convw = np.ascontiguousarray(
        np.asarray(inputs["L0_attn_conv_w"])[:, 0, :].reshape(80, 128, 4)
        .transpose(1, 0, 2).reshape(128, 80 * 4), dtype=np.float32)
    convb = np.ascontiguousarray(
        np.asarray(inputs["L0_attn_conv_b"]).reshape(80, 128).T, dtype=np.float32)
    dtbias = np.ascontiguousarray(np.asarray(inputs["L0_attn_dt_bias"]).reshape(128, 1),
                                  dtype=np.float32)
    alogv = np.ascontiguousarray(np.asarray(inputs["L0_attn_A_log"]).reshape(128, 1),
                                 dtype=np.float32)
    Dv = np.asarray(inputs["L0_attn_D"]).astype(np.float32)
    dexp = np.empty((128, 64), dtype=np.float32)
    for j in range(64):
        dexp[0:64, j] = Dv[2 * j]
        dexp[64:128, j] = Dv[2 * j + 1]
    normw = np.asarray(inputs["L0_attn_norm_w"]).astype(np.float32)
    normwT = np.ascontiguousarray(normw.reshape(64, 128).T, dtype=np.float32)

    tp = tau_perm(FI)

    def u8(x):
        a = np.asarray(x)
        return np.ascontiguousarray(a.view(np.uint8).reshape(a.shape[0], a.shape[1], 4)[:, :, 0])

    gqv = np.ascontiguousarray(u8(inputs["L0_ffn_gate_Wq"])[tp])
    uqv = np.ascontiguousarray(u8(inputs["L0_ffn_up_Wq"])[tp])
    dqv = u8(inputs["L0_ffn_down_Wq"])
    gscv = np.ascontiguousarray(np.asarray(inputs["L0_ffn_gate_scale"], dtype=np.float32)[tp])
    gzrv = np.ascontiguousarray(np.asarray(inputs["L0_ffn_gate_zero"], dtype=np.float32)[tp])
    uscv = np.ascontiguousarray(np.asarray(inputs["L0_ffn_up_scale"], dtype=np.float32)[tp])
    uzrv = np.ascontiguousarray(np.asarray(inputs["L0_ffn_up_zero"], dtype=np.float32)[tp])
    dscv = np.ascontiguousarray(np.asarray(inputs["L0_ffn_down_scale"], dtype=np.float32))
    dzrv = np.ascontiguousarray(np.asarray(inputs["L0_ffn_down_zero"], dtype=np.float32))

    shared = dict(inprojT=inprojT, outprojT=outprojT, convw=convw, convb=convb,
                  dtbias=dtbias, alog=alogv, dexp=dexp, normwT=normwT,
                  gq=gqv, uq=uqv, dq=dqv, gsc=gscv, gzr=gzrv, usc=uscv, uzr=uzrv,
                  dsc=dscv, dzr=dzrv)

    x = np.asarray(inputs["L0_attn_x"], dtype=np.float32)
    own_len = (NGRP - NPREF) * TG
    in_maps = []
    for core in range(8):
        b, q = core // 4, core % 4
        xb = x[b]
        npref_real = min((own_len // 128) * q, NPREF * 4)
        njunk = NPREF * 4 - npref_real
        parts = []
        if njunk > 0:
            parts.append(np.zeros((njunk * 128, D), dtype=np.float32))
        if npref_real > 0:
            parts.append(xb[0:npref_real * 128])
        parts.append(xb[own_len * q:own_len * (q + 1)])
        xinv = np.ascontiguousarray(np.concatenate(parts, axis=0))
        mask = np.zeros(NGRP * 4, dtype=np.float32)
        mask[njunk:] = 1.0
        maskcv = np.ascontiguousarray(np.tile(mask[None, :], (128, 1)).astype(np.float32))
        m = dict(shared)
        m["xin"] = xinv
        m["maskc"] = maskcv
        in_maps.append(m)
    return in_maps


def kernel(**inputs):
    in_maps = host_prep(inputs)
    nc, _ = build()
    res = run_bass_kernel_spmd(nc, in_maps, core_ids=list(range(8)))
    out = np.empty((2, 4096, 4096), dtype=np.float32)
    for core in range(8):
        b, q = core // 4, core % 4
        out[b, 1024 * q:1024 * (q + 1)] = res.results[core]["yout"]
    return out


# version 05 (final)
# speedup vs baseline: 1.1668x; 1.0104x over previous
import numpy as np
import ml_dtypes
import concourse.bass as bass
import concourse.mybir as mybir
from concourse.bass_utils import run_bass_kernel_spmd

F32 = mybir.dt.float32
BF16 = mybir.dt.bfloat16
U8 = mybir.dt.uint8
AF = mybir.ActivationFunctionType
ALU = mybir.AluOpType

D = 4096
INTER = 8192
PROJ = 18560
FI = 11008
TG = 512
NGRP = 8
NPREF = 6
EPS = 1e-5
MAGIC = 8388608.0 + 4194304.0

HS0, B0, DT0, GATE0, C0 = 0, 8192, 9216, 9344, 17536

BF16NP = ml_dtypes.bfloat16


def tau_perm(n):
    out = np.empty(n, dtype=np.int64)
    ar = 2 * np.arange(128)
    for m in range(n // 256):
        out[m * 256:m * 256 + 128] = m * 256 + ar
        out[m * 256 + 128:m * 256 + 256] = m * 256 + ar + 1
    return out


class _StopEmit(Exception):
    pass


def build(debug=(), skip_phase_a=False, only_ln=False, limit=None):
    ngrp, npref = NGRP, NPREF
    nfull = ngrp - npref
    nc = bass.Bass("TRN2", target_bir_lowering=False)

    xin = nc.dram_tensor("xin", [ngrp * TG, D], F32, kind="ExternalInput").ap()
    maskc = nc.dram_tensor("maskc", [128, ngrp * 4], F32, kind="ExternalInput").ap()
    inprojT = nc.dram_tensor("inprojT", [D, PROJ], F32, kind="ExternalInput").ap()
    outprojT = nc.dram_tensor("outprojT", [INTER, D], F32, kind="ExternalInput").ap()
    convw = nc.dram_tensor("convw", [128, 80 * 4], F32, kind="ExternalInput").ap()
    convb = nc.dram_tensor("convb", [128, 80], F32, kind="ExternalInput").ap()
    dtbias = nc.dram_tensor("dtbias", [128, 1], F32, kind="ExternalInput").ap()
    alog = nc.dram_tensor("alog", [128, 1], F32, kind="ExternalInput").ap()
    dexp = nc.dram_tensor("dexp", [128, 64], F32, kind="ExternalInput").ap()
    normwT = nc.dram_tensor("normwT", [128, 64], F32, kind="ExternalInput").ap()
    gq = nc.dram_tensor("gq", [FI, D // 2], U8, kind="ExternalInput").ap()
    uq = nc.dram_tensor("uq", [FI, D // 2], U8, kind="ExternalInput").ap()
    dq = nc.dram_tensor("dq", [D, FI // 2], U8, kind="ExternalInput").ap()
    gsc = nc.dram_tensor("gsc", [FI, 64], F32, kind="ExternalInput").ap()
    gzr = nc.dram_tensor("gzr", [FI, 64], F32, kind="ExternalInput").ap()
    usc = nc.dram_tensor("usc", [FI, 64], F32, kind="ExternalInput").ap()
    uzr = nc.dram_tensor("uzr", [FI, 64], F32, kind="ExternalInput").ap()
    dsc = nc.dram_tensor("dsc", [D, 172], F32, kind="ExternalInput").ap()
    dzr = nc.dram_tensor("dzr", [D, 172], F32, kind="ExternalInput").ap()

    yout = nc.dram_tensor("yout", [nfull * TG, D], F32, kind="ExternalOutput").ap()

    dbg = {}

    def dbgt(name, shape):
        if name in debug and name not in dbg:
            dbg[name] = nc.dram_tensor("dbg_" + name, shape, F32, kind="ExternalOutput").ap()
        return dbg.get(name)

    g_scr = nc.dram_tensor("g_scr", [86, 128, 4096], BF16, kind="Internal").ap()
    u_scr = nc.dram_tensor("u_scr", [86, 128, 4096], BF16, kind="Internal").ap()
    d_scr = nc.dram_tensor("d_scr", [8, 86, 128, 512], BF16, kind="Internal").ap()
    o_scr = nc.dram_tensor("o_scr", [64, 128, 4096], BF16, kind="Internal").ap()
    gg_scr = nc.dram_tensor("gg_scr", [64, 128, TG], BF16, kind="Internal").ap()
    x1_scr = nc.dram_tensor("x1_scr", [4, 128, D], F32, kind="Internal").ap()
    ip2 = nc.dram_tensor("ip_scr", [PROJ // 128, 128, 4096], BF16, kind="Internal").ap()

    ident_c = nc.inline_tensor(np.eye(128, dtype=BF16NP), "identc")
    identf_c = nc.inline_tensor(np.eye(128, dtype=np.float32), "identfc")
    ones_row_c = nc.inline_tensor(np.ones((1, 128), dtype=np.float32), "onesrowc")
    ones64_c = nc.inline_tensor(np.ones((128, 64), dtype=np.float32), "ones64c")
    triu_c = nc.inline_tensor(
        np.triu(np.ones((128, 128), dtype=np.float32)).astype(BF16NP), "triuc")
    epj = np.zeros((128, 64), dtype=BF16NP)
    for h in range(128):
        epj[h, h // 2] = 1
    epj01_c = nc.inline_tensor(epj, "epj01c")
    fp_ = np.zeros((128, 128), dtype=BF16NP)
    for h in range(128):
        fp_[h, (h % 2) * 64:(h % 2) * 64 + 64] = 1
    fpair_c = nc.inline_tensor(fp_, "fpairc")

    CVALS = [-1.0, -0.5, 1.0 / 1024.0, 1.0 / 16.0, 16.0, MAGIC, -16.0 * MAGIC,
             -0.46875, EPS, 1.0]
    CIDX = {v: i for i, v in enumerate(CVALS)}
    consts_c = nc.inline_tensor(
        np.tile(np.array(CVALS, dtype=np.float32)[None, :], (128, 1)), "constsc")

    A = nc.alloc_sbuf_tensor
    ident = A("ident", [128, 128], BF16)
    identf = A("identf", [128, 128], F32)
    onesr = A("onesr", [1, 128], F32)
    ones64 = A("ones64", [128, 64], F32)
    triu = A("triu", [128, 128], BF16)
    epj01 = A("epj01", [128, 64], BF16)
    fpair = A("fpair", [128, 128], BF16)
    cpool = A("cpool", [128, len(CVALS)], F32)
    cw = A("cw", [128, 80 * 4], F32)
    cb = A("cb", [128, 80], F32)
    dtb = A("dtb", [128, 1], F32)
    acol = A("acol", [128, 1], F32)
    dxp = A("dxp", [128, 64], F32)
    nwT = A("nwT", [128, 64], F32)
    mcol = A("mcol", [128, ngrp * 4], F32)
    zeros128 = A("zeros128", [128, 128], F32)
    lnst = A("lnst", [128, 8], F32)
    rmsq = A("rmsq", [128, 8], F32)
    s_run = A("s_run", [128, 64 * 128], F32)
    halo = A("halo", [128, 2 * 80 * 3], F32)
    xlnT = A("xlnT", [128, 32 * TG], BF16)

    def cp(v):
        return cpool[:, CIDX[v]:CIDX[v] + 1]

    ARENA = 136 * 1024
    arena = A("arena", [128, ARENA], U8)

    class Rg:
        def __init__(self):
            self.off = 0

        def al(self, nbytes, align=64):
            o = (self.off + align - 1) // align * align
            self.off = o + nbytes
            assert self.off <= ARENA, f"arena overflow {self.off}"
            return o

    def view(off, nelem, dtype, parts=128):
        esz = {F32: 4, BF16: 2, U8: 1}[dtype]
        return arena[0:parts, off:off + nelem * esz].bitcast(dtype)

    ra = Rg()
    qb = view(ra.al(2048 * 4), 2048, F32)
    qh = view(ra.al(2048 * 4), 2048, F32)
    ql = view(ra.al(2048 * 4), 2048, F32)
    qtmp = qb
    qwhi = view(ra.al(2048 * 2), 2048, BF16)
    qwlo = view(ra.al(2048 * 2), 2048, BF16)
    qstg = view(ra.al(4096 * 2), 4096, BF16)
    qsc = view(ra.al(64 * 4), 64, F32)
    qsc16 = view(ra.al(64 * 4), 64, F32)
    qzs = view(ra.al(64 * 4), 64, F32)

    ra2 = Rg()
    qb2 = view(ra2.al(5504 * 4), 5504, F32)
    qh2 = view(ra2.al(5504 * 4), 5504, F32)
    ql2 = view(ra2.al(5504 * 4), 5504, F32)
    qtmp2 = qb2
    qwhi2 = view(ra2.al(2 * 5504 * 2), 2 * 5504, BF16)
    qwlo2 = view(ra2.al(2 * 5504 * 2), 2 * 5504, BF16)
    qstg2 = view(ra2.al(512 * 2), 512, BF16)
    qsc2 = view(ra2.al(172 * 4), 172, F32)
    qsc216 = view(ra2.al(172 * 4), 172, F32)
    qzs2 = view(ra2.al(172 * 4), 172, F32)
    owstg = view(ra2.al(4096 * 2), 4096, BF16)

    rm = Rg()
    wbuf = view(rm.al(2 * 4096 * 2), 2 * 4096, BF16)
    bcm = view(rm.al(32 * 128 * 2), 32 * 128, BF16)
    hsall = view(rm.al(8 * TG * 2), 8 * TG, BF16)
    gsall = view(rm.al(8 * TG * 2), 8 * TG, BF16)
    bsil = view(rm.al(8 * TG * 2), 8 * TG, BF16)
    csil = view(rm.al(8 * TG * 2), 8 * TG, BF16)
    bcpre = view(rm.al((3 + TG) * 4), 3 + TG, F32)
    ggtb = view(rm.al(8 * TG * 2), 8 * TG, BF16)
    dtpre = view(rm.al(TG * 4), TG, F32)
    dtt = view(rm.al(TG * 4), TG, F32)
    adt = view(rm.al(TG * 4), TG, F32)
    acum = view(rm.al(TG * 4), TG, F32)
    acumT = view(rm.al(4 * 128 * 4), 4 * 128, F32)
    d1T = view(rm.al(4 * 128 * 4), 4 * 128, F32)
    dtT = view(rm.al(4 * 128 * 4), 4 * 128, F32)
    dd1T = view(rm.al(4 * 128 * 4), 4 * 128, F32)
    dscr = view(rm.al(TG * 4), TG, F32)
    expacT = view(rm.al(4 * 128 * 4), 4 * 128, F32)
    alast = view(rm.al(4 * 4), 4, F32)
    exal = view(rm.al(4 * 4), 4, F32)
    gexp = view(rm.al(64 * 2), 64, BF16)
    expal = view(rm.al(4 * 64 * 4), 4 * 64, F32)
    hspre = view(rm.al((3 + TG) * 4), 3 + TG, F32)
    convo = view(rm.al(TG * 4), TG, F32)
    convo2 = view(rm.al(TG * 4), TG, F32)
    convs = view(rm.al(TG * 4), TG, F32)
    xch_cm = view(rm.al(128 * 2), 128, BF16)
    xchp = view(rm.al(128 * 2), 128, BF16)
    hsdt = view(rm.al(128 * 2), 128, BF16)
    g1m = view(rm.al(4 * 128 * 2), 4 * 128, BF16)
    mh = view(rm.al(2 * 128 * 2), 2 * 128, BF16)
    relu_s = view(rm.al(2 * 128 * 4), 2 * 128, F32)
    prevT = view(rm.al(128 * 2), 128, BF16)
    yoffc = view(rm.al(128 * 4), 128, F32)
    expacx = view(rm.al(128 * 2), 128, BF16)
    ggf = view(rm.al(1024 * 2), 1024, BF16)
    ggb = view(rm.al(1024 * 2), 1024, BF16)
    acrow = view(rm.al(2048 * 4), 2048, F32, parts=1)
    sub_o = rm.al(max(D * 4 + D * 2, 2 * 4 * 1024 * 2))
    xt = view(sub_o, D, F32)
    xlntm = view(sub_o + D * 4, D, BF16)
    ytm = view(sub_o, 4 * 1024, BF16)
    gatm = view(sub_o + 4 * 1024 * 2, 4 * 1024, BF16)
    mix_end = rm.off

    rf = Rg()
    hsw = view(rf.al(86 * TG * 2), 86 * TG, BF16)
    fwbuf = view(rf.al(2 * 4096 * 2), 2 * 4096, BF16)
    fwbuf2 = view(rf.al(8 * 512 * 2), 8 * 512, BF16)
    fsil = view(rf.al(TG * 2), TG, BF16)
    xsl = view(rf.al(512 * 4), 512, F32)
    outt = view(rf.al(512 * 4), 512, F32)
    obuf = view(rf.al(2 * 512 * 2), 2 * 512, BF16)
    ggl = view(rf.al(2 * 512 * 2), 2 * 512, BF16)
    ffn_end = rf.off

    PS = nc.alloc_psum_tensor
    pA = PS("pA", [128, 512], F32)
    pB = PS("pB", [128, 512], F32)
    pC = PS("pC", [128, 512], F32)
    pD = PS("pD", [128, 512], F32)
    pE = PS("pE", [128, 512], F32)
    pF = PS("pF", [128, 512], F32)
    pT1 = PS("pT1", [128, 1024], BF16)
    pT2 = PS("pT2", [128, 1024], BF16)
    PBANKS = [pA, pB, pC, pD]

    with nc.Block() as block:
        sem = {}
        cnt = {}
        for s in ("ld", "ld2", "cast", "cast2", "ts", "vs", "ss", "st",
                  "ldp", "ldp2", "ipc0", "ipc1", "ipc2", "ipc3"):
            sem[s] = nc.alloc_semaphore(s)
            cnt[s] = 0

        def inc(instr, s, n=1):
            instr.then_inc(sem[s], n)
            cnt[s] += n
            return instr

        last_wait = {}

        def wt(eng, s, val=None):
            v = cnt[s] if val is None else val
            key = (id(eng), s)
            if last_wait.get(key, -1) >= v:
                return
            last_wait[key] = v
            if v > 0:
                eng.wait_ge(sem[s], v)

        ALLSEMS = ("ld", "ld2", "cast", "cast2", "ts", "vs", "ss", "st",
                   "ldp", "ldp2", "ipc0", "ipc1", "ipc2", "ipc3")

        def on(eng_attr, fn, deps=None):
            if deps is not None and debug:
                deps = tuple(deps) + ("cast2", "st")

            def wrapped(eng):
                for d in (ALLSEMS if deps is None else deps):
                    wt(eng, d)
                fn(eng)
            getattr(block, eng_attr)(wrapped)

        def onx(eng_attr, fn, deps):
            if debug:
                deps = tuple(deps) + ("cast2", "st")

            def wrapped(eng):
                for d in deps:
                    wt(eng, d)
                fn(eng)
            getattr(block, eng_attr)(wrapped)

        def tap(name, src_ap, shape):
            t_ = dbgt(name, shape)
            if t_ is None:
                return

            def tp_(gp_, t_=t_, src_ap=src_ap):
                inc(gp_.dma_start(out=t_, in_=src_ap), "cast2", 16)
            on("gpsimd", tp_, deps=("vs", "ss", "ts", "ld", "st"))

        def tap_s(name, src_ap, shape):
            t_ = dbgt(name, shape)
            if t_ is None:
                return

            def tp_(sy_, t_=t_, src_ap=src_ap):
                inc(sy_.dma_start(out=t_, in_=src_ap), "st", 16)
            on("sync", tp_, deps=("vs", "ss", "ts", "ld", "st"))

        def p0_sync(sync):
            for dst, src in ((ident, ident_c), (identf, identf_c), (onesr, ones_row_c),
                             (ones64, ones64_c), (triu, triu_c), (epj01, epj01_c),
                             (fpair, fpair_c), (cpool, consts_c)):
                inc(sync.dma_start(out=dst[:], in_=src.ap()), "ld", 16)
            inc(sync.dma_start(out=cw[:], in_=convw), "ld", 16)
            inc(sync.dma_start(out=cb[:], in_=convb), "ld", 16)
            inc(sync.dma_start(out=dtb[:], in_=dtbias), "ld", 16)
            inc(sync.dma_start(out=acol[:], in_=alog), "ld", 16)
            inc(sync.dma_start(out=dxp[:], in_=dexp), "ld", 16)
            inc(sync.dma_start(out=nwT[:], in_=normwT), "ld", 16)
            inc(sync.dma_start(out=mcol[:], in_=maskc), "ld", 16)
        block.sync(p0_sync)

        def p0_scal(s):
            wt(s, "ld")
            inc(s.activation(out=acol[:], in_=acol[:], func=AF.Exp), "ss", 1)
        block.scalar(p0_scal)

        def p0_vec(v):
            wt(v, "ss")
            v.tensor_scalar(out=acol[:], in0=acol[:], scalar1=-1.0, scalar2=None, op0=ALU.mult)
            v.memset(zeros128[:], 0.0)
            v.memset(s_run[:], 0.0)
            inc(v.memset(halo[:], 0.0), "vs", 1)
        block.vector(p0_vec)

        tap("xin0", xin[0:128, :], [128, D])

        CONV_ORDER = (list(range(64, 73)) + list(range(64)) +
                      list(range(137, 145)) + list(range(73, 137)))
        conv_pos = {t: k for k, t in enumerate(CONV_ORDER)}

        def conv_waits(eng, t):
            k = conv_pos[t]
            for j in range(4):
                need = (k - j) // 4 + 1 if k >= j else 0
                if need:
                    eng.wait_ge(sem[f"ipc{j}"], 16 * need)

        def ipconv(gp_):
            for k, t in enumerate(CONV_ORDER):
                if k >= 4:
                    gp_.wait_ge(sem[f"ipc{k % 4}"], 16 * (k // 4))
                src_ = inprojT[:, 128 * t:128 * t + 128].rearrange(
                    "(ks p) c -> p ks c", p=128)
                dst_ = ip2[t].rearrange("p (ks c) -> p ks c", ks=32)
                inc(gp_.dma_start(out=dst_, in_=src_), f"ipc{k % 4}", 16)
        on("gpsimd", ipconv, deps=())

        def unpack_ops(s, bsrc, hdst, n):
            inc(s.activation(out=hdst[:, 0:n], in_=bsrc[:, 0:n], func=AF.Identity,
                             scale=cp(1.0 / 16.0), bias=cp(-0.46875)), "ss", 1)
            wt(s, "ss")
            inc(s.activation(out=hdst[:, 0:n], in_=hdst[:, 0:n], func=AF.Identity,
                             bias=cp(MAGIC)), "ss", 1)
            wt(s, "ss")
            return s.activation(out=hdst[:, 0:n], in_=hdst[:, 0:n], func=AF.Identity,
                                scale=cp(16.0), bias=cp(-16.0 * MAGIC))

        def dequant_ops(v, bsrc, hsrc, ldst, scb, sc16b, zsb, tmpb, whib, wlob, n, ng):
            w = n // ng
            sc_e = scb[:, 0:ng].unsqueeze(2).broadcast_to((128, ng, w))
            sc16_e = sc16b[:, 0:ng].unsqueeze(2).broadcast_to((128, ng, w))
            zs_e = zsb[:, 0:ng].unsqueeze(2).broadcast_to((128, ng, w))

            def g3(ap_, o=0):
                return ap_[:, o:o + n].rearrange("p (g r) -> p g r", g=ng)
            inc(v.tensor_sub(out=ldst[:, 0:n], in0=bsrc[:, 0:n], in1=hsrc[:, 0:n]),
                "vs", 1)
            wt(v, "vs")
            v.tensor_mul(out=g3(tmpb), in0=g3(hsrc), in1=sc16_e)
            inc(v.tensor_mul(out=g3(ldst), in0=g3(ldst), in1=sc_e), "vs", 1)
            wt(v, "vs")
            v.tensor_add(out=g3(whib), in0=g3(tmpb), in1=zs_e)
            return v.tensor_add(out=g3(wlob), in0=g3(ldst), in1=zs_e)

        def scale_prep(v, scb, sc16b, zsb, ng):
            v.scalar_tensor_tensor(out=zsb[:, 0:ng], in0=zsb[:, 0:ng], scalar=-1.0,
                                   in1=scb[:, 0:ng], op0=ALU.mult, op1=ALU.mult)
            v.tensor_scalar(out=sc16b[:, 0:ng], in0=scb[:, 0:ng], scalar1=1.0 / 16.0,
                            scalar2=None, op0=ALU.mult)

        for (qsrc, scsrc, zrsrc, scr) in (() if skip_phase_a else ((gq, gsc, gzr, g_scr), (uq, usc, uzr, u_scr))):
          with nc.named_scope("phA_gu"):
            for ib in range(86):
                r0 = ib * 128

                def ga(g, qsrc=qsrc, scsrc=scsrc, zrsrc=zrsrc, r0=r0):
                    inc(g.dma_start(out=qb[:], in_=qsrc[r0:r0 + 128, :]), "cast", 16)
                    inc(g.dma_start(out=qsc[:], in_=scsrc[r0:r0 + 128, :]), "cast", 16)
                    inc(g.dma_start(out=qzs[:], in_=zrsrc[r0:r0 + 128, :]), "cast", 16)
                on("gpsimd", ga, deps=("ss", "vs"))

                def sa(s):
                    inc(unpack_ops(s, qb, qh, 2048), "ss", 1)
                on("scalar", sa, deps=("cast", "vs"))

                def va(v):
                    scale_prep(v, qsc, qsc16, qzs, 64)
                    inc(dequant_ops(v, qb, qh, ql, qsc, qsc16, qzs, qtmp, qwhi, qwlo,
                                    2048, 64), "vs", 1)
                on("vector", va, deps=("cast", "ss", "ts"))

                def t_wave(t, wave):
                    for half in range(2):
                        pt = pT1 if half == 0 else pT2
                        for m in range(8):
                            sub = wave * 16 + half * 8 + m
                            src = qwhi if sub % 2 == 0 else qwlo
                            blk = sub // 2
                            mmi = t.transpose(pt[:, 128 * m:128 * m + 128],
                                              src[:, 128 * blk:128 * blk + 128], ident[:])
                    inc(mmi, "ts", 1)

                def v_wave(v, wave):
                    v.tensor_copy(out=qstg[:, 2048 * wave:2048 * wave + 1024], in_=pT1[:])
                    inc(v.tensor_copy(out=qstg[:, 2048 * wave + 1024:2048 * wave + 2048],
                                      in_=pT2[:]), "vs", 1)

                on("tensor", lambda t: t_wave(t, 0), deps=("vs",))
                on("vector", lambda v: v_wave(v, 0), deps=("ts", "st"))
                on("tensor", lambda t: t_wave(t, 1), deps=("vs",))
                on("vector", lambda v: v_wave(v, 1), deps=("ts",))

                def wr(sync, scr=scr, ib=ib):
                    inc(sync.dma_start(out=scr[ib], in_=qstg[:]), "st", 16)
                on("sync", wr, deps=("vs",))

        for dc in ([] if skip_phase_a else range(8)):
          with nc.named_scope("phA_dn"):
            for half2 in range(2):
                for dbi in range(2):
                    db = 2 * half2 + dbi
                    r0 = (4 * dc + db) * 128

                    def ga2(g, r0=r0):
                        inc(g.dma_start(out=qb2[:], in_=dq[r0:r0 + 128, :]), "cast", 16)
                        inc(g.dma_start(out=qsc2[:], in_=dsc[r0:r0 + 128, :]), "cast", 16)
                        inc(g.dma_start(out=qzs2[:], in_=dzr[r0:r0 + 128, :]), "cast", 16)
                    on("gpsimd", ga2, deps=("ss", "vs", "st"))

                    def sa2(s):
                        inc(unpack_ops(s, qb2, qh2, 5504), "ss", 1)
                    on("scalar", sa2, deps=("cast", "vs"))

                    def va2(v, dbi=dbi):
                        scale_prep(v, qsc2, qsc216, qzs2, 172)
                        whiv = qwhi2[:, 5504 * dbi:5504 * dbi + 5504]
                        wlov = qwlo2[:, 5504 * dbi:5504 * dbi + 5504]
                        inc(dequant_ops(v, qb2, qh2, ql2, qsc2, qsc216, qzs2, qtmp2,
                                        whiv, wlov, 5504, 172), "vs", 1)
                    on("vector", va2, deps=("cast", "ss", "ts"))

                for tsub in range(86):
                    m, hi = tsub // 2, (tsub % 2 == 0)

                    def t_tr(t, m=m, hi=hi):
                        for dbi in range(2):
                            base = 5504 * dbi
                            src = qwhi2 if hi else qwlo2
                            mmi = t.transpose(pT1[:, 128 * dbi:128 * dbi + 128],
                                              src[:, base + 128 * m:base + 128 * m + 128],
                                              ident[:])
                        inc(mmi, "ts", 1)
                    on("tensor", t_tr, deps=("vs",))

                    def v_tr(v):
                        inc(v.tensor_copy(out=qstg2[:, 0:256], in_=pT1[:, 0:256]), "vs", 1)
                    on("vector", v_tr, deps=("ts", "st"))

                    def wr2(sync, dc=dc, tsub=tsub, half2=half2):
                        inc(sync.dma_start(
                            out=d_scr[dc, tsub][:, 256 * half2:256 * half2 + 256],
                            in_=qstg2[:, 0:256]), "st", 16)
                    on("sync", wr2, deps=("vs",))

        for kk in ([] if skip_phase_a else range(64)):
          with nc.named_scope("phA_ow"):
            def gow(g, kk=kk):
                inc(g.dma_start(out=owstg[:], in_=outprojT[128 * kk:128 * kk + 128, :]),
                    "cast", 16)
            on("gpsimd", gow, deps=("st",))

            def sow(sync, kk=kk):
                inc(sync.dma_start(out=o_scr[kk], in_=owstg[:]), "st", 16)
            on("sync", sow, deps=("cast",))

        tap("gscr0", g_scr[0], [128, 4096])
        tap("uscr0", u_scr[0], [128, 4096])
        tap("dscr00", d_scr[0, 0], [128, 512])
        tap("oscr0", o_scr[0], [128, 4096])

        def ln_stage(g, from_x1):
            sig = from_x1
            for tt in range(4):
                def ldx(sync, tt=tt, g=g, from_x1=from_x1):
                    if from_x1:
                        inc(sync.dma_start(out=xt[:], in_=x1_scr[tt]), "ld", 16)
                    else:
                        r0 = TG * g + 128 * tt
                        inc(sync.dma_start(out=xt[:], in_=xin[r0:r0 + 128, :]), "ld", 16)
                on("sync", ldx, deps=("vs", "ss", "st"))

                def lns(s):
                    inc(s.activation(out=xlntm[:], in_=xt[:], func=AF.Square,
                                     accum_out=lnst[:, 0:1]), "ss", 1)
                on("scalar", lns, deps=("ld", "ts", "vs"))

                def lnv(v):
                    inc(v.reduce_sum(out=lnst[:, 1:2], in_=xt[:],
                                     axis=mybir.AxisListType.X), "vs", 1)
                    wt(v, "vs")
                    v.tensor_scalar(out=lnst[:, 2:3], in0=lnst[:, 1:2], scalar1=1.0 / D,
                                    scalar2=None, op0=ALU.mult)
                    inc(v.tensor_scalar(out=lnst[:, 3:4], in0=lnst[:, 0:1],
                                        scalar1=1.0 / D, scalar2=None, op0=ALU.mult),
                        "vs", 1)
                    wt(v, "vs")
                    inc(v.tensor_mul(out=lnst[:, 4:5], in0=lnst[:, 2:3],
                                     in1=lnst[:, 2:3]), "vs", 1)
                    wt(v, "vs")
                    inc(v.tensor_sub(out=lnst[:, 5:6], in0=lnst[:, 3:4], in1=lnst[:, 4:5]),
                        "vs", 1)
                on("vector", lnv, deps=("ld", "ss"))

                def lns2(s):
                    inc(s.activation(out=lnst[:, 6:7], in_=lnst[:, 5:6], func=AF.Ln,
                                     bias=cp(EPS)), "ss", 1)
                    wt(s, "ss")
                    inc(s.activation(out=lnst[:, 7:8], in_=lnst[:, 6:7], func=AF.Exp,
                                     scale=cp(-0.5)), "ss", 1)
                on("scalar", lns2, deps=("vs",))

                def lnv2(v):
                    inc(v.tensor_scalar(out=xlntm[:], in0=xt[:], scalar1=lnst[:, 2:3],
                                        scalar2=lnst[:, 7:8], op0=ALU.subtract,
                                        op1=ALU.mult), "vs", 1)
                on("vector", lnv2, deps=("ss",))

                for wave in range(2):
                    def t_tw(t, wave=wave, sig=sig):
                        for half in range(2):
                            pt = pT1 if half == 0 else pT2
                            for m in range(8):
                                ks = wave * 16 + half * 8 + m
                                if not sig:
                                    src = xlntm[:, 128 * ks:128 * ks + 128]
                                else:
                                    mm2, par = ks // 2, ks % 2
                                    src = xlntm[:, 256 * mm2 + par:256 * mm2 + par + 255:2]
                                mmi = t.transpose(pt[:, 128 * m:128 * m + 128], src, ident[:])
                        inc(mmi, "ts", 1)
                    on("tensor", t_tw, deps=("vs",))

                    def v_tw(v, tt=tt, wave=wave):
                        for half in range(2):
                            pt = pT1 if half == 0 else pT2
                            ks0 = wave * 16 + half * 8
                            dsts = xlnT[:, TG * ks0:TG * (ks0 + 8)].rearrange(
                                "p (k r) -> p k r", k=8)[:, :, 128 * tt:128 * tt + 128]
                            inc(v.tensor_copy(
                                out=dsts, in_=pt[:].rearrange("p (k r) -> p k r", k=8)),
                                "vs", 1)
                    on("vector", v_tw, deps=("ts",))

        def proj_stage(blocks):
            base_cast = {0: cnt["ldp"], 1: cnt["ldp2"]}
            base_ts = cnt["ts"]
            base_vs = cnt["vs"]
            base_ss = cnt["ss"]
            ev_mark = []
            vv = ss_ = 0
            for (off, kind, dest) in blocks:
                if kind == "silu":
                    ss_ += 1
                    ev_mark.append(("ss", ss_))
                else:
                    vv += 1
                    ev_mark.append(("vs", vv))

            def gp(gps):
                for i, (off, kind, dest) in enumerate(blocks):
                    if i >= 2:
                        gps.wait_ge(sem["ts"], base_ts + (i - 2) + 1)
                    conv_waits(gps, off // 128)
                    inc(gps.dma_start(
                        out=wbuf[:, 4096 * (i % 2):4096 * (i % 2) + 4096],
                        in_=ip2[off // 128]),
                        "ldp" if i % 2 == 0 else "ldp2", 16)
            on("sync", gp, deps=("ts", "vs", "ss"))

            def tp(t):
                for i, (off, kind, dest) in enumerate(blocks):
                    cs_ = "ldp" if i % 2 == 0 else "ldp2"
                    t.wait_ge(sem[cs_], base_cast[i % 2] + 16 * (i // 2 + 1))
                    bank = PBANKS[i % 2]
                    if i >= 2:
                        s_, v_ = ev_mark[i - 2]
                        t.wait_ge(sem[s_], (base_ss if s_ == "ss" else base_vs) + v_)
                    w = wbuf[:, 4096 * (i % 2):4096 * (i % 2) + 4096]
                    for ks in range(32):
                        mmi = t.matmul(bank[:], w[:, 128 * ks:128 * ks + 128],
                                       xlnT[:, TG * ks:TG * ks + TG],
                                       start=(ks == 0), stop=(ks == 31))
                    inc(mmi, "ts", 1)
            onx("tensor", tp, ())

            def se(s):
                for i, (off, kind, dest) in enumerate(blocks):
                    if kind != "silu":
                        continue
                    s.wait_ge(sem["ts"], base_ts + i + 1)
                    inc(s.activation(out=dest, in_=PBANKS[i % 2][:], func=AF.Silu), "ss", 1)
            onx("scalar", se, ("ts",))

            def vep(v):
                for i, (off, kind, dest) in enumerate(blocks):
                    if kind == "silu":
                        continue
                    v.wait_ge(sem["ts"], base_ts + i + 1)
                    inc(v.tensor_copy(out=dest, in_=PBANKS[i % 2][:]), "vs", 1)
            onx("vector", vep, ("ss",))

        def conv_into(v, tile_idx, pre, par):
            rd = 240 * par + 3 * tile_idx
            sv = 240 * (1 - par) + 3 * tile_idx
            v.tensor_copy(out=pre[:, 0:3], in_=halo[:, rd:rd + 3])
            inc(v.tensor_copy(out=halo[:, sv:sv + 3], in_=pre[:, TG:TG + 3]), "vs", 1)
            wt(v, "vs")
            v.tensor_scalar(out=convo[:], in0=pre[:, 0:TG],
                            scalar1=cw[:, 4 * tile_idx:4 * tile_idx + 1],
                            scalar2=cb[:, tile_idx:tile_idx + 1], op0=ALU.mult, op1=ALU.add)
            inc(v.tensor_scalar(out=convo2[:], in0=pre[:, 1:1 + TG],
                                scalar1=cw[:, 4 * tile_idx + 1:4 * tile_idx + 2],
                                scalar2=None, op0=ALU.mult), "vs", 1)
            wt(v, "vs")
            v.scalar_tensor_tensor(out=convo[:], in0=pre[:, 2:2 + TG],
                                   scalar=cw[:, 4 * tile_idx + 2:4 * tile_idx + 3],
                                   in1=convo[:], op0=ALU.mult, op1=ALU.add)
            inc(v.scalar_tensor_tensor(out=convo2[:], in0=pre[:, 3:3 + TG],
                                       scalar=cw[:, 4 * tile_idx + 3:4 * tile_idx + 4],
                                       in1=convo2[:], op0=ALU.mult, op1=ALU.add), "vs", 1)
            wt(v, "vs")
            return v.tensor_add(out=convs[:], in0=convo[:], in1=convo2[:])

        _ck = [0]

        def ckpt():
            _ck[0] += 1
            if limit is not None and _ck[0] >= limit:
                raise _StopEmit()

        if only_ln:
            ln_stage(0, False)
            def fin0(sync):
                wt(sync, "st")
            on("sync", fin0)
            return nc, dbg
        def _emit_groups():
          for g in range(ngrp):
            full = g >= npref
            with nc.named_scope(f"g{g}_ln"):
                ln_stage(g, False)
            ckpt()
            if g == npref:
                tap("srun6", s_run[:], [128, 64 * 128])
                tap("xlnT6", xlnT[:], [128, 32 * TG])

            sec = [(B0 + 128 * gb, 64 + gb, bsil, gb) for gb in range(8)]
            if full:
                sec += [(C0 + 128 * gb, 72 + gb, csil, gb) for gb in range(8)]
            scope_bc = nc.enter_named_scope(f"g{g}_bc", False)
            for (off, tile, dstbuf, gb) in sec:
                proj_stage([(off, "f32", bcpre[:, 3:3 + TG])])

                def cvb(v, tile=tile, g=g):
                    inc(conv_into(v, tile, bcpre, g % 2), "vs", 1)
                on("vector", cvb, deps=("vs", "ss"))

                def slb(s, dstbuf=dstbuf, gb=gb):
                    inc(s.activation(out=dstbuf[:, TG * gb:TG * gb + TG],
                                     in_=convs[:], func=AF.Silu), "ss", 1)
                on("scalar", slb, deps=("vs",))

            nc.leave_named_scope(f"g{g}_bc", scope_bc[0], False)
            ckpt()
            scope_dt = nc.enter_named_scope(f"g{g}_dt", False)
            proj_stage([(DT0, "f32", dtpre[:])])

            def dt_vec(v):
                inc(v.tensor_scalar(out=dtt[:], in0=dtpre[:], scalar1=dtb[:, 0:1],
                                    scalar2=None, op0=ALU.add), "vs", 1)
                wt(v, "vs")
                inc(v.tensor_relu(out=dtpre[:], in_=dtt[:]), "vs", 1)
            on("vector", dt_vec, deps=("vs", "ts"))

            def dt_scal(s):
                inc(s.activation(out=adt[:], in_=dtt[:], func=AF.Abs), "ss", 1)
                wt(s, "ss")
                inc(s.activation(out=adt[:], in_=adt[:], func=AF.Exp, scale=cp(-1.0)),
                    "ss", 1)
                wt(s, "ss")
                inc(s.activation(out=adt[:], in_=adt[:], func=AF.Ln, bias=cp(1.0)), "ss", 1)
            on("scalar", dt_scal, deps=("vs",))

            def dt_vec2(v, g=g):
                inc(v.tensor_add(out=dtt[:], in0=dtpre[:], in1=adt[:]), "vs", 1)
                wt(v, "vs")
                inc(v.tensor_scalar(out=dtt[:], in0=dtt[:], scalar1=100.0, scalar2=None,
                                    op0=ALU.min), "vs", 1)
                wt(v, "vs")
                for c in range(4):
                    slot = 4 * g + c
                    i_ = v.tensor_scalar(out=dtt[:, 128 * c:128 * c + 128],
                                         in0=dtt[:, 128 * c:128 * c + 128],
                                         scalar1=mcol[:, slot:slot + 1], scalar2=None,
                                         op0=ALU.mult)
                inc(i_, "vs", 1)
                wt(v, "vs")
                inc(v.tensor_scalar(out=adt[:], in0=dtt[:], scalar1=acol[:, 0:1],
                                    scalar2=None, op0=ALU.mult), "vs", 1)
                wt(v, "vs")
                for c in range(4):
                    cs = slice(128 * c, 128 * c + 128)
                    i_ = v.tensor_tensor_scan(out=acum[:, cs], data0=adt[:, cs],
                                              data1=zeros128[:], initial=0.0,
                                              op0=ALU.add, op1=ALU.add)
                inc(i_, "vs", 1)
                wt(v, "vs")
                for c in range(4):
                    i_ = v.tensor_copy(out=alast[:, c:c + 1],
                                       in_=acum[:, 128 * c + 127:128 * c + 128])
                inc(i_, "vs", 1)
                wt(v, "vs")
                for c in range(4):
                    cs = slice(128 * c, 128 * c + 128)
                    i_ = v.tensor_scalar(out=dscr[:, cs], in0=acum[:, cs],
                                         scalar1=alast[:, c:c + 1], scalar2=None,
                                         op0=ALU.subtract)
                inc(i_, "vs", 1)
            on("vector", dt_vec2, deps=("ss", "vs", "ld", "ts"))

            def dt_scal2(s):
                s.activation(out=dscr[:], in_=dscr[:], func=AF.Exp, scale=cp(-1.0))
                inc(s.activation(out=exal[:], in_=alast[:], func=AF.Exp), "ss", 1)
            on("scalar", dt_scal2, deps=("vs", "ts"))

            for c in range(4):
                def gv(v, c=c):
                    inc(v.tensor_scalar(out=gexp[:], in0=epj01[:], scalar1=exal[:, c:c + 1],
                                        scalar2=None, op0=ALU.mult), "vs", 1)
                on("vector", gv, deps=("ss", "ts"))

                def gt(t, c=c, full=full):
                    cs = slice(128 * c, 128 * c + 128)
                    t.matmul(pF[:, 0:64], fpair[:], gexp[:], start=True, stop=True)
                    t.transpose(pF[:, 128:256], dscr[:, cs], identf[:])
                    mmi = t.transpose(pF[:, 256:384], dtt[:, cs], identf[:])
                    if full:
                        mmi = t.transpose(pF[:, 384:512], acum[:, cs], identf[:])
                    inc(mmi, "ts", 1)
                on("tensor", gt, deps=("vs", "ss"))

                def gv2(v, c=c, full=full):
                    v.tensor_copy(out=expal[:, 64 * c:64 * c + 64], in_=pF[:, 0:64])
                    v.tensor_copy(out=d1T[:, 128 * c:128 * c + 128], in_=pF[:, 128:256])
                    i2 = v.tensor_copy(out=dtT[:, 128 * c:128 * c + 128], in_=pF[:, 256:384])
                    if full:
                        i2 = v.tensor_copy(out=acumT[:, 128 * c:128 * c + 128],
                                           in_=pF[:, 384:512])
                    inc(i2, "vs", 1)
                on("vector", gv2, deps=("ts",))

            def ddv(v):
                inc(v.tensor_mul(out=dd1T[:], in0=dtT[:], in1=d1T[:]), "vs", 1)
            on("vector", ddv, deps=("vs",))

            if full:
                def exa(s):
                    inc(s.activation(out=expacT[:], in_=acumT[:], func=AF.Exp), "ss", 1)
                on("scalar", exa, deps=("vs",))

            if g == npref:
                tap("alast6", alast[:], [128, 4])
                tap("exal6", exal[:], [128, 4])
                tap("adt6", adt[:], [128, TG])
                tap("dtpre6", dtpre[:], [128, TG])
                tap("dscrx6", dscr[:], [128, TG])
                tap("bsil6", bsil[:], [128, 8 * TG])
                tap("csil6", csil[:], [128, 8 * TG])
                tap("dtt6", dtt[:], [128, TG])
                tap("acum6", acum[:], [128, TG])
                tap("d1T6", d1T[:], [128, 4 * 128])
                tap("dtT6", dtT[:], [128, 4 * 128])
                tap("expal6", expal[:], [128, 4 * 64])
                tap("acumT6", acumT[:], [128, 4 * 128])
                tap("expacT6", expacT[:], [128, 4 * 128])

            for wave in range(2):
                def bt(t, wave=wave):
                    for half in range(2):
                        pt = pT1 if half == 0 else pT2
                        for m in range(8):
                            idx = wave * 16 + half * 8 + m
                            c, gb = idx // 8, idx % 8
                            mmi = t.transpose(
                                pt[:, 128 * m:128 * m + 128],
                                bsil[:, TG * gb + 128 * c:TG * gb + 128 * c + 128],
                                ident[:])
                    inc(mmi, "ts", 1)
                on("tensor", bt, deps=("vs", "ss"))

                def bv(v, wave=wave):
                    for half in range(2):
                        pt = pT1 if half == 0 else pT2
                        idx0 = wave * 16 + half * 8
                        dsts = bcm[:, 128 * idx0:128 * (idx0 + 8)].rearrange(
                            "p (k r) -> p k r", k=8)
                        inc(v.tensor_copy(out=dsts,
                                          in_=pt[:].rearrange("p (k r) -> p k r", k=8)),
                            "vs", 1)
                on("vector", bv, deps=("ts",))

            tap(f"sr_a{g}", s_run[:], [128, 64 * 128])
            tap(f"dd_{g}", dd1T[:], [128, 4 * 128])
            tap(f"ex_{g}", expal[:], [128, 4 * 64])
            tap(f"hs_{g}", hsall[:], [128, 8 * TG])

            nc.leave_named_scope(f"g{g}_dt", scope_dt[0], False)
            ckpt()
            scope_hb = nc.enter_named_scope(f"g{g}_hb", False)
            for hb in range(8):
                ckpt()
                for j8 in range(8):
                    jj = 8 * hb + j8
                    blocks = []
                    if full:
                        blocks.append((GATE0 + 128 * jj, "silu",
                                       gsall[:, TG * j8:TG * j8 + TG]))
                    blocks.append((HS0 + 128 * jj, "f32", hspre[:, 3:3 + TG]))
                    proj_stage(blocks)

                    def cvh(v, jj=jj, g=g):
                        inc(conv_into(v, jj, hspre, g % 2), "vs", 1)
                    on("vector", cvh, deps=("vs", "ss"))

                    def svh(s, j8=j8):
                        inc(s.activation(out=hsall[:, TG * j8:TG * j8 + TG],
                                         in_=convs[:], func=AF.Silu), "ss", 1)
                    on("scalar", svh, deps=("vs",))

                if g == npref and hb == 0:
                    tap("hsall6", hsall[:], [128, 8 * TG])
                    tap("gsall6", gsall[:], [128, 8 * TG])
                if full:
                    for c in range(4):
                        def g1tc(t, hb=hb, c=c):
                            cs = slice(TG * hb + 128 * c, TG * hb + 128 * c + 128)
                            inc(t.matmul(pE[:, 0:128], bsil[:, cs], csil[:, cs],
                                         start=True, stop=True), "ts", 1)
                        on("tensor", g1tc, deps=("vs", "ss"))

                        def g1vc(v, c=c):
                            inc(v.tensor_mul(out=g1m[:, 128 * c:128 * c + 128],
                                             in0=pE[:, 0:128], in1=triu[:]), "vs", 1)
                        on("vector", g1vc, deps=("ts",))
                    if full and g == npref and hb == 0:
                        tap("g1m0", g1m[:], [128, 512])

                for c in range(4):
                    if full:
                        def acg(sync, hb=hb, c=c):
                            cs = slice(128 * c, 128 * c + 128)
                            inc(sync.dma_start(out=acrow[0:1, :],
                                               in_=acum[16 * hb:16 * hb + 16, cs]),
                                "ld", 16)
                        on("sync", acg, deps=("vs", "ts"))

                    for j8 in range(8):
                        jj = 8 * hb + j8

                        def trh(t, j8=j8, full=full, c=c):
                            mmi = t.transpose(
                                pT2[:, 0:128],
                                hsall[:, TG * j8 + 128 * c:TG * j8 + 128 * c + 128],
                                ident[:])
                            if full:
                                mmi = t.transpose(
                                    pT2[:, 128:256],
                                    gsall[:, TG * j8 + 128 * c:TG * j8 + 128 * c + 128],
                                    ident[:])
                            inc(mmi, "ts", 1)
                        on("tensor", trh, deps=("vs", "ss"))

                        def hcm(v, jj=jj, j8=j8, c=c, full=full):
                            for hh in range(2):
                                h = 2 * jj + hh
                                band = slice(64 * hh, 64 * hh + 64)
                                i1 = v.tensor_scalar(
                                    out=xchp[:, band], in0=pT2[:, band],
                                    scalar1=dd1T[:, 128 * c + h:128 * c + h + 1],
                                    scalar2=None, op0=ALU.mult)
                            if full:
                                for hh in range(2):
                                    h = 2 * jj + hh
                                    band = slice(64 * hh, 64 * hh + 64)
                                    v.tensor_scalar(out=xch_cm[:, band], in0=pT2[:, band],
                                                    scalar1=dtT[:, 128 * c + h:128 * c + h + 1],
                                                    scalar2=None, op0=ALU.mult)
                                dg = gatm[:].rearrange("p (k r) -> p k r", k=4)[
                                    :, c:c + 1, 128 * j8:128 * j8 + 128]
                                v.tensor_copy(out=dg, in_=pT2[:, 128:256].unsqueeze(1))
                                i1 = v.tensor_scalar(
                                    out=hsdt[:],
                                    in0=hsall[:, TG * j8 + 128 * c:TG * j8 + 128 * c + 128],
                                    scalar1=dxp[:, jj:jj + 1], scalar2=None, op0=ALU.mult)
                            inc(i1, "vs", 1)
                        on("vector", hcm, deps=("ts",))
                        if g == npref and hb == 0 and c == 0 and j8 == 0:
                            tap("xchp0", xchp[:], [128, 128])
                            tap("xcm0", xch_cm[:], [128, 128])
                            tap("hsdt0", hsdt[:], [128, 128])

                        if full:
                            def sst(t, jj=jj, j8=j8, c=c, hb=hb):
                                t.transpose(pF[:, 0:128],
                                            s_run[:, 128 * jj:128 * jj + 128], identf[:])
                                for hh in range(2):
                                    lh = 2 * j8 + hh
                                    t.matmul(pF[:, 128 + 128 * hh:256 + 128 * hh], onesr[:],
                                             acrow[0:1, 128 * lh:128 * lh + 128],
                                             start=True, stop=True)
                                inc(t.matmul(pE[:, 0:128], xchp[:],
                                             bcm[:, 128 * (8 * c + hb):128 * (8 * c + hb) + 128],
                                             start=True, stop=True), "ts", 1)
                            on("tensor", sst, deps=("vs", "ss", "ld"))
                            if g == npref and hb == 0 and c == 0 and j8 == 0:
                                def dcp1(v):
                                    v.tensor_copy(out=ggf[:, 0:384], in_=pF[:, 0:384])
                                    inc(v.tensor_copy(out=ggf[:, 384:512],
                                                      in_=pE[:, 0:128]), "vs", 1)
                                on("vector", dcp1)
                                tap("pFE0", ggf[:, 0:512], [128, 512])
                                tap("acrow0", acrow[0:1, :], [1, 2048])

                            def ssa(s, jj=jj, c=c):
                                for hh in range(2):
                                    h = 2 * jj + hh
                                    ii = s.activation(
                                        out=relu_s[:, 128 * hh:128 * hh + 128],
                                        in_=pF[:, 128 + 128 * hh:256 + 128 * hh],
                                        func=AF.Relu, scale=cp(-1.0),
                                        bias=acumT[:, 128 * c + h:128 * c + h + 1])
                                inc(ii, "ss", 1)
                                wt(s, "ss")
                                for hh in range(2):
                                    ii = s.activation(out=relu_s[:, 128 * hh:128 * hh + 128],
                                                      in_=relu_s[:, 128 * hh:128 * hh + 128],
                                                      func=AF.Exp, scale=cp(-1.0))
                                inc(ii, "ss", 1)
                            on("scalar", ssa, deps=("ts",))
                            if g == npref and hb == 0 and c == 0 and j8 == 0:
                                tap("relu0", relu_s[:], [128, 256])

                            def ssv(v, jj=jj, c=c):
                                v.tensor_copy(out=prevT[:], in_=pF[:, 0:128])
                                for hh in range(2):
                                    h = 2 * jj + hh
                                    band = slice(64 * hh, 64 * hh + 64)
                                    v.tensor_scalar(
                                        out=expacx[:, band], in0=ones64[:],
                                        scalar1=expacT[:, 128 * c + h:128 * c + h + 1],
                                        scalar2=None, op0=ALU.mult)
                                for hh in range(2):
                                    ii = v.tensor_mul(out=mh[:, 128 * hh:128 * hh + 128],
                                                      in0=relu_s[:, 128 * hh:128 * hh + 128],
                                                      in1=g1m[:, 128 * c:128 * c + 128])
                                inc(ii, "vs", 1)
                            on("vector", ssv, deps=("ss",))
                            if g == npref and hb == 0 and c == 0 and j8 == 0:
                                tap("mh0", mh[:], [128, 256])
                                tap("prevT0", prevT[:], [128, 128])
                                tap("expacx0", expacx[:], [128, 128])

                            def sst2(t, jj=jj, c=c, hb=hb):
                                for hh in range(2):
                                    t.matmul(pC[:, 64 * hh:64 * hh + 64],
                                             mh[:, 128 * hh:128 * hh + 128],
                                             xch_cm[:, 64 * hh:64 * hh + 64],
                                             start=(hh == 0), stop=False)
                                t.matmul(pC[:, 0:128], hsdt[:], ident[:],
                                         start=False, stop=True)
                                inc(t.matmul(pD[:, 0:128],
                                             csil[:, TG * hb + 128 * c:TG * hb + 128 * c + 128],
                                             prevT[:], start=True, stop=True), "ts", 1)
                            on("tensor", sst2, deps=("vs",))
                            if g == npref and hb == 0 and c == 0 and j8 == 0:
                                def dcp2(v):
                                    v.tensor_copy(out=ggf[:, 0:128], in_=pC[:, 0:128])
                                    inc(v.tensor_copy(out=ggf[:, 128:256],
                                                      in_=pD[:, 0:128]), "vs", 1)
                                on("vector", dcp2)
                                tap("pCD0", ggf[:, 0:256], [128, 256])

                            def ssv2(v, jj=jj, c=c, j8=j8):
                                v.tensor_mul(out=yoffc[:], in0=pD[:, 0:128], in1=expacx[:])
                                inc(v.scalar_tensor_tensor(
                                    out=s_run[:, 128 * jj:128 * jj + 128],
                                    in0=s_run[:, 128 * jj:128 * jj + 128],
                                    scalar=expal[:, 64 * c + jj:64 * c + jj + 1],
                                    in1=pE[:, 0:128], op0=ALU.mult, op1=ALU.add), "vs", 1)
                                wt(v, "vs")
                                inc(v.tensor_add(
                                    out=ytm[:, 1024 * c + 128 * j8:1024 * c + 128 * j8 + 128],
                                    in0=pC[:, 0:128], in1=yoffc[:]), "vs", 1)
                            on("vector", ssv2, deps=("ts",))
                            if g == npref and hb == 0 and c == 0 and j8 == 0:
                                tap("yoffc0", yoffc[:], [128, 128])
                        else:
                            def sstp(t, c=c, hb=hb):
                                inc(t.matmul(pE[:, 0:128], xchp[:],
                                             bcm[:, 128 * (8 * c + hb):128 * (8 * c + hb) + 128],
                                             start=True, stop=True), "ts", 1)
                            on("tensor", sstp, deps=("vs",))

                            def ssvp(v, jj=jj, c=c):
                                inc(v.scalar_tensor_tensor(
                                    out=s_run[:, 128 * jj:128 * jj + 128],
                                    in0=s_run[:, 128 * jj:128 * jj + 128],
                                    scalar=expal[:, 64 * c + jj:64 * c + jj + 1],
                                    in1=pE[:, 0:128], op0=ALU.mult, op1=ALU.add), "vs", 1)
                            on("vector", ssvp, deps=("ts",))

                    if full:
                        def ggv(v, c=c):
                            inc(v.tensor_mul(out=ggf[:], in0=ytm[:, 1024 * c:1024 * c + 1024],
                                             in1=gatm[:, 1024 * c:1024 * c + 1024]), "vs", 1)
                        on("vector", ggv, deps=("vs", "ss"))

                        def ggs(s, c=c):
                            inc(s.activation(out=ytm[:, 1024 * c:1024 * c + 1024],
                                             in_=ggf[:], func=AF.Square,
                                             accum_out=rmsq[:, c:c + 1]), "ss", 1)
                            wt(s, "ss")
                            inc(s.activation(out=rmsq[:, 4 + c:5 + c], in_=rmsq[:, c:c + 1],
                                             func=AF.Ln, scale=cp(1.0 / 1024.0),
                                             bias=cp(EPS)), "ss", 1)
                            wt(s, "ss")
                            inc(s.activation(out=rmsq[:, 4 + c:5 + c],
                                             in_=rmsq[:, 4 + c:5 + c],
                                             func=AF.Exp, scale=cp(-0.5)), "ss", 1)
                        on("scalar", ggs, deps=("vs",))

                        def ggv2(v, c=c):
                            inc(v.tensor_scalar(out=ggb[:], in0=ggf[:],
                                                scalar1=rmsq[:, 4 + c:5 + c], scalar2=None,
                                                op0=ALU.mult), "vs", 1)
                        on("vector", ggv2, deps=("ss",))

                        def ggt(t):
                            for j8 in range(8):
                                mmi = t.transpose(pT1[:, 128 * j8:128 * j8 + 128],
                                                  ggb[:, 128 * j8:128 * j8 + 128], ident[:])
                            inc(mmi, "ts", 1)
                        on("tensor", ggt, deps=("vs",))

                        def ggv3(v, c=c, hb=hb):
                            for j8 in range(8):
                                jj = 8 * hb + j8
                                ii = v.tensor_scalar(
                                    out=ggtb[:, TG * j8 + 128 * c:TG * j8 + 128 * c + 128],
                                    in0=pT1[:, 128 * j8:128 * j8 + 128],
                                    scalar1=nwT[:, jj:jj + 1], scalar2=None, op0=ALU.mult)
                            inc(ii, "vs", 1)
                        on("vector", ggv3, deps=("ts", "st"))

                if g == npref and hb == 0:
                    tap("ytm6", ytm[:], [128, 4 * 1024])
                    tap("gatm6", gatm[:], [128, 4 * 1024])
                    tap("ggtb6", ggtb[:], [128, 8 * TG])
                if full:
                    def ggw(sync, hb=hb):
                        inc(sync.dma_start(
                            out=gg_scr[8 * hb:8 * hb + 8].rearrange("k p c -> p k c"),
                            in_=ggtb[:].rearrange("p (k c) -> p k c", k=8)), "st", 16)
                    on("sync", ggw, deps=("vs",))

            nc.leave_named_scope(f"g{g}_hb", scope_hb[0], False)
            tap(f"sr_b{g}", s_run[:], [128, 64 * 128])
            ckpt()

            if full:
                scope_op = nc.enter_named_scope(f"g{g}_op", False)
                for db in range(8):
                    base_ld = {0: cnt["ld"], 1: cnt["ld2"]}
                    base_ts = cnt["ts"]

                    def owl(sync, db=db, base_ts=base_ts):
                        for kk in range(64):
                            if kk >= 2:
                                sync.wait_ge(sem["ts"], base_ts + (kk - 2) + 1)
                            ls_ = "ld" if kk % 2 == 0 else "ld2"
                            inc(sync.dma_start(
                                out=obuf[:, 512 * (kk % 2):512 * (kk % 2) + 512],
                                in_=o_scr[kk][:, 512 * db:512 * db + 512]), ls_, 16)
                            inc(sync.dma_start(
                                out=ggl[:, 512 * (kk % 2):512 * (kk % 2) + 512],
                                in_=gg_scr[kk]), ls_, 16)
                    on("sync", owl, deps=("ts", "st"))

                    def owt(t, base_ld=base_ld):
                        for kk in range(64):
                            ls_ = "ld" if kk % 2 == 0 else "ld2"
                            t.wait_ge(sem[ls_], base_ld[kk % 2] + 32 * (kk // 2 + 1))
                            for tb in range(4):
                                mmi = t.matmul(
                                    PBANKS[tb][:],
                                    ggl[:, 512 * (kk % 2) + 128 * tb:512 * (kk % 2) + 128 * tb + 128],
                                    obuf[:, 512 * (kk % 2):512 * (kk % 2) + 512],
                                    start=(kk == 0), stop=(kk == 63))
                            inc(mmi, "ts", 1)
                    onx("tensor", owt, ("vs",))

                    for tb in range(4):
                        def exl(sync, g=g, db=db, tb=tb):
                            r0 = TG * g + 128 * tb
                            inc(sync.dma_start(out=xsl[:],
                                               in_=xin[r0:r0 + 128, 512 * db:512 * db + 512]),
                                "ld", 16)
                        on("sync", exl, deps=("ts",))

                        def exv(v, tb=tb):
                            inc(v.scalar_tensor_tensor(out=outt[:], in0=PBANKS[tb][:],
                                                       scalar=1.0, in1=xsl[:],
                                                       op0=ALU.mult, op1=ALU.add), "vs", 1)
                        on("vector", exv, deps=("ts", "ld", "st"))

                        def exw(sync, db=db, tb=tb):
                            inc(sync.dma_start(out=x1_scr[tb][:, 512 * db:512 * db + 512],
                                               in_=outt[:]), "st", 16)
                        on("sync", exw, deps=("vs",))

                if g == npref:
                    for tt in range(4):
                        tap(f"x1_{tt}", x1_scr[tt], [128, D])

                nc.leave_named_scope(f"g{g}_op", scope_op[0], False)
                with nc.named_scope(f"g{g}_ln2"):
                    ln_stage(g, True)
                if g == npref:
                    tap("xln2T6", xlnT[:], [128, 32 * TG])

                scope_fg = nc.enter_named_scope(f"g{g}_fgu", False)
                for ib in range(86):
                    def fl(sync, ib=ib):
                        inc(sync.dma_start(out=fwbuf[:, 0:4096], in_=g_scr[ib]), "ld", 16)
                        inc(sync.dma_start(out=fwbuf[:, 4096:8192], in_=u_scr[ib]), "ld", 16)
                    on("sync", fl, deps=("ts", "vs"))

                    def ft(t):
                        for ks in range(32):
                            t.matmul(pA[:], fwbuf[:, 128 * ks:128 * ks + 128],
                                     xlnT[:, TG * ks:TG * ks + TG],
                                     start=(ks == 0), stop=(ks == 31))
                        for ks in range(32):
                            mmi = t.matmul(pB[:], fwbuf[:, 4096 + 128 * ks:4096 + 128 * ks + 128],
                                           xlnT[:, TG * ks:TG * ks + TG],
                                           start=(ks == 0), stop=(ks == 31))
                        inc(mmi, "ts", 1)
                    on("tensor", ft, deps=("ld",))

                    def fs(s):
                        inc(s.activation(out=fsil[:], in_=pA[:], func=AF.Silu), "ss", 1)
                    on("scalar", fs, deps=("ts",))

                    def fv(v, ib=ib):
                        inc(v.tensor_mul(out=hsw[:, TG * ib:TG * ib + TG], in0=pB[:],
                                         in1=fsil[:]), "vs", 1)
                    on("vector", fv, deps=("ss",))

                if g == npref:
                    tap("hsw6", hsw[:, 0:4 * TG], [128, 4 * TG])

                nc.leave_named_scope(f"g{g}_fgu", scope_fg[0], False)
                scope_fd = nc.enter_named_scope(f"g{g}_fdn", False)
                for dc in range(8):
                    for tb8 in range(11):
                        nt = 8 if tb8 < 10 else 6
                        t0_ = 8 * tb8

                        def fdl(sync, dc=dc, t0_=t0_, nt=nt):
                            inc(sync.dma_start(
                                out=fwbuf2[:, 0:512 * nt].rearrange("p (t c) -> p t c", t=nt),
                                in_=d_scr[dc, t0_:t0_ + nt].rearrange("t p c -> p t c")),
                                "ld", 16)
                        on("sync", fdl, deps=("ts",))

                        def fdt(t, t0_=t0_, nt=nt):
                            for ti in range(nt):
                                tsub = t0_ + ti
                                for tb in range(4):
                                    mmi = t.matmul(
                                        PBANKS[tb][:],
                                        hsw[:, TG * tsub + 128 * tb:TG * tsub + 128 * tb + 128],
                                        fwbuf2[:, 512 * ti:512 * ti + 512],
                                        start=(tsub == 0), stop=(tsub == 85))
                            inc(mmi, "ts", 1)
                        on("tensor", fdt, deps=("ld", "ss", "vs"))

                    for tb in range(4):
                        def fxl(sync, dc=dc, tb=tb):
                            inc(sync.dma_start(out=xsl[:],
                                               in_=x1_scr[tb][:, 512 * dc:512 * dc + 512]),
                                "ld", 16)
                        on("sync", fxl, deps=("ts",))

                        def fxv(v, tb=tb):
                            inc(v.scalar_tensor_tensor(out=outt[:], in0=PBANKS[tb][:],
                                                       scalar=1.0, in1=xsl[:],
                                                       op0=ALU.mult, op1=ALU.add), "vs", 1)
                        on("vector", fxv, deps=("ts", "ld", "st"))

                        def fxw(sync, g=g, dc=dc, tb=tb):
                            r0 = TG * (g - npref) + 128 * tb
                            inc(sync.dma_start(out=yout[r0:r0 + 128, 512 * dc:512 * dc + 512],
                                               in_=outt[:]), "st", 16)
                        on("sync", fxw, deps=("vs",))

            if full:
                nc.leave_named_scope(f"g{g}_fdn", scope_fd[0], False)
        try:
            _emit_groups()
        except _StopEmit:
            pass
        def fin(sync):
            wt(sync, "st")
            wt(sync, "cast2")
        on("sync", fin)

    return nc, dbg



def host_prep(inputs):
    W = np.asarray(inputs["L0_attn_in_proj_w"])
    inprojT = np.concatenate([
        W[8192:16384].T, W[16384:17408].T, W[18432:18560].T,
        W[0:8192].T, W[17408:18432].T], axis=1)
    inprojT = np.ascontiguousarray(inprojT, dtype=np.float32)
    outprojT = np.ascontiguousarray(np.asarray(inputs["L0_attn_out_proj_w"]).T,
                                    dtype=np.float32)
    convw = np.ascontiguousarray(
        np.asarray(inputs["L0_attn_conv_w"])[:, 0, :].reshape(80, 128, 4)
        .transpose(1, 0, 2).reshape(128, 80 * 4), dtype=np.float32)
    convb = np.ascontiguousarray(
        np.asarray(inputs["L0_attn_conv_b"]).reshape(80, 128).T, dtype=np.float32)
    dtbias = np.ascontiguousarray(np.asarray(inputs["L0_attn_dt_bias"]).reshape(128, 1),
                                  dtype=np.float32)
    alogv = np.ascontiguousarray(np.asarray(inputs["L0_attn_A_log"]).reshape(128, 1),
                                 dtype=np.float32)
    Dv = np.asarray(inputs["L0_attn_D"]).astype(np.float32)
    dexp = np.empty((128, 64), dtype=np.float32)
    for j in range(64):
        dexp[0:64, j] = Dv[2 * j]
        dexp[64:128, j] = Dv[2 * j + 1]
    normw = np.asarray(inputs["L0_attn_norm_w"]).astype(np.float32)
    normwT = np.ascontiguousarray(normw.reshape(64, 128).T, dtype=np.float32)

    tp = tau_perm(FI)

    def u8(x):
        a = np.asarray(x)
        return np.ascontiguousarray(a.view(np.uint8).reshape(a.shape[0], a.shape[1], 4)[:, :, 0])

    gqv = np.ascontiguousarray(u8(inputs["L0_ffn_gate_Wq"])[tp])
    uqv = np.ascontiguousarray(u8(inputs["L0_ffn_up_Wq"])[tp])
    dqv = u8(inputs["L0_ffn_down_Wq"])
    gscv = np.ascontiguousarray(np.asarray(inputs["L0_ffn_gate_scale"], dtype=np.float32)[tp])
    gzrv = np.ascontiguousarray(np.asarray(inputs["L0_ffn_gate_zero"], dtype=np.float32)[tp])
    uscv = np.ascontiguousarray(np.asarray(inputs["L0_ffn_up_scale"], dtype=np.float32)[tp])
    uzrv = np.ascontiguousarray(np.asarray(inputs["L0_ffn_up_zero"], dtype=np.float32)[tp])
    dscv = np.ascontiguousarray(np.asarray(inputs["L0_ffn_down_scale"], dtype=np.float32))
    dzrv = np.ascontiguousarray(np.asarray(inputs["L0_ffn_down_zero"], dtype=np.float32))

    shared = dict(inprojT=inprojT, outprojT=outprojT, convw=convw, convb=convb,
                  dtbias=dtbias, alog=alogv, dexp=dexp, normwT=normwT,
                  gq=gqv, uq=uqv, dq=dqv, gsc=gscv, gzr=gzrv, usc=uscv, uzr=uzrv,
                  dsc=dscv, dzr=dzrv)

    x = np.asarray(inputs["L0_attn_x"], dtype=np.float32)
    own_len = (NGRP - NPREF) * TG
    in_maps = []
    for core in range(8):
        b, q = core // 4, core % 4
        xb = x[b]
        npref_real = min((own_len // 128) * q, NPREF * 4)
        njunk = NPREF * 4 - npref_real
        parts = []
        if njunk > 0:
            parts.append(np.zeros((njunk * 128, D), dtype=np.float32))
        if npref_real > 0:
            parts.append(xb[0:npref_real * 128])
        parts.append(xb[own_len * q:own_len * (q + 1)])
        xinv = np.ascontiguousarray(np.concatenate(parts, axis=0))
        mask = np.zeros(NGRP * 4, dtype=np.float32)
        mask[njunk:] = 1.0
        maskcv = np.ascontiguousarray(np.tile(mask[None, :], (128, 1)).astype(np.float32))
        m = dict(shared)
        m["xin"] = xinv
        m["maskc"] = maskcv
        in_maps.append(m)
    return in_maps


def kernel(**inputs):
    in_maps = host_prep(inputs)
    nc, _ = build()
    res = run_bass_kernel_spmd(nc, in_maps, core_ids=list(range(8)))
    out = np.empty((2, 4096, 4096), dtype=np.float32)
    for core in range(8):
        b, q = core // 4, core % 4
        out[b, 1024 * q:1024 * (q + 1)] = res.results[core]["yout"]
    return out
